# version 07
# baseline (speedup 1.0000x reference)
import math
import numpy as np
import ml_dtypes
from contextlib import ExitStack

import concourse.bass as bass
import concourse.mybir as mybir
from concourse.bass_utils import run_bass_kernel_spmd

F32 = mybir.dt.float32
BF16 = mybir.dt.bfloat16
U8 = mybir.dt.uint8
AF = mybir.ActivationFunctionType
ALU = mybir.AluOpType

C = 2048
TL = 2048
NT = 16
H = 16
W = 8
INTER = 8192
NEG = -1.0e30
EPS_LN = 1e-5
R2048 = 1.0 / math.sqrt(2048.0)
R8192 = 1.0 / math.sqrt(8192.0)
NBF = ml_dtypes.bfloat16

ENGINES = ("sync", "gpsimd", "tensor", "scalar", "vector")

_NMC = [0]


def _nm():
    _NMC[0] += 1
    return "t%d" % _NMC[0]


def _butterfly_plan(n, dists):
    slot = list(range(n))
    spare = n
    steps = []
    for dist in dists:
        stage = []
        for base in range(0, n, 2 * dist):
            for a in range(base, base + dist):
                b = a + dist
                pa, pb = slot[a], slot[b]
                stage.append((pa, pb, spare))
                slot[b] = spare
                spare = pb
        steps.append(stage)
    return steps, slot


def _pack_ktiles(w):
    K, M = w.shape
    nk, nj = K // 128, M // 128
    p = w.reshape(nk, 128, nj, 128).transpose(2, 1, 0, 3).reshape(nj, 128, K)
    return np.ascontiguousarray(p)


def _cols(v):
    return np.ascontiguousarray(v.reshape(-1, 128).T)


def _hadamard(n):
    Hm = np.array([[1.0]], dtype=np.float32)
    while Hm.shape[0] < n:
        Hm = np.block([[Hm, Hm], [Hm, -Hm]])
    return np.ascontiguousarray(Hm.astype(np.float32))


def pack_inputs(inp):
    f32 = np.float32
    sh = {}
    sh["wqkvP"] = _pack_ktiles(np.asarray(inp["L0_attn_w_qkv"], f32))
    sh["woutP"] = _pack_ktiles(np.asarray(inp["L0_attn_w_out"], f32))
    sh["wupP"] = _pack_ktiles(np.ascontiguousarray(np.asarray(inp["L0_ffn_W_up"], f32).T))
    sh["wgateP"] = _pack_ktiles(np.ascontiguousarray(np.asarray(inp["L0_ffn_W_gate"], f32).T))
    wdT = np.ascontiguousarray(np.asarray(inp["L0_ffn_W_down"], f32).T)
    sh["wdownP"] = _pack_ktiles(wdT)
    sh["wzP"] = _pack_ktiles(np.ascontiguousarray(np.asarray(inp["L1_attn_W_z"], f32).T))
    sh["whP"] = _pack_ktiles(np.ascontiguousarray(np.asarray(inp["L1_attn_W_h"], f32).T))
    sh["wbP"] = _pack_ktiles(np.ascontiguousarray(np.asarray(inp["L1_ffn_weight"], f32).T))

    sh["decay_row"] = np.ascontiguousarray(np.asarray(inp["L0_attn_decay_param"], f32).reshape(1, H))
    sh["su_ug_rows"] = np.ascontiguousarray(np.asarray(inp["L0_ffn_SU_ug"], f32).reshape(1, C))
    sh["su_dp_cols"] = _cols(np.asarray(inp["L0_ffn_SU_dp"], f32))
    sh["wsu_up_cols"] = _cols(np.asarray(inp["L0_ffn_Wscale_ug"], f32)[:INTER])
    sh["wsu_gt_cols"] = _cols(np.asarray(inp["L0_ffn_Wscale_ug"], f32)[INTER:])
    sh["wsd_cols"] = _cols(np.asarray(inp["L0_ffn_Wscale_dp"], f32))
    sh["bz_cols"] = _cols(np.asarray(inp["L1_attn_b_z"], f32))
    sh["bh_cols"] = _cols(np.asarray(inp["L1_attn_b_h"], f32))
    sh["lnw_cols"] = _cols(np.asarray(inp["L1_attn_ln_w"], f32))
    sh["lnb_cols"] = _cols(np.asarray(inp["L1_attn_ln_b"], f32))
    sh["h0_cols"] = _cols(np.asarray(inp["L1_attn_h0"], f32))
    sh["bias_cols"] = _cols(np.asarray(inp["L1_ffn_bias"], f32))

    sh["H128b"] = _hadamard(128).astype(NBF)
    sh["TRIb"] = np.ascontiguousarray(np.triu(np.ones((128, 128), f32)).astype(NBF))
    sh["ones1f"] = np.ones((1, 128), f32)
    sh["ones128b"] = np.ones((128, 128), NBF)
    sh["ones_row"] = np.ones((1, C), f32)

    x = np.asarray(inp["L0_attn_x"], f32)
    pc = []
    for c in range(W):
        b, s = c // 4, c % 4
        d = {}
        d["xT"] = np.ascontiguousarray(x[b, s * TL:(s + 1) * TL, :].T)
        cm = np.zeros((3, W), f32)
        m = np.zeros((W,), f32)
        for j in range(W):
            bj, sj = j // 4, j % 4
            if bj == b and sj < s:
                m[j] = 1.0
                g = s - 1 - sj
                if g < 3:
                    cm[g, j] = 1.0
        d["l0cm"] = np.ascontiguousarray(np.broadcast_to(cm.T.reshape(1, W * 3), (128, W * 3)))
        d["l1m"] = np.ascontiguousarray(np.broadcast_to(m.reshape(1, W), (128, W)))
        d["l1mneg"] = np.ascontiguousarray(np.broadcast_to(((m - 1.0) * 1e30).reshape(1, W), (128, W)))
        first = 1.0 if s == 0 else 0.0
        d["f_col"] = np.full((128, 1), first, f32)
        d["gneg_col"] = np.full((128, 1), (first - 1.0) * 1e30, f32)
        pc.append(d)
    return sh, pc


class Sems:
    def __init__(self, ctx, nc):
        self.nc = nc
        self.ctx = ctx
        self.sem = {e: ctx.enter_context(nc.semaphore(name="sem_" + e)) for e in ENGINES}
        self.cnt = {e: 0 for e in ENGINES}
        self.dpool = []
        self.dcnt = []

    def dma_pool(self, n):
        while len(self.dpool) < n:
            self.dpool.append(self.ctx.enter_context(
                self.nc.semaphore(name="dma%d" % len(self.dpool))))
            self.dcnt.append(0)
        return self.dpool


class Prog:

    def __init__(self, nc, S):
        self.nc = nc
        self.S = S
        self.steps = []
        self.last_writer = {}
        self.readers = {}

    def step(self, engine, fn, reads=(), writes=(), dma=False, fence=False,
             hazard=False):
        idx = len(self.steps)
        deps = set()
        for r in reads:
            k = id(r)
            if k in self.last_writer:
                deps.add(self.last_writer[k])
        for w in writes:
            k = id(w)
            if k in self.last_writer:
                deps.add(self.last_writer[k])
            for rd in self.readers.get(k, ()):
                deps.add(rd)
        for w in writes:
            k = id(w)
            self.last_writer[k] = idx
            self.readers[k] = []
        for r in reads:
            self.readers.setdefault(id(r), []).append(idx)
        if dma:
            key = ("d", id(writes[0]) if writes else id(reads[0]))
        else:
            key = engine
        self.steps.append([engine, fn, 16 if dma else 1, sorted(deps), key])

    def emit(self):
        S = self.S
        steps = self.steps
        slot = {}
        for st in steps:
            if not isinstance(st[4], str) and st[4] not in slot:
                slot[st[4]] = len(slot)
        pool = S.dma_pool(len(slot))
        cnt = dict(S.cnt)
        for k, s_ in slot.items():
            cnt[k] = S.dcnt[s_]
        after = []
        for st in steps:
            cnt[st[4]] += st[2]
            after.append(cnt[st[4]])

        def handle(key):
            return S.sem[key] if isinstance(key, str) else pool[slot[key]]

        waits = []
        marks = {}
        for i, (e, fn, inc, deps, key) in enumerate(steps):
            wl = {}
            for d in deps:
                kd = steps[d][4]
                v = after[d]
                if kd == e:
                    wl[kd] = max(wl.get(kd, 0), v)
                    continue
                m = marks.get((e, kd))
                if m is None:
                    m = S.cnt[kd] if isinstance(kd, str) else S.dcnt[slot[kd]]
                if v > m:
                    wl[kd] = max(wl.get(kd, 0), v)
            for kd, v in wl.items():
                marks[(e, kd)] = max(marks.get((e, kd), 0), v)
            waits.append(wl)

        dkeys = {}
        for i, st in enumerate(steps):
            if not isinstance(st[4], str):
                dkeys[st[4]] = after[i]
        qeng = "sync" if any(st[0] == "sync" for st in steps) else (
            "gpsimd" if any(st[0] == "gpsimd" for st in steps) else None)

        def mkbody(mine, eng_name):
            def body(eng):
                for i in mine:
                    fn, inc, key = steps[i][1], steps[i][2], steps[i][4]
                    for kd, v in sorted(waits[i].items(), key=lambda kv: str(kv[0])):
                        eng.wait_ge(handle(kd), v)
                    fn(eng).then_inc(handle(key), inc)
                if eng_name == qeng:
                    for kd, v in sorted(dkeys.items(), key=lambda kv: str(kv[0])):
                        eng.wait_ge(handle(kd), v)
            return body

        with self.nc.Block() as blk:
            for eng_name in ENGINES:
                mine = [i for i, st in enumerate(steps) if st[0] == eng_name]
                if not mine:
                    continue
                getattr(blk, eng_name)(mkbody(mine, eng_name))
        for st in steps:
            if isinstance(st[4], str):
                S.cnt[st[4]] += st[2]
        for k, s_ in slot.items():
            S.dcnt[s_] = dkeys[k]


def _tile(ap, g, t0=0, t1=TL):
    return ap[g * 128:(g + 1) * 128, t0:t1]


def emit_ln(nc, S, A, x_ap, out_bf=None, out_f32=None, su=False):
    with ExitStack() as ctx:
        sb = lambda shape, dt=F32: ctx.enter_context(nc.sbuf_tensor(_nm(), shape, dt))
        xs = [sb([128, TL]) for _ in range(2)]
        xb = [sb([128, TL], BF16) for _ in range(2)]
        sq = [sb([128, TL], BF16) for _ in range(2)]
        rows = sb([1, 6 * C])
        t1f = sb([128, TL])
        t2f = [sb([128, TL]) for _ in range(2)]
        t2b = [sb([128, TL], BF16) for _ in range(2)]
        o128 = sb([128, 128], BF16)
        onesr = sb([1, C])
        ps = ctx.enter_context(nc.psum_tensor(_nm(), [128, TL], F32))
        pss = ctx.enter_context(nc.psum_tensor(_nm(), [128, TL], F32))

        p = Prog(nc, S)
        p.step("sync", lambda e: e.dma_start(out=o128[:, :], in_=A["ones128b"]),
               writes=[o128], dma=True)
        srcrow = A["su_ug_rows"] if su else A["ones_row"]
        p.step("sync", lambda e, s=srcrow: e.dma_start(out=onesr[0:1, :], in_=s),
               writes=[onesr], dma=True)
        for g in range(NT):
            xg, xbg, sqg = xs[g % 2], xb[g % 2], sq[g % 2]
            p.step("sync", lambda e, xg=xg, g=g: e.dma_start(out=xg[:, :], in_=_tile(x_ap, g)),
                   writes=[xg], dma=True)
            p.step("gpsimd", lambda e, xg=xg, xbg=xbg: e.tensor_copy(xbg[:, :], xg[:, :]),
                   reads=[xg], writes=[xbg])
            p.step("scalar", lambda e, xg=xg, sqg=sqg: e.activation(sqg[:, :], xg[:, :], AF.Square),
                   reads=[xg], writes=[sqg])

            def mms(e, g=g, xbg=xbg, sqg=sqg):
                for n in range(4):
                    e.matmul(ps[:, n * 512:(n + 1) * 512], o128[:, :],
                             xbg[:, n * 512:(n + 1) * 512],
                             start=(g == 0), stop=(g == NT - 1))
                for n in range(4):
                    i = e.matmul(pss[:, n * 512:(n + 1) * 512], o128[:, :],
                                 sqg[:, n * 512:(n + 1) * 512],
                                 start=(g == 0), stop=(g == NT - 1))
                return i
            p.step("tensor", mms, reads=[o128, xbg, sqg], writes=[ps, pss])
        p.emit()

        p = Prog(nc, S)

        def rows1(e):
            e.tensor_scalar_mul(rows[0:1, 0:C], ps[0:1, :], 1.0 / C)
            e.tensor_scalar_mul(rows[0:1, C:2 * C], pss[0:1, :], 1.0 / C)
            e.tensor_mul(rows[0:1, 4 * C:5 * C], rows[0:1, 0:C], rows[0:1, 0:C])
            e.tensor_sub(rows[0:1, C:2 * C], rows[0:1, C:2 * C], rows[0:1, 4 * C:5 * C])
            return e.tensor_scalar_add(rows[0:1, 5 * C:6 * C], rows[0:1, C:2 * C], EPS_LN)
        p.step("vector", rows1, reads=[ps, pss], writes=[rows])

        def rows2(e):
            e.activation(rows[0:1, 5 * C:6 * C], rows[0:1, 5 * C:6 * C], AF.Ln)
            return e.activation(rows[0:1, 2 * C:3 * C], rows[0:1, 5 * C:6 * C],
                                AF.Exp, scale=-0.5)
        p.step("scalar", rows2, reads=[rows], writes=[rows])
        p.step("vector", lambda e: e.tensor_mul(rows[0:1, 3 * C:4 * C],
                                                rows[0:1, 0:C], rows[0:1, 2 * C:3 * C]),
               reads=[rows], writes=[rows])

        ngrp = NT if su else 1
        for g in range(NT):
            xg = xs[g % 2]
            of = t2f[g % 2]
            ob = t2b[g % 2]
            p.step("sync", lambda e, xg=xg, g=g: e.dma_start(out=xg[:, :], in_=_tile(x_ap, g)),
                   writes=[xg], dma=True)
            if g < ngrp:
                def bcast(e, g=g):
                    for n in range(4):
                        e.matmul(ps[:, n * 512:(n + 1) * 512],
                                 onesr[0:1, g * 128:(g + 1) * 128],
                                 rows[0:1, 2 * C + n * 512:2 * C + (n + 1) * 512],
                                 start=True, stop=True)
                    for n in range(4):
                        i = e.matmul(pss[:, n * 512:(n + 1) * 512],
                                     onesr[0:1, g * 128:(g + 1) * 128],
                                     rows[0:1, 3 * C + n * 512:3 * C + (n + 1) * 512],
                                     start=True, stop=True)
                    return i
                p.step("tensor", bcast, reads=[onesr, rows], writes=[ps, pss])

            def apply(e, xg=xg, of=of):
                e.scalar_tensor_tensor(t1f[:, :], xg[:, :], 1.0, ps[:, :],
                                       ALU.mult, ALU.mult)
                return e.tensor_sub(of[:, :], t1f[:, :], pss[:, :])
            p.step("vector", apply, reads=[xg, ps, pss], writes=[t1f, of])
            if out_bf is not None:
                p.step("vector", lambda e, of=of, ob=ob: e.tensor_copy(ob[:, :], of[:, :]),
                       reads=[of], writes=[ob])
                p.step("gpsimd", lambda e, ob=ob, g=g: e.dma_start(out=_tile(out_bf, g), in_=ob[:, :]),
                       reads=[ob], dma=True)
            if out_f32 is not None:
                p.step("gpsimd", lambda e, of=of, g=g: e.dma_start(out=_tile(out_f32, g), in_=of[:, :]),
                       reads=[of], dma=True)
        p.emit()


def _sigmoid_rows(p, drow, drow2, drow3):
    p.step("scalar", lambda e: e.activation(drow[0:1, 16:32], drow[0:1, 0:16],
                                            AF.Exp, scale=-1.0),
           reads=[drow], writes=[drow])
    p.step("vector", lambda e: e.tensor_scalar_add(drow3[0:1, 0:16],
                                                   drow[0:1, 16:32], 1.0),
           reads=[drow], writes=[drow3])
    p.step("vector", lambda e: e.reciprocal(drow2[0:1, 0:16], drow3[0:1, 0:16]),
           reads=[drow3], writes=[drow2], fence=True, hazard=True)


def _phi(p, psum, e_scr, out):
    p.step("vector", lambda e: e.tensor_scalar_min(e_scr[:, :], psum[:, :], 0.0),
           reads=[psum], writes=[e_scr])
    p.step("scalar", lambda e: e.activation(e_scr[:, :], e_scr[:, :], AF.Exp),
           reads=[e_scr], writes=[e_scr])
    p.step("vector", lambda e: e.scalar_tensor_tensor(out[:, :], psum[:, :], 1.0,
                                                      e_scr[:, :], ALU.add, ALU.max),
           reads=[psum, e_scr], writes=[out])


def _mm_ct(e, psum, wtile, lnb, nts=NT, tl=TL):
    nch = tl // 512
    for ct in range(nts):
        for n in range(4 if tl == TL else nch):
            i = e.matmul(psum[:, n * 512:(n + 1) * 512],
                         wtile[:, ct * 128:(ct + 1) * 128],
                         lnb[:, ct * tl + n * 512:ct * tl + (n + 1) * 512],
                         start=(ct == 0), stop=(ct == nts - 1))
    return i


def emit_l0a(nc, S, A, x_ap, ln0b_ap, states_ap, phase2, x1_ap=None,
             states_all_ap=None, y_ap=None):
    if not phase2:
        emit_ln(nc, S, A, x_ap, out_bf=ln0b_ap)
    with ExitStack() as ctx:
        sb = lambda shape, dt=F32: ctx.enter_context(nc.sbuf_tensor(_nm(), shape, dt))
        lnb = sb([128, NT * TL], BF16)
        wk = sb([128, C], BF16)
        wv = sb([128, C], BF16)
        wq = sb([128, C], BF16)
        kf = sb([128, TL])
        qf = sb([128, TL])
        kv = sb([128, TL])
        es = sb([128, TL])
        sckv = sb([128, TL])
        scks = sb([128, TL])
        qks = sb([128, TL], BF16)
        drow = sb([1, 64])
        drow2 = sb([1, 16])
        drow3 = sb([1, 16])
        denr = sb([1, TL])
        denr2 = sb([1, TL])
        dcols = sb([128, 16])
        stc = sb([128, 32])
        o1f = sb([1, 128])
        o128 = sb([128, 128], BF16)
        PA = ctx.enter_context(nc.psum_tensor(_nm(), [128, TL], F32))
        PB = ctx.enter_context(nc.psum_tensor(_nm(), [128, TL], F32))

        p = Prog(nc, S)
        for g in range(NT):
            p.step("sync", lambda e, g=g: e.dma_start(
                out=lnb[:, g * TL:(g + 1) * TL], in_=_tile(ln0b_ap, g)),
                writes=[lnb], dma=True)
        p.step("sync", lambda e: e.dma_start(out=drow[0:1, 0:16], in_=A["decay_row"]),
               writes=[drow], dma=True)
        p.step("sync", lambda e: e.dma_start(out=o1f[0:1, :], in_=A["ones1f"]),
               writes=[o1f], dma=True)
        p.step("sync", lambda e: e.dma_start(out=o128[:, :], in_=A["ones128b"]),
               writes=[o128], dma=True)
        _sigmoid_rows(p, drow, drow2, drow3)
        p.step("tensor", lambda e: e.matmul(PA[:, 0:16], o1f[0:1, :], drow2[0:1, 0:16],
                                            start=True, stop=True),
               reads=[o1f, drow2], writes=[PA])
        p.step("vector", lambda e: e.tensor_copy(dcols[:, :], PA[:, 0:16]),
               reads=[PA], writes=[dcols])

        if phase2:
            sall = sb([128, 8 * 32])
            cmks = sb([128, 24])
            dLc = sb([128, 16 * 4])
            Pkv = sb([128, 16])
            Pks = sb([128, 16])
            coef = sb([128, 16])
            ys = sb([128, TL], BF16)
            p.step("sync", lambda e: e.dma_start(
                out=sall[:, :], in_=states_all_ap.rearrange("(j p) s -> p j s", p=128)),
                writes=[sall], dma=True)
            p.step("sync", lambda e: e.dma_start(out=cmks[:, :], in_=A["l0cm"]),
                   writes=[cmks], dma=True)
            p.step("scalar", lambda e: e.activation(dLc[:, 16:32], dcols[:, :], AF.Ln),
                   reads=[dcols], writes=[dLc])

            p.step("scalar", lambda e: e.activation(dLc[:, 32:48], dLc[:, 16:32],
                                                    AF.Exp, scale=4096.0),
                   reads=[dLc], writes=[dLc])
            p.step("scalar", lambda e: e.activation(dLc[:, 16:32], dLc[:, 16:32],
                                                    AF.Exp, scale=2048.0),
                   reads=[dLc], writes=[dLc])
            p.step("gpsimd", lambda e: e.memset(dLc[:, 0:16], 1.0), writes=[dLc])
            p.step("gpsimd", lambda e: e.memset(Pkv[:, :], 0.0), writes=[Pkv])
            p.step("gpsimd", lambda e: e.memset(Pks[:, :], 0.0), writes=[Pks])
            for j in range(W):
                fr = [dLc, cmks, sall, Pkv, Pks, coef]
                fw = [coef, dLc, Pkv, Pks]
                for fs in [
                    lambda e, j=j: e.tensor_scalar(coef[:, :], dLc[:, 16:32],
                                                   cmks[:, 3 * j + 1:3 * j + 2], None, ALU.mult),
                    lambda e, j=j: e.tensor_scalar(dLc[:, 48:64], dLc[:, 32:48],
                                                   cmks[:, 3 * j + 2:3 * j + 3], None, ALU.mult),
                    lambda e: e.tensor_add(coef[:, :], coef[:, :], dLc[:, 48:64]),
                    lambda e, j=j: e.tensor_scalar(dLc[:, 48:64], dLc[:, 0:16],
                                                   cmks[:, 3 * j:3 * j + 1], None, ALU.mult),
                    lambda e: e.tensor_add(coef[:, :], coef[:, :], dLc[:, 48:64]),
                    lambda e, j=j: e.tensor_mul(dLc[:, 48:64], coef[:, :],
                                                sall[:, 32 * j:32 * j + 16]),
                    lambda e: e.tensor_add(Pkv[:, :], Pkv[:, :], dLc[:, 48:64]),
                    lambda e, j=j: e.tensor_mul(dLc[:, 48:64], coef[:, :],
                                                sall[:, 32 * j + 16:32 * j + 32]),
                    lambda e: e.tensor_add(Pks[:, :], Pks[:, :], dLc[:, 48:64]),
                ]:
                    p.step("vector", fs, reads=fr, writes=fw)

        for h in range(H):
            p.step("gpsimd", lambda e, h=h: e.dma_start(
                out=wk[:, :], in_=A["wqkvP"][16 + h, :, :]), writes=[wk], dma=True)
            p.step("gpsimd", lambda e, h=h: e.dma_start(
                out=wv[:, :], in_=A["wqkvP"][32 + h, :, :]), writes=[wv], dma=True)
            p.step("tensor", lambda e: _mm_ct(e, PA, wk, lnb),
                   reads=[wk, lnb], writes=[PA])
            p.step("tensor", lambda e: _mm_ct(e, PB, wv, lnb),
                   reads=[wv, lnb], writes=[PB])
            p.step("vector", lambda e: e.tensor_scalar_min(es[:, :], PA[:, :], 0.0),
                   reads=[PA], writes=[es])
            p.step("scalar", lambda e: e.activation(es[:, :], es[:, :], AF.Exp),
                   reads=[es], writes=[es])
            p.step("vector", lambda e: e.scalar_tensor_tensor(
                kf[:, :], PA[:, :], 1.0, es[:, :], ALU.add, ALU.max),
                reads=[PA, es], writes=[kf])
            p.step("vector", lambda e: e.tensor_mul(kv[:, :], kf[:, :], PB[:, :]),
                   reads=[kf, PB], writes=[kv])
            p.step("gpsimd", lambda e: e.memset(es[:, :], 1.0), writes=[es])
            p.step("vector", lambda e, h=h: e.tensor_scalar(
                es[:, :], es[:, :], dcols[:, h:h + 1], None, ALU.mult),
                reads=[es, dcols], writes=[es])
            if phase2:
                p.step("gpsimd", lambda e, h=h: e.dma_start(
                    out=wq[:, :], in_=A["wqkvP"][h, :, :]), writes=[wq], dma=True)
                p.step("tensor", lambda e: _mm_ct(e, PA, wq, lnb),
                       reads=[wq, lnb], writes=[PA])
                p.step("vector", lambda e, h=h: e.tensor_tensor_scan(
                    sckv[:, :], es[:, :], kv[:, :], Pkv[:, h:h + 1], ALU.mult, ALU.add),
                    reads=[es, kv, Pkv], writes=[sckv], fence=True, hazard=True)
                p.step("vector", lambda e, h=h: e.tensor_tensor_scan(
                    scks[:, :], es[:, :], kf[:, :], Pks[:, h:h + 1], ALU.mult, ALU.add),
                    reads=[es, kf, Pks], writes=[scks], fence=True, hazard=True)
                p.step("vector", lambda e: e.tensor_scalar_min(es[:, :], PA[:, :], 0.0),
                       reads=[PA], writes=[es])
                p.step("scalar", lambda e: e.activation(es[:, :], es[:, :], AF.Exp),
                       reads=[es], writes=[es])
                p.step("vector", lambda e: e.scalar_tensor_tensor(
                    qf[:, :], PA[:, :], 1.0, es[:, :], ALU.add, ALU.max),
                    reads=[PA, es], writes=[qf])
                p.step("vector", lambda e: e.tensor_mul(qks[:, :], qf[:, :], scks[:, :]),
                       reads=[qf, scks], writes=[qks])

                def denmm(e):
                    for n in range(4):
                        i = e.matmul(PB[:, n * 512:(n + 1) * 512], o128[:, :],
                                     qks[:, n * 512:(n + 1) * 512], start=True, stop=True)
                    return i
                p.step("tensor", denmm, reads=[o128, qks], writes=[PB])

                def denrow(e):
                    e.tensor_scalar_max(denr[0:1, :], PB[0:1, :], 1e-6)
                    return e.reciprocal(denr2[0:1, :], denr[0:1, :])
                p.step("vector", denrow, reads=[PB], writes=[denr, denr2], fence=True, hazard=True)

                def denbc(e):
                    for n in range(4):
                        i = e.matmul(PB[:, n * 512:(n + 1) * 512], o1f[0:1, :],
                                     denr2[0:1, n * 512:(n + 1) * 512],
                                     start=True, stop=True)
                    return i
                p.step("tensor", denbc, reads=[o1f, denr2], writes=[PB])
                p.step("vector", lambda e: e.tensor_mul(scks[:, :], qf[:, :], sckv[:, :]),
                       reads=[qf, sckv], writes=[scks])
                p.step("vector", lambda e: e.tensor_mul(ys[:, :], scks[:, :], PB[:, :]),
                       reads=[scks, PB], writes=[ys])
                p.step("sync", lambda e, h=h: e.dma_start(
                    out=_tile(y_ap, h), in_=ys[:, :]), reads=[ys, y_ap], dma=True)
            else:
                p.step("vector", lambda e: e.tensor_tensor_scan(
                    sckv[:, :], es[:, :], kv[:, :], 0.0, ALU.mult, ALU.add),
                    reads=[es, kv], writes=[sckv], fence=True, hazard=True)
                p.step("vector", lambda e, h=h: e.tensor_copy(
                    stc[:, h:h + 1], sckv[:, TL - 1:TL]), reads=[sckv], writes=[stc])
                p.step("vector", lambda e: e.tensor_tensor_scan(
                    scks[:, :], es[:, :], kf[:, :], 0.0, ALU.mult, ALU.add),
                    reads=[es, kf], writes=[scks], fence=True, hazard=True)
                p.step("vector", lambda e, h=h: e.tensor_copy(
                    stc[:, 16 + h:16 + h + 1], scks[:, TL - 1:TL]),
                    reads=[scks], writes=[stc])

        if not phase2:
            p.step("sync", lambda e: e.dma_start(out=states_ap, in_=stc[:, :]),
                   reads=[stc], dma=True)
            p.emit()
        else:
            p.emit()
            ybs = lnb
            p = Prog(nc, S)
            wo = wq
            xr = kf
            ot = kv
            for g in range(NT):
                p.step("sync", lambda e, g=g: e.dma_start(
                    out=ybs[:, g * TL:(g + 1) * TL], in_=_tile(y_ap, g)),
                    reads=[y_ap], writes=[ybs], dma=True)
            for j in range(NT):
                PX = PA if j % 2 == 0 else PB
                p.step("gpsimd", lambda e, j=j: e.dma_start(
                    out=wo[:, :], in_=A["woutP"][j, :, :]), writes=[wo], dma=True)
                p.step("sync", lambda e, j=j: e.dma_start(
                    out=xr[:, :], in_=_tile(x_ap, j)), writes=[xr], dma=True)
                p.step("tensor", lambda e, PX=PX: _mm_ct(e, PX, wo, ybs),
                       reads=[wo, ybs], writes=[PX])
                p.step("vector", lambda e, PX=PX: e.tensor_add(
                    ot[:, :], PX[:, :], xr[:, :]), reads=[PX, xr], writes=[ot])
                p.step("sync", lambda e, j=j: e.dma_start(
                    out=_tile(x1_ap, j), in_=ot[:, :]), reads=[ot], dma=True)
            p.emit()


def _mm_den(e, psum, o128, qks):
    for ct in range(1):
        pass
    for n in range(4):
        i = e.matmul(psum[:, n * 512:(n + 1) * 512], o128[:, :],
                     qks[:, n * 512:(n + 1) * 512], start=True, stop=True)
    return i


def emit_l0f(nc, S, A, x_ap, x2_ap, lnsu_ap, x1h_ap, w2_ap, p6_ap, x1c_ap=None):
    emit_ln(nc, S, A, x_ap, out_f32=lnsu_ap, su=True)

    with ExitStack() as ctx:
        sb = lambda shape, dt=F32: ctx.enter_context(nc.sbuf_tensor(_nm(), shape, dt))
        h128 = sb([128, 128], BF16)
        wdb = [sb([128, INTER], BF16) for _ in range(2)]
        w2s = [sb([128, INTER], BF16) for _ in range(2)]
        PW = ctx.enter_context(nc.psum_tensor(_nm(), [128, 2048], F32))
        p = Prog(nc, S)
        p.step("sync", lambda e: e.dma_start(out=h128[:, :], in_=A["H128b"]),
               writes=[h128], dma=True)
        for j in range(NT):
            wdj, w2j = wdb[j % 2], w2s[j % 2]
            p.step("gpsimd", lambda e, wdj=wdj, j=j: e.dma_start(
                out=wdj[:, :], in_=A["wdownP"][j, :, :]), writes=[wdj], dma=True)
            for q in range(4):
                def fold(e, wdj=wdj, q=q):
                    for s in range(16):
                        i = e.matmul(PW[:, s * 128:(s + 1) * 128], h128[:, :],
                                     wdj[:, (q * 16 + s) * 128:(q * 16 + s + 1) * 128],
                                     start=True, stop=True)
                    return i
                p.step("tensor", fold, reads=[h128, wdj], writes=[PW])
                p.step("scalar", lambda e, w2j=w2j, q=q: e.copy(
                    w2j[:, q * 2048:(q + 1) * 2048], PW[:, :]), reads=[PW], writes=[w2j])
            p.step("sync", lambda e, w2j=w2j, j=j: e.dma_start(
                out=w2_ap[j, :, :], in_=w2j[:, :]), reads=[w2j, w2_ap], dma=True)
        p.emit()

    bsteps16, _ = _butterfly_plan(16, [8, 4, 2, 1])
    with ExitStack() as ctx:
        sb = lambda shape, dt=F32: ctx.enter_context(nc.sbuf_tensor(_nm(), shape, dt))
        TH = TL // 2
        ftile = [sb([128, TH]) for _ in range(17)]
        x1c = sb([128, NT * TH], BF16)
        x1o = sb([128, NT * TH], BF16)
        h128 = sb([128, 128], BF16)
        PF = ctx.enter_context(nc.psum_tensor(_nm(), [128, TH], F32))
        PG = ctx.enter_context(nc.psum_tensor(_nm(), [128, TH], F32))
        for half in range(2):
            t0 = half * TH
            p = Prog(nc, S)
            if half == 0:
                p.step("sync", lambda e: e.dma_start(out=h128[:, :], in_=A["H128b"]),
                       writes=[h128], dma=True)
            for g in range(NT):
                p.step("sync", lambda e, g=g, t0=t0: e.dma_start(
                    out=ftile[g][:, :], in_=_tile(lnsu_ap, g, t0, t0 + TH)),
                    writes=[ftile[g]], dma=True)
            for si, stage in enumerate(bsteps16):
                last = si == len(bsteps16) - 1
                for pi, (pa, pb, sp) in enumerate(stage):
                    if not last:
                        eng = "vector" if pi % 2 == 0 else "gpsimd"
                        def bfs(e, pa=pa, pb=pb, sp=sp):
                            return e.tensor_sub(ftile[sp][:, :], ftile[pa][:, :], ftile[pb][:, :])
                        def bfa(e, pa=pa, pb=pb):
                            return e.tensor_add(ftile[pa][:, :], ftile[pa][:, :], ftile[pb][:, :])
                        p.step(eng, bfs, reads=[ftile[pa], ftile[pb]], writes=[ftile[sp]])
                        p.step(eng, bfa, reads=[ftile[pa], ftile[pb]], writes=[ftile[pa]])
                    else:
                        la = 2 * pi
                        def bf2(e, pa=pa, pb=pb, la=la):
                            e.tensor_sub(x1c[:, (la + 1) * TH:(la + 2) * TH],
                                         ftile[pa][:, :], ftile[pb][:, :])
                            return e.tensor_add(x1c[:, la * TH:(la + 1) * TH],
                                                ftile[pa][:, :], ftile[pb][:, :])
                        p.step("vector", bf2, reads=[ftile[pa], ftile[pb]], writes=[x1c])
            if x1c_ap is not None:
                for a in range(NT):
                    p.step("sync", lambda e, a=a, t0=t0: e.dma_start(
                        out=_tile(x1c_ap, a, t0, t0 + TH), in_=x1c[:, a * TH:(a + 1) * TH]),
                        reads=[x1c], dma=True)
            for a in range(NT):
                PX = PF if a % 2 == 0 else PG
                def h128mm(e, a=a, PX=PX):
                    for n in range(TH // 512):
                        i = e.matmul(PX[:, n * 512:(n + 1) * 512], h128[:, :],
                                     x1c[:, a * TH + n * 512:a * TH + (n + 1) * 512],
                                     start=True, stop=True)
                    return i
                p.step("tensor", h128mm, reads=[h128, x1c], writes=[PX])
                p.step("scalar", lambda e, a=a, PX=PX: e.copy(
                    x1o[:, a * TH:(a + 1) * TH], PX[:, :]), reads=[PX], writes=[x1o])
                p.step("sync", lambda e, a=a, t0=t0: e.dma_start(
                    out=_tile(x1h_ap, a, t0, t0 + TH), in_=x1o[:, a * TH:(a + 1) * TH]),
                    reads=[x1o, x1h_ap], dma=True)
            p.emit()

    bsteps64, _ = _butterfly_plan(64, [32, 16, 8, 4, 2, 1])
    TC = 512
    with ExitStack() as ctx:
        sb = lambda shape, dt=F32: ctx.enter_context(nc.sbuf_tensor(_nm(), shape, dt))
        sup = sb([128, 64])
        sgt = sb([128, 64])
        sdn = sb([128, 16])
        sud = sb([128, 64])
        p = Prog(nc, S)
        p.step("sync", lambda e: e.dma_start(out=sup[:, :], in_=A["wsu_up_cols"]),
               writes=[sup], dma=True)
        p.step("sync", lambda e: e.dma_start(out=sgt[:, :], in_=A["wsu_gt_cols"]),
               writes=[sgt], dma=True)
        p.step("sync", lambda e: e.dma_start(out=sdn[:, :], in_=A["wsd_cols"]),
               writes=[sdn], dma=True)
        p.step("sync", lambda e: e.dma_start(out=sud[:, :], in_=A["su_dp_cols"]),
               writes=[sud], dma=True)
        p.step("vector", lambda e: e.tensor_scalar_mul(sup[:, :], sup[:, :], R2048),
               reads=[sup], writes=[sup])
        p.step("vector", lambda e: e.tensor_scalar_mul(sgt[:, :], sgt[:, :], R2048),
               reads=[sgt], writes=[sgt])
        p.step("vector", lambda e: e.tensor_scalar_mul(sdn[:, :], sdn[:, :], R8192),
               reads=[sdn], writes=[sdn])
        p.emit()

        for tc in range(TL // TC):
            t0 = tc * TC
            with ExitStack() as c1:
                sb1 = lambda shape, dt=F32: c1.enter_context(nc.sbuf_tensor(_nm(), shape, dt))
                x1s = sb1([128, NT * TC], BF16)
                wub = [sb1([128, C], BF16) for _ in range(2)]
                wgb = [sb1([128, C], BF16) for _ in range(2)]
                us = [sb1([128, TC]) for _ in range(2)]
                gs = [sb1([128, TC]) for _ in range(2)]
                pF = [sb1([128, TC]) for _ in range(65)]
                bnc = sb1([128, 2 * TC], BF16)
                PU = [c1.enter_context(nc.psum_tensor(_nm(), [128, TC], F32))
                      for _ in range(2)]
                PG2 = [c1.enter_context(nc.psum_tensor(_nm(), [128, TC], F32))
                       for _ in range(2)]
                p = Prog(nc, S)
                p.step("sync", lambda e, t0=t0: e.dma_start(
                    out=x1s[:, :],
                    in_=x1h_ap.rearrange("(g p) t -> p g t", p=128)[:, :, t0:t0 + TC]),
                    reads=[x1h_ap], writes=[x1s], dma=True)
                for j in range(64):
                    wuj, wgj = wub[j % 2], wgb[j % 2]
                    PUj, PGj = PU[j % 2], PG2[j % 2]
                    usj, gsj = us[j % 2], gs[j % 2]
                    p.step("gpsimd", lambda e, wuj=wuj, j=j: e.dma_start(
                        out=wuj[:, :], in_=A["wupP"][j, :, :]), writes=[wuj], dma=True)
                    p.step("gpsimd", lambda e, wgj=wgj, j=j: e.dma_start(
                        out=wgj[:, :], in_=A["wgateP"][j, :, :]), writes=[wgj], dma=True)
                    p.step("tensor", lambda e, wuj=wuj, PUj=PUj: _mm_ct(e, PUj, wuj, x1s, tl=TC),
                           reads=[wuj, x1s], writes=[PUj])
                    p.step("tensor", lambda e, wgj=wgj, PGj=PGj: _mm_ct(e, PGj, wgj, x1s, tl=TC),
                           reads=[wgj, x1s], writes=[PGj])
                    p.step("scalar", lambda e, j=j, PUj=PUj, usj=usj: e.activation(
                        usj[:, :], PUj[:, :], AF.Identity, scale=sup[:, j:j + 1]),
                        reads=[PUj, sup], writes=[usj])
                    p.step("scalar", lambda e, j=j, PGj=PGj, gsj=gsj: e.activation(
                        gsj[:, :], PGj[:, :], AF.Silu, scale=sgt[:, j:j + 1]),
                        reads=[PGj, sgt], writes=[gsj])
                    p.step("vector", lambda e, j=j, gsj=gsj, usj=usj: e.scalar_tensor_tensor(
                        pF[j][:, :], gsj[:, :], sud[:, j:j + 1], usj[:, :],
                        ALU.mult, ALU.mult),
                        reads=[gsj, sud, usj], writes=[pF[j]])
                for si, stage in enumerate(bsteps64):
                    last = si == len(bsteps64) - 1
                    for pi, (pa, pb, sp) in enumerate(stage):
                        if not last:
                            eng = "vector" if pi % 2 == 0 else "gpsimd"
                            def bfs(e, pa=pa, pb=pb, sp=sp):
                                return e.tensor_sub(pF[sp][:, :], pF[pa][:, :], pF[pb][:, :])
                            def bfa(e, pa=pa, pb=pb):
                                return e.tensor_add(pF[pa][:, :], pF[pa][:, :], pF[pb][:, :])
                            p.step(eng, bfs, reads=[pF[pa], pF[pb]], writes=[pF[sp]])
                            p.step(eng, bfa, reads=[pF[pa], pF[pb]], writes=[pF[pa]])
                        else:
                            la = 2 * pi
                            def bf2(e, pa=pa, pb=pb):
                                e.tensor_sub(bnc[:, TC:2 * TC], pF[pa][:, :], pF[pb][:, :])
                                return e.tensor_add(bnc[:, 0:TC], pF[pa][:, :], pF[pb][:, :])
                            p.step("vector", bf2, reads=[pF[pa], pF[pb]], writes=[bnc])
                            p.step("sync", lambda e, la=la: e.dma_start(
                                out=p6_ap[:, la * TC:(la + 2) * TC], in_=bnc[:, :]),
                                reads=[bnc, p6_ap], dma=True)
                p.emit()

            with ExitStack() as c2:
                sb2 = lambda shape, dt=F32: c2.enter_context(nc.sbuf_tensor(_nm(), shape, dt))
                p6s = sb2([128, 64 * TC], BF16)
                w2b = [sb2([128, INTER], BF16) for _ in range(2)]
                us2 = sb2([128, TC])
                xr = [sb2([128, TC]) for _ in range(2)]
                ot = [sb2([128, TC]) for _ in range(2)]
                PO = c2.enter_context(nc.psum_tensor(_nm(), [128, TC], F32))
                PO2 = c2.enter_context(nc.psum_tensor(_nm(), [128, TC], F32))
                p = Prog(nc, S)
                p.step("sync", lambda e: e.dma_start(out=p6s[:, :], in_=p6_ap),
                       reads=[p6_ap], writes=[p6s], dma=True)
                for j in range(NT):
                    w2j = w2b[j % 2]
                    xrj, otj = xr[j % 2], ot[j % 2]
                    PX = PO if j % 2 == 0 else PO2
                    p.step("gpsimd", lambda e, w2j=w2j, j=j: e.dma_start(
                        out=w2j[:, :], in_=w2_ap[j, :, :]),
                        reads=[w2_ap], writes=[w2j], dma=True)
                    p.step("sync", lambda e, xrj=xrj, j=j, t0=t0: e.dma_start(
                        out=xrj[:, :], in_=_tile(x_ap, j, t0, t0 + TC)),
                        writes=[xrj], dma=True)

                    def wdmm(e, w2j=w2j, PX=PX):
                        for s in range(64):
                            i = e.matmul(PX[:, :], w2j[:, s * 128:(s + 1) * 128],
                                         p6s[:, s * TC:(s + 1) * TC],
                                         start=(s == 0), stop=(s == 63))
                        return i
                    p.step("tensor", wdmm, reads=[w2j, p6s], writes=[PX])
                    p.step("scalar", lambda e, j=j, PX=PX: e.activation(
                        us2[:, :], PX[:, :], AF.Identity, scale=sdn[:, j:j + 1]),
                        reads=[PX, sdn], writes=[us2])
                    p.step("vector", lambda e, xrj=xrj, otj=otj: e.tensor_add(
                        otj[:, :], us2[:, :], xrj[:, :]), reads=[us2, xrj], writes=[otj])
                    p.step("sync", lambda e, otj=otj, j=j, t0=t0: e.dma_start(
                        out=_tile(x2_ap, j, t0, t0 + TC), in_=otj[:, :]),
                        reads=[otj], dma=True)
                p.emit()


def emit_l1a1(nc, S, A, x_ap, hs_ap, l_ap, ast_ap, mz_ap, pcA):
    emit_ln(nc, S, A, x_ap, out_f32=hs_ap)
    with ExitStack() as ctx:
        sb = lambda shape, dt=F32: ctx.enter_context(nc.sbuf_tensor(_nm(), shape, dt))
        hsb = sb([128, NT * TL], BF16)
        wz = sb([128, C], BF16)
        wh = sb([128, C], BF16)
        k1 = sb([128, TL])
        z2 = sb([128, TL])
        sp_ = sb([128, TL])
        scr = sb([128, TL])
        lcb = sb([128, TL], BF16)
        lzt = sb([128, TL])
        u_s = sb([128, TL])
        ast = sb([128, TL])
        R_s = sb([128, TL])
        d_s = sb([128, TL])
        b_s = sb([128, TL])
        Z_s = sb([128, TL])
        l_s = sb([128, TL])
        msk = sb([128, TL], U8)
        bzc = sb([128, 16])
        bhc = sb([128, 16])
        h0c = sb([128, 16])
        u0c = sb([128, 16])
        tnc = sb([128, 16])
        fcol = sb([128, 1])
        gcol = sb([128, 1])
        rin = sb([128, 16])
        cst = sb([128, 2])
        carry = sb([1, TL])
        crow = sb([1, TL])
        mzc = sb([128, 32])
        trib = sb([128, 128], BF16)
        o1f = sb([1, 128])
        o128 = sb([128, 128], BF16)
        PA = ctx.enter_context(nc.psum_tensor(_nm(), [128, TL], F32))
        PB = ctx.enter_context(nc.psum_tensor(_nm(), [128, TL], F32))

        p = Prog(nc, S)
        p.step("gpsimd", lambda e: e.dma_start(
            out=hsb[:, :], in_=hs_ap.rearrange("(g p) t -> p g t", p=128)),
            writes=[hsb], dma=True)
        for nm, dst in (("bz_cols", bzc), ("bh_cols", bhc), ("h0_cols", h0c),
                        ("TRIb", trib), ("ones1f", o1f), ("ones128b", o128),
                        ("f_col", fcol), ("gneg_col", gcol)):
            p.step("sync", lambda e, nm=nm, dst=dst: e.dma_start(out=dst[:, :], in_=A[nm]),
                   writes=[dst], dma=True)
        p.step("gpsimd", lambda e: e.memset(carry[0:1, :], 0.0), writes=[carry])
        p.step("gpsimd", lambda e: e.memset(cst[:, 0:1], 0.5), writes=[cst])
        p.step("gpsimd", lambda e: e.memset(cst[:, 1:2], 1.0), writes=[cst])

        for fs in [
            lambda e: e.activation(u0c[:, :], h0c[:, :], AF.Relu),
            lambda e: e.activation(u0c[:, :], u0c[:, :], AF.Ln, bias=cst[:, 0:1]),
            lambda e: e.activation(tnc[:, :], h0c[:, :], AF.Abs),
            lambda e: e.activation(tnc[:, :], tnc[:, :], AF.Exp, scale=-1.0),
            lambda e: e.activation(tnc[:, :], tnc[:, :], AF.Ln, bias=cst[:, 1:2]),
        ]:
            p.step("scalar", fs, reads=[h0c, cst, u0c, tnc], writes=[u0c, tnc])

        for fs in [
            lambda e: e.scalar_tensor_tensor(tnc[:, :], h0c[:, :], 0.0, tnc[:, :],
                                             ALU.max, ALU.add),
            lambda e: e.tensor_sub(tnc[:, :], h0c[:, :], tnc[:, :]),
            lambda e: e.tensor_single_scalar(rin[:, :], h0c[:, :], 0.0, ALU.is_ge),
        ]:
            p.step("vector", fs, reads=[h0c, tnc], writes=[tnc, rin])
        p.step("vector", lambda e: e.tensor_copy(msk[:, 0:16], rin[:, :]),
               reads=[rin], writes=[msk])
        p.step("vector", lambda e: e.select(u0c[:, :], msk[:, 0:16], u0c[:, :], tnc[:, :]),
               reads=[msk, u0c, tnc], writes=[u0c])
        p.step("vector", lambda e: e.tensor_scalar(rin[:, :], u0c[:, :], fcol[:, 0:1],
                                                   None, ALU.mult),
               reads=[u0c, fcol], writes=[rin])
        p.step("vector", lambda e: e.tensor_scalar(rin[:, :], rin[:, :], gcol[:, 0:1],
                                                   None, ALU.add),
               reads=[rin, gcol], writes=[rin])

        for j in range(NT):
            p.step("gpsimd", lambda e, j=j: e.dma_start(
                out=wz[:, :], in_=A["wzP"][j, :, :]), writes=[wz], dma=True)
            p.step("gpsimd", lambda e, j=j: e.dma_start(
                out=wh[:, :], in_=A["whP"][j, :, :]), writes=[wh], dma=True)
            p.step("tensor", lambda e: _mm_ct(e, PA, wz, hsb),
                   reads=[wz, hsb], writes=[PA])
            p.step("tensor", lambda e: _mm_ct(e, PB, wh, hsb),
                   reads=[wh, hsb], writes=[PB])
            p.step("scalar", lambda e, j=j: e.activation(
                k1[:, :], PA[:, :], AF.Identity, bias=bzc[:, j:j + 1]),
                reads=[PA, bzc], writes=[k1])
            p.step("scalar", lambda e, j=j: e.activation(
                z2[:, :], PB[:, :], AF.Identity, bias=bhc[:, j:j + 1]),
                reads=[PB, bhc], writes=[z2])

            def spa(e):
                e.activation(scr[:, :], k1[:, :], AF.Abs)
                e.activation(scr[:, :], scr[:, :], AF.Exp, scale=-1.0)
                return e.activation(scr[:, :], scr[:, :], AF.Ln, bias=cst[:, 1:2])
            p.step("scalar", spa, reads=[k1, cst], writes=[scr])
            p.step("vector", lambda e: e.scalar_tensor_tensor(
                sp_[:, :], k1[:, :], 0.0, scr[:, :], ALU.max, ALU.add),
                reads=[k1, scr], writes=[sp_])
            p.step("vector", lambda e: e.tensor_scalar_mul(lcb[:, :], sp_[:, :], -1.0),
                   reads=[sp_], writes=[lcb])
            def gla(e):
                e.activation(lzt[:, :], z2[:, :], AF.Relu)
                e.activation(lzt[:, :], lzt[:, :], AF.Ln, bias=cst[:, 0:1])
                e.activation(scr[:, :], z2[:, :], AF.Abs)
                e.activation(scr[:, :], scr[:, :], AF.Exp, scale=-1.0)
                return e.activation(scr[:, :], scr[:, :], AF.Ln, bias=cst[:, 1:2])
            p.step("scalar", gla, reads=[z2, cst], writes=[lzt, scr])

            def glb(e):
                e.scalar_tensor_tensor(scr[:, :], z2[:, :], 0.0, scr[:, :], ALU.max, ALU.add)
                e.tensor_sub(scr[:, :], z2[:, :], scr[:, :])
                e.tensor_single_scalar(msk[:, :], z2[:, :], 0.0, ALU.is_ge)
                return e.select(lzt[:, :], msk[:, :], lzt[:, :], scr[:, :])
            p.step("vector", glb, reads=[z2, scr, lzt], writes=[scr, msk, lzt])

            def lz2(e):
                e.tensor_sub(u_s[:, :], k1[:, :], sp_[:, :])
                return e.tensor_add(lzt[:, :], lzt[:, :], u_s[:, :])
            p.step("vector", lz2, reads=[k1, sp_, lzt], writes=[lzt, u_s])

            def amm(e):
                for n in range(4):
                    e.matmul(PA[:, n * 512:(n + 1) * 512], trib[:, :],
                             lcb[:, n * 512:(n + 1) * 512], start=True, stop=True)
                for n in range(4):
                    i = e.matmul(PB[:, n * 512:(n + 1) * 512], o1f[0:1, :],
                                 carry[0:1, n * 512:(n + 1) * 512], start=True, stop=True)
                return i
            p.step("tensor", amm, reads=[trib, lcb, o1f, carry], writes=[PA, PB])

            p.step("scalar", lambda e: e.copy(ast[:, :], PA[:, :]),
                   reads=[PA], writes=[ast])
            p.step("vector", lambda e: e.tensor_add(ast[:, :], ast[:, :], PB[:, :]),
                   reads=[ast, PB], writes=[ast])
            p.step("vector", lambda e: e.tensor_sub(u_s[:, :], lzt[:, :], ast[:, :]),
                   reads=[lzt, ast], writes=[u_s])
            p.step("sync", lambda e, j=j: e.dma_start(out=_tile(ast_ap, j), in_=ast[:, :]),
                   reads=[ast], dma=True)

            def cmm(e):
                for ct in range(1):
                    pass
                for n in range(4):
                    i = e.matmul(PA[:, n * 512:(n + 1) * 512], o128[:, :],
                                 lcb[:, n * 512:(n + 1) * 512], start=True, stop=True)
                return i
            p.step("tensor", cmm, reads=[o128, lcb], writes=[PA])
            p.step("vector", lambda e: e.tensor_add(carry[0:1, :], carry[0:1, :], PA[0:1, :]),
                   reads=[PA, carry], writes=[carry])

            p.step("gpsimd", lambda e: e.memset(scr[:, :], 0.0), writes=[scr])
            p.step("vector", lambda e, j=j: e.tensor_tensor_scan(
                R_s[:, :], u_s[:, :], scr[:, :], rin[:, j:j + 1], ALU.max, ALU.add),
                reads=[u_s, scr, rin], writes=[R_s], fence=True, hazard=True)

            def dsh(e, j=j):
                e.tensor_sub(d_s[:, 1:TL], R_s[:, 0:TL - 1], R_s[:, 1:TL])
                e.tensor_scalar(d_s[:, 0:1], R_s[:, 0:1], -1.0, rin[:, j:j + 1],
                                ALU.mult, ALU.add)
                return e.tensor_sub(b_s[:, :], u_s[:, :], R_s[:, :])
            p.step("vector", dsh, reads=[R_s, u_s, rin], writes=[d_s, b_s])

            def exps(e):
                e.activation(d_s[:, :], d_s[:, :], AF.Exp)
                return e.activation(b_s[:, :], b_s[:, :], AF.Exp)
            p.step("scalar", exps, reads=[d_s, b_s], writes=[d_s, b_s])
            p.step("vector", lambda e, j=j: e.tensor_tensor_scan(
                Z_s[:, :], d_s[:, :], b_s[:, :], fcol[:, 0:1], ALU.mult, ALU.add),
                reads=[d_s, b_s, fcol], writes=[Z_s], fence=True, hazard=True)
            p.step("scalar", lambda e: e.activation(l_s[:, :], Z_s[:, :], AF.Ln),
                   reads=[Z_s], writes=[l_s])
            p.step("vector", lambda e: e.tensor_add(l_s[:, :], l_s[:, :], R_s[:, :]),
                   reads=[R_s, l_s], writes=[l_s])
            p.step("sync", lambda e, j=j: e.dma_start(out=_tile(l_ap, j), in_=l_s[:, :]),
                   reads=[l_s], dma=True)
            p.step("vector", lambda e, j=j: e.tensor_copy(mzc[:, j:j + 1], R_s[:, TL - 1:TL]),
                   reads=[R_s], writes=[mzc])
            p.step("vector", lambda e, j=j: e.tensor_copy(mzc[:, 16 + j:17 + j], Z_s[:, TL - 1:TL]),
                   reads=[Z_s], writes=[mzc])
        p.step("sync", lambda e: e.dma_start(out=mz_ap, in_=mzc[:, :]),
               reads=[mzc], dma=True)
        p.emit()


def emit_l1a2(nc, S, A, l_ap, ast_ap, mzall_ap, logh_ap, stats_ap):
    with ExitStack() as ctx:
        sb = lambda shape, dt=F32: ctx.enter_context(nc.sbuf_tensor(_nm(), shape, dt))
        sall = sb([128, 8 * 32])
        l1m = sb([128, 8])
        l1mn = sb([128, 8])
        fcol = sb([128, 1])
        PM = sb([128, 16])
        PZ = sb([128, 16])
        Me = sb([128, 16])
        Ze = sb([128, 16])
        Mn = sb([128, 16])
        t1 = sb([128, 16])
        t2 = sb([128, 16])
        Pcol = sb([128, 16])
        h0c = sb([128, 16])
        u0c = sb([128, 16])
        tnc = sb([128, 16])
        msk16 = sb([128, 16], U8)
        cst = sb([128, 2])
        l_s = sb([128, TL])
        ast = sb([128, TL])
        dl = sb([128, TL])
        mx = sb([128, TL])
        scr = sb([128, TL])
        acc = sb([128, 34])
        o1c = sb([128, 1])
        rowst = sb([1, 40])
        PS = ctx.enter_context(nc.psum_tensor(_nm(), [1, 512], F32))

        p = Prog(nc, S)
        p.step("sync", lambda e: e.dma_start(
            out=sall[:, :], in_=mzall_ap.rearrange("(j p) s -> p j s", p=128)),
            writes=[sall], dma=True)
        p.step("sync", lambda e: e.dma_start(out=l1m[:, :], in_=A["l1m"]),
               writes=[l1m], dma=True)
        p.step("sync", lambda e: e.dma_start(out=l1mn[:, :], in_=A["l1mneg"]),
               writes=[l1mn], dma=True)
        p.step("sync", lambda e: e.dma_start(out=fcol[:, :], in_=A["f_col"]),
               writes=[fcol], dma=True)
        p.step("sync", lambda e: e.dma_start(out=h0c[:, :], in_=A["h0_cols"]),
               writes=[h0c], dma=True)
        p.step("gpsimd", lambda e: e.memset(PM[:, :], NEG), writes=[PM])
        p.step("gpsimd", lambda e: e.memset(PZ[:, :], 0.0), writes=[PZ])
        p.step("gpsimd", lambda e: e.memset(o1c[:, :], 1.0), writes=[o1c])
        p.step("gpsimd", lambda e: e.memset(cst[:, 0:1], 0.5), writes=[cst])
        p.step("gpsimd", lambda e: e.memset(cst[:, 1:2], 1.0), writes=[cst])
        for j in range(W):
            fr = [sall, l1m, l1mn, PM, Me, Ze, Mn]
            fw = [Me, Ze, Mn, t1, t2]
            for fs in [
                lambda e, j=j: e.tensor_scalar(Me[:, :], sall[:, 32 * j:32 * j + 16],
                                               l1m[:, j:j + 1], l1mn[:, j:j + 1],
                                               ALU.mult, ALU.add),
                lambda e, j=j: e.tensor_scalar(Ze[:, :], sall[:, 32 * j + 16:32 * j + 32],
                                               l1m[:, j:j + 1], None, ALU.mult),
                lambda e: e.tensor_max(Mn[:, :], PM[:, :], Me[:, :]),
                lambda e: e.tensor_sub(t1[:, :], PM[:, :], Mn[:, :]),
                lambda e: e.tensor_sub(t2[:, :], Me[:, :], Mn[:, :]),
            ]:
                p.step("vector", fs, reads=fr, writes=fw)

            def f2(e):
                e.activation(t1[:, :], t1[:, :], AF.Exp)
                return e.activation(t2[:, :], t2[:, :], AF.Exp)
            p.step("scalar", f2, reads=[t1, t2], writes=[t1, t2])

            for fs in [
                lambda e: e.tensor_mul(PZ[:, :], PZ[:, :], t1[:, :]),
                lambda e: e.tensor_mul(t2[:, :], Ze[:, :], t2[:, :]),
                lambda e: e.tensor_add(PZ[:, :], PZ[:, :], t2[:, :]),
                lambda e: e.tensor_copy(PM[:, :], Mn[:, :]),
            ]:
                p.step("vector", fs, reads=[PZ, t1, t2, Ze, Mn], writes=[PZ, PM, t2])
        p.step("vector", lambda e: e.tensor_scalar_max(PZ[:, :], PZ[:, :], 1e-38),
               reads=[PZ], writes=[PZ])
        p.step("scalar", lambda e: e.activation(Pcol[:, :], PZ[:, :], AF.Ln),
               reads=[PZ], writes=[Pcol])
        p.step("vector", lambda e: e.tensor_add(Pcol[:, :], Pcol[:, :], PM[:, :]),
               reads=[Pcol, PM], writes=[Pcol])

        for fs in [
            lambda e: e.activation(u0c[:, :], h0c[:, :], AF.Relu),
            lambda e: e.activation(u0c[:, :], u0c[:, :], AF.Ln, bias=cst[:, 0:1]),
            lambda e: e.activation(tnc[:, :], h0c[:, :], AF.Abs),
            lambda e: e.activation(tnc[:, :], tnc[:, :], AF.Exp, scale=-1.0),
            lambda e: e.activation(tnc[:, :], tnc[:, :], AF.Ln, bias=cst[:, 1:2]),
        ]:
            p.step("scalar", fs, reads=[h0c, cst, u0c, tnc], writes=[u0c, tnc])

        for fs in [
            lambda e: e.scalar_tensor_tensor(tnc[:, :], h0c[:, :], 0.0, tnc[:, :],
                                             ALU.max, ALU.add),
            lambda e: e.tensor_sub(tnc[:, :], h0c[:, :], tnc[:, :]),
            lambda e: e.tensor_single_scalar(msk16[:, :], h0c[:, :], 0.0, ALU.is_ge),
            lambda e: e.select(u0c[:, :], msk16[:, :], u0c[:, :], tnc[:, :]),
            lambda e: e.tensor_scalar(u0c[:, :], u0c[:, :], fcol[:, 0:1], None, ALU.mult),
        ]:
            p.step("vector", fs, reads=[h0c, tnc, msk16, u0c, fcol],
                   writes=[tnc, msk16, u0c])
        p.step("scalar", lambda e: e.activation(scr[:, 0:16], u0c[:, :], AF.Copy,
                                                accum_out=acc[:, 32:33]),
               reads=[u0c], writes=[scr, acc])
        p.step("scalar", lambda e: e.activation(scr[:, 0:16], u0c[:, :], AF.Square,
                                                accum_out=acc[:, 33:34]),
               reads=[u0c], writes=[scr, acc])

        for j in range(NT):
            p.step("sync", lambda e, j=j: e.dma_start(out=l_s[:, :], in_=_tile(l_ap, j)),
                   writes=[l_s], dma=True)
            p.step("sync", lambda e, j=j: e.dma_start(out=ast[:, :], in_=_tile(ast_ap, j)),
                   writes=[ast], dma=True)

            def lae(e, j=j):
                e.tensor_scalar(dl[:, :], l_s[:, :], Pcol[:, j:j + 1], None, ALU.subtract)
                return e.tensor_scalar(mx[:, :], l_s[:, :], Pcol[:, j:j + 1], None, ALU.max)
            p.step("vector", lae, reads=[l_s, Pcol], writes=[dl, mx])

            def lex(e):
                e.activation(dl[:, :], dl[:, :], AF.Abs)
                e.activation(dl[:, :], dl[:, :], AF.Exp, scale=-1.0)
                return e.activation(dl[:, :], dl[:, :], AF.Ln, bias=cst[:, 1:2])
            p.step("scalar", lex, reads=[dl, cst], writes=[dl])

            def lsum(e):
                e.tensor_add(mx[:, :], mx[:, :], dl[:, :])
                return e.tensor_add(mx[:, :], mx[:, :], ast[:, :])
            p.step("vector", lsum, reads=[mx, dl, ast], writes=[mx])
            p.step("sync", lambda e, j=j: e.dma_start(out=_tile(logh_ap, j), in_=mx[:, :]),
                   reads=[mx], dma=True)
            p.step("scalar", lambda e, j=j: e.activation(
                scr[:, :], mx[:, :], AF.Copy, accum_out=acc[:, j:j + 1]),
                reads=[mx], writes=[scr, acc])
            p.step("scalar", lambda e, j=j: e.activation(
                scr[:, :], mx[:, :], AF.Square, accum_out=acc[:, 16 + j:17 + j]),
                reads=[mx], writes=[scr, acc])

        def statmm(e):
            return e.matmul(PS[0:1, 0:34], o1c[:, 0:1], acc[:, :], start=True, stop=True)
        p.step("tensor", statmm, reads=[o1c, acc], writes=[PS])

        for fs in [
            lambda e: e.reduce_sum(rowst[0:1, 0:1], PS[0:1, 0:16], axis=mybir.AxisListType.X),
            lambda e: e.reduce_sum(rowst[0:1, 1:2], PS[0:1, 16:32], axis=mybir.AxisListType.X),
            lambda e: e.tensor_add(rowst[0:1, 0:1], rowst[0:1, 0:1], PS[0:1, 32:33]),
            lambda e: e.tensor_add(rowst[0:1, 1:2], rowst[0:1, 1:2], PS[0:1, 33:34]),
        ]:
            p.step("vector", fs, reads=[PS, rowst], writes=[rowst])
        p.step("sync", lambda e: e.dma_start(out=stats_ap, in_=rowst[0:1, 0:2]),
               reads=[rowst], dma=True)
        p.emit()


def emit_l1a3(nc, S, A, x_ap, hs_ap, logh_ap, statsall_ap, x3_ap):
    NTOT = 2.0 * 8193.0 * 2048.0
    with ExitStack() as ctx:
        sb = lambda shape, dt=F32: ctx.enter_context(nc.sbuf_tensor(_nm(), shape, dt))
        stall = sb([8, 2])
        rowst = sb([1, 16])
        o8 = sb([8, 1])
        o1f = sb([1, 128])
        sccol = sb([128, 2])
        x2a = sb([128, TL])
        lg = [sb([128, TL]) for _ in range(2)]
        hsx = [sb([128, TL]) for _ in range(2)]
        x2b = [sb([128, TL], BF16) for _ in range(2)]
        sqb = [sb([128, TL], BF16) for _ in range(2)]
        rows = sb([1, 6 * C])
        t1f = sb([128, TL])
        t2f = sb([128, TL])
        ret = sb([128, TL])
        xr = sb([128, TL])
        lnw = sb([128, 16])
        lnb_ = sb([128, 16])
        o128 = sb([128, 128], BF16)
        ps = ctx.enter_context(nc.psum_tensor(_nm(), [128, TL], F32))
        pss = ctx.enter_context(nc.psum_tensor(_nm(), [128, TL], F32))

        p = Prog(nc, S)
        p.step("sync", lambda e: e.dma_start(out=stall[:, :], in_=statsall_ap),
               writes=[stall], dma=True)
        p.step("sync", lambda e: e.dma_start(out=o1f[0:1, :], in_=A["ones1f"]),
               writes=[o1f], dma=True)
        p.step("sync", lambda e: e.dma_start(out=o128[:, :], in_=A["ones128b"]),
               writes=[o128], dma=True)
        p.step("sync", lambda e: e.dma_start(out=lnw[:, :], in_=A["lnw_cols"]),
               writes=[lnw], dma=True)
        p.step("sync", lambda e: e.dma_start(out=lnb_[:, :], in_=A["lnb_cols"]),
               writes=[lnb_], dma=True)
        p.step("gpsimd", lambda e: e.memset(o8[:, :], 1.0), writes=[o8])
        p.step("tensor", lambda e: e.matmul(ps[0:1, 0:2], o8[:, 0:1], stall[:, :],
                                            start=True, stop=True),
               reads=[o8, stall], writes=[ps])

        for fs in [
            lambda e: e.tensor_scalar_mul(rowst[0:1, 0:1], ps[0:1, 0:1], 1.0 / NTOT),
            lambda e: e.tensor_mul(rowst[0:1, 2:3], rowst[0:1, 0:1], rowst[0:1, 0:1]),
            lambda e: e.tensor_scalar_mul(rowst[0:1, 2:3], rowst[0:1, 2:3], NTOT),
            lambda e: e.tensor_sub(rowst[0:1, 3:4], ps[0:1, 1:2], rowst[0:1, 2:3]),
            lambda e: e.tensor_scalar_mul(rowst[0:1, 3:4], rowst[0:1, 3:4],
                                          1.0 / (NTOT - 1.0)),
        ]:
            p.step("vector", fs, reads=[ps, rowst], writes=[rowst])

        p.step("scalar", lambda e: e.activation(rowst[0:1, 3:4], rowst[0:1, 3:4], AF.Ln),
               reads=[rowst], writes=[rowst])
        p.step("scalar", lambda e: e.activation(rowst[0:1, 4:5], rowst[0:1, 3:4],
                                                AF.Exp, scale=-0.5),
               reads=[rowst], writes=[rowst])

        p.step("vector", lambda e: e.tensor_mul(rowst[0:1, 5:6], rowst[0:1, 0:1],
                                                rowst[0:1, 4:5]),
               reads=[rowst], writes=[rowst])
        p.step("vector", lambda e: e.tensor_scalar_mul(rowst[0:1, 5:6], rowst[0:1, 5:6],
                                                       -1.0),
               reads=[rowst], writes=[rowst])
        p.step("tensor", lambda e: e.matmul(ps[:, 2:4], o1f[0:1, :], rowst[0:1, 4:6],
                                            start=True, stop=True),
               reads=[o1f, rowst], writes=[ps])
        p.step("vector", lambda e: e.tensor_copy(sccol[:, :], ps[:, 2:4]),
               reads=[ps], writes=[sccol])

        for g in range(NT):
            lgg, hsg = lg[g % 2], hsx[g % 2]
            x2bg, sqg = x2b[g % 2], sqb[g % 2]
            p.step("sync", lambda e, lgg=lgg, g=g: e.dma_start(
                out=lgg[:, :], in_=_tile(logh_ap, g)), writes=[lgg], dma=True)
            p.step("sync", lambda e, hsg=hsg, g=g: e.dma_start(
                out=hsg[:, :], in_=_tile(hs_ap, g)), writes=[hsg], dma=True)
            p.step("scalar", lambda e, lgg=lgg: e.activation(
                x2a[:, :], lgg[:, :], AF.Exp, scale=sccol[:, 0:1], bias=sccol[:, 1:2]),
                reads=[lgg, sccol], writes=[x2a])
            p.step("vector", lambda e, hsg=hsg: e.tensor_add(
                x2a[:, :], x2a[:, :], hsg[:, :]), reads=[x2a, hsg], writes=[x2a])
            p.step("gpsimd", lambda e, x2bg=x2bg: e.tensor_copy(x2bg[:, :], x2a[:, :]),
                   reads=[x2a], writes=[x2bg])
            p.step("scalar", lambda e, sqg=sqg: e.activation(sqg[:, :], x2a[:, :], AF.Square),
                   reads=[x2a], writes=[sqg])

            def mms(e, g=g, x2bg=x2bg, sqg=sqg):
                for n in range(4):
                    e.matmul(ps[:, n * 512:(n + 1) * 512], o128[:, :],
                             x2bg[:, n * 512:(n + 1) * 512],
                             start=(g == 0), stop=(g == NT - 1))
                for n in range(4):
                    i = e.matmul(pss[:, n * 512:(n + 1) * 512], o128[:, :],
                                 sqg[:, n * 512:(n + 1) * 512],
                                 start=(g == 0), stop=(g == NT - 1))
                return i
            p.step("tensor", mms, reads=[o128, x2bg, sqg], writes=[ps, pss])

        def rows1(e):
            e.tensor_scalar_mul(rows[0:1, 0:C], ps[0:1, :], 1.0 / C)
            e.tensor_scalar_mul(rows[0:1, C:2 * C], pss[0:1, :], 1.0 / C)
            e.tensor_mul(rows[0:1, 4 * C:5 * C], rows[0:1, 0:C], rows[0:1, 0:C])
            e.tensor_sub(rows[0:1, C:2 * C], rows[0:1, C:2 * C], rows[0:1, 4 * C:5 * C])
            return e.tensor_scalar_add(rows[0:1, 5 * C:6 * C], rows[0:1, C:2 * C], EPS_LN)
        p.step("vector", rows1, reads=[ps, pss], writes=[rows])

        def rows2(e):
            e.activation(rows[0:1, 5 * C:6 * C], rows[0:1, 5 * C:6 * C], AF.Ln)
            return e.activation(rows[0:1, 2 * C:3 * C], rows[0:1, 5 * C:6 * C],
                                AF.Exp, scale=-0.5)
        p.step("scalar", rows2, reads=[rows], writes=[rows])
        p.step("vector", lambda e: e.tensor_mul(rows[0:1, 3 * C:4 * C],
                                                rows[0:1, 0:C], rows[0:1, 2 * C:3 * C]),
               reads=[rows], writes=[rows])

        def bcast(e):
            for n in range(4):
                e.matmul(ps[:, n * 512:(n + 1) * 512], o1f[0:1, :],
                         rows[0:1, 2 * C + n * 512:2 * C + (n + 1) * 512],
                         start=True, stop=True)
            for n in range(4):
                i = e.matmul(pss[:, n * 512:(n + 1) * 512], o1f[0:1, :],
                             rows[0:1, 3 * C + n * 512:3 * C + (n + 1) * 512],
                             start=True, stop=True)
            return i
        p.step("tensor", bcast, reads=[o1f, rows], writes=[ps, pss])

        for g in range(NT):
            lgg, hsg = lg[g % 2], hsx[g % 2]
            p.step("sync", lambda e, lgg=lgg, g=g: e.dma_start(
                out=lgg[:, :], in_=_tile(logh_ap, g)), writes=[lgg], dma=True)
            p.step("sync", lambda e, hsg=hsg, g=g: e.dma_start(
                out=hsg[:, :], in_=_tile(hs_ap, g)), writes=[hsg], dma=True)
            p.step("sync", lambda e, g=g: e.dma_start(
                out=xr[:, :], in_=_tile(x_ap, g)), writes=[xr], dma=True)
            p.step("scalar", lambda e, lgg=lgg: e.activation(
                x2a[:, :], lgg[:, :], AF.Exp, scale=sccol[:, 0:1], bias=sccol[:, 1:2]),
                reads=[lgg, sccol], writes=[x2a])
            p.step("vector", lambda e, hsg=hsg: e.tensor_add(
                x2a[:, :], x2a[:, :], hsg[:, :]), reads=[x2a, hsg], writes=[x2a])

            def apply(e):
                e.scalar_tensor_tensor(t1f[:, :], x2a[:, :], 1.0, ps[:, :],
                                       ALU.mult, ALU.mult)
                return e.tensor_sub(t2f[:, :], t1f[:, :], pss[:, :])
            p.step("vector", apply, reads=[x2a, ps, pss], writes=[t1f, t2f])
            p.step("scalar", lambda e, g=g: e.activation(
                ret[:, :], t2f[:, :], AF.Identity, scale=lnw[:, g:g + 1], bias=lnb_[:, g:g + 1]),
                reads=[t2f, lnw, lnb_], writes=[ret])
            p.step("vector", lambda e: e.tensor_add(ret[:, :], ret[:, :], xr[:, :]),
                   reads=[ret, xr], writes=[ret])
            p.step("sync", lambda e, g=g: e.dma_start(
                out=_tile(x3_ap, g), in_=ret[:, :]), reads=[ret], dma=True)
        p.emit()


def emit_l1f(nc, S, A, x_ap, ln4b_ap, out_ap):
    emit_ln(nc, S, A, x_ap, out_bf=ln4b_ap)
    with ExitStack() as ctx:
        sb = lambda shape, dt=F32: ctx.enter_context(nc.sbuf_tensor(_nm(), shape, dt))
        lnb = sb([128, NT * TL], BF16)
        wbf = [sb([128, C]) for _ in range(2)]
        awb = sb([128, C], BF16)
        sgb = sb([128, C], BF16)
        sgf = sb([128, C])
        arow = sb([1, 128])
        acol = sb([128, 1])
        bcol = sb([128, 16])
        o128 = sb([128, 128], BF16)
        o11 = sb([1, 1])
        ot = [sb([128, TL]) for _ in range(2)]
        xr = [sb([128, TL]) for _ in range(2)]
        PA = ctx.enter_context(nc.psum_tensor(_nm(), [128, TL], F32))
        PAL = ctx.enter_context(nc.psum_tensor(_nm(), [128, 128], F32))
        PAC = ctx.enter_context(nc.psum_tensor(_nm(), [128, 1], F32))

        p = Prog(nc, S)
        p.step("sync", lambda e: e.dma_start(out=o128[:, :], in_=A["ones128b"]),
               writes=[o128], dma=True)
        p.step("sync", lambda e: e.dma_start(out=bcol[:, :], in_=A["bias_cols"]),
               writes=[bcol], dma=True)
        p.step("gpsimd", lambda e: e.memset(o11[:, :], 1.0), writes=[o11])
        for g in range(NT):
            p.step("sync", lambda e, g=g: e.dma_start(
                out=lnb[:, g * TL:(g + 1) * TL], in_=_tile(ln4b_ap, g)),
                writes=[lnb], dma=True)
        for j in range(NT):
            wbj = wbf[j % 2]
            PX = PA
            otj, xrj = ot[j % 2], xr[j % 2]
            p.step("sync", lambda e, wbj=wbj, j=j: e.dma_start(
                out=wbj[:, :], in_=A["wbP"][j, :, :]), writes=[wbj], dma=True)
            p.step("scalar", lambda e, wbj=wbj: e.activation(awb[:, :], wbj[:, :], AF.Abs),
                   reads=[wbj], writes=[awb])

            def amm(e):
                for ct in range(NT):
                    i = e.matmul(PAL[:, :], o128[:, :], awb[:, ct * 128:(ct + 1) * 128],
                                 start=(ct == 0), stop=(ct == NT - 1))
                return i
            p.step("tensor", amm, reads=[o128, awb], writes=[PAL])
            p.step("vector", lambda e: e.tensor_scalar_mul(arow[0:1, :], PAL[0:1, :], 1.0 / C),
                   reads=[PAL], writes=[arow])
            p.step("tensor", lambda e: e.matmul(PAC[:, :], arow[0:1, :], o11[0:1, :],
                                                start=True, stop=True),
                   reads=[arow, o11], writes=[PAC])
            p.step("vector", lambda e: e.tensor_copy(acol[:, :], PAC[:, :]),
                   reads=[PAC], writes=[acol])

            def sgn(e, wbj=wbj):
                e.tensor_single_scalar(sgf[:, :], wbj[:, :], 0.0, ALU.is_ge)
                return e.tensor_scalar(sgb[:, :], sgf[:, :], 2.0, -1.0, ALU.mult, ALU.add)
            p.step("vector", sgn, reads=[wbj], writes=[sgf, sgb])
            p.step("tensor", lambda e, PX=PX: _mm_ct(e, PX, sgb, lnb),
                   reads=[sgb, lnb], writes=[PX])
            p.step("sync", lambda e, xrj=xrj, j=j: e.dma_start(
                out=xrj[:, :], in_=_tile(x_ap, j)), writes=[xrj], dma=True)
            p.step("scalar", lambda e, PX=PX, otj=otj, j=j: e.activation(
                otj[:, :], PX[:, :], AF.Identity, scale=acol[:, 0:1], bias=bcol[:, j:j + 1]),
                reads=[PX, acol, bcol], writes=[otj])
            p.step("vector", lambda e, otj=otj, xrj=xrj: e.tensor_add(
                otj[:, :], otj[:, :], xrj[:, :]), reads=[otj, xrj], writes=[otj])
            p.step("sync", lambda e, otj=otj, j=j: e.dma_start(
                out=_tile(out_ap, j), in_=otj[:, :]), reads=[otj], dma=True)
        p.emit()


def emit_allgather(nc, S, in_ap, out_ap):
    p = Prog(nc, S)
    p.step("gpsimd", lambda e: e.collective_compute(
        "AllGather", mybir.AluOpType.bypass,
        replica_groups=[list(range(W))],
        ins=[in_ap], outs=[out_ap]),
        reads=[in_ap], writes=[out_ap])
    p.step("gpsimd", lambda e: e.engine_nop(), reads=[out_ap])
    p.emit()


def _mk_nc():
    return bass.Bass("TRN2", target_bir_lowering=False, debug=False)


_SHAPES = {
    "wqkvP": ([48, 128, C], F32), "woutP": ([16, 128, C], F32),
    "wupP": ([64, 128, C], F32), "wgateP": ([64, 128, C], F32),
    "wdownP": ([16, 128, INTER], F32), "wzP": ([16, 128, C], F32),
    "whP": ([16, 128, C], F32), "wbP": ([16, 128, C], F32),
    "decay_row": ([1, H], F32), "su_ug_rows": ([1, C], F32),
    "su_dp_cols": ([128, 64], F32), "wsu_up_cols": ([128, 64], F32),
    "wsu_gt_cols": ([128, 64], F32), "wsd_cols": ([128, 16], F32),
    "bz_cols": ([128, 16], F32), "bh_cols": ([128, 16], F32),
    "lnw_cols": ([128, 16], F32), "lnb_cols": ([128, 16], F32),
    "h0_cols": ([128, 16], F32), "bias_cols": ([128, 16], F32),
    "ones1f": ([1, 128], F32), "ones_row": ([1, C], F32),
    "l0cm": ([128, 24], F32), "l1m": ([128, 8], F32), "l1mneg": ([128, 8], F32),
    "f_col": ([128, 1], F32), "gneg_col": ([128, 1], F32),
    "H128b": ([128, 128], BF16), "TRIb": ([128, 128], BF16),
    "ones128b": ([128, 128], BF16),
}


class LazyAps(dict):
    def __init__(self, nc):
        super().__init__()
        self.nc = nc

    def __missing__(self, k):
        shape, dt = _SHAPES[k]
        ap = self.nc.dram_tensor(k, shape, dt, kind="ExternalInput").ap()
        self[k] = ap
        return ap


def _shared_aps(nc):
    return LazyAps(nc)


_PROG_CACHE = {}


def _build(name):
    if name in _PROG_CACHE:
        return _PROG_CACHE[name]
    nc = _mk_nc()
    A = _shared_aps(nc)
    with ExitStack() as ctx:
        S = Sems(ctx, nc)
        if name == "l0a1":
            x = nc.dram_tensor("xT", [C, TL], F32, kind="ExternalInput").ap()
            ln0 = nc.dram_tensor("ln0b", [C, TL], BF16, kind="ExternalOutput").ap()
            st = nc.dram_tensor("states", [128, 32], F32, kind="ExternalOutput").ap()
            emit_l0a(nc, S, A, x, ln0, st, phase2=False)
        elif name in ("l0a2", "l0a2dbg"):
            x = nc.dram_tensor("xT", [C, TL], F32, kind="ExternalInput").ap()
            ln0 = nc.dram_tensor("ln0b", [C, TL], BF16, kind="ExternalInput").ap()
            sall = nc.dram_tensor("states_all", [W * 128, 32], F32, kind="ExternalInput").ap()
            x1 = nc.dram_tensor("x1T", [C, TL], F32, kind="ExternalOutput").ap()
            yT = nc.dram_tensor("yT", [C, TL], BF16,
                                kind="ExternalOutput" if name == "l0a2dbg" else "Internal").ap()
            emit_l0a(nc, S, A, x, ln0, None, phase2=True, x1_ap=x1, states_all_ap=sall,
                     y_ap=yT)
        elif name in ("l0f", "l0fdbg"):
            x = nc.dram_tensor("xT", [C, TL], F32, kind="ExternalInput").ap()
            x2 = nc.dram_tensor("x2T", [C, TL], F32, kind="ExternalOutput").ap()
            lnsu = nc.dram_tensor("lnsu", [C, TL], F32,
                                  kind="ExternalOutput" if name == "l0fdbg" else "Internal").ap()
            x1h = nc.dram_tensor("x1h", [C, TL], BF16,
                                 kind="ExternalOutput" if name == "l0fdbg" else "Internal").ap()
            x1c_d = (nc.dram_tensor("x1c_dbg", [C, TL], BF16, kind="ExternalOutput").ap()
                     if name == "l0fdbg" else None)
            w2 = nc.dram_tensor("w2", [16, 128, INTER], BF16).ap()
            p6 = nc.dram_tensor("p6", [128, 64 * 512], BF16).ap()
            emit_l0f(nc, S, A, x, x2, lnsu, x1h, w2, p6,
                     x1c_ap=(x1c_d if name == "l0fdbg" else None))
        elif name == "l1a1":
            x = nc.dram_tensor("xT", [C, TL], F32, kind="ExternalInput").ap()
            hs = nc.dram_tensor("hsT", [C, TL], F32, kind="ExternalOutput").ap()
            l = nc.dram_tensor("lT", [C, TL], F32, kind="ExternalOutput").ap()
            ast = nc.dram_tensor("astT", [C, TL], F32, kind="ExternalOutput").ap()
            mz = nc.dram_tensor("mz", [128, 32], F32, kind="ExternalOutput").ap()
            emit_l1a1(nc, S, A, x, hs, l, ast, mz, None)
        elif name == "l1a2":
            l = nc.dram_tensor("lT", [C, TL], F32, kind="ExternalInput").ap()
            ast = nc.dram_tensor("astT", [C, TL], F32, kind="ExternalInput").ap()
            mzall = nc.dram_tensor("mz_all", [W * 128, 32], F32, kind="ExternalInput").ap()
            logh = nc.dram_tensor("loghT", [C, TL], F32, kind="ExternalOutput").ap()
            stats = nc.dram_tensor("stats", [1, 2], F32, kind="ExternalOutput").ap()
            emit_l1a2(nc, S, A, l, ast, mzall, logh, stats)
        elif name == "l1a3":
            x = nc.dram_tensor("xT", [C, TL], F32, kind="ExternalInput").ap()
            hs = nc.dram_tensor("hsT", [C, TL], F32, kind="ExternalInput").ap()
            logh = nc.dram_tensor("loghT", [C, TL], F32, kind="ExternalInput").ap()
            stall = nc.dram_tensor("stats_all", [W, 2], F32, kind="ExternalInput").ap()
            x3 = nc.dram_tensor("x3T", [C, TL], F32, kind="ExternalOutput").ap()
            emit_l1a3(nc, S, A, x, hs, logh, stall, x3)
        elif name == "l1f":
            x = nc.dram_tensor("xT", [C, TL], F32, kind="ExternalInput").ap()
            ln4 = nc.dram_tensor("ln4b", [C, TL], BF16).ap()
            out = nc.dram_tensor("outT", [C, TL], F32, kind="ExternalOutput").ap()
            emit_l1f(nc, S, A, x, ln4, out)
        elif name == "full":
            x = nc.dram_tensor("xT", [C, TL], F32, kind="ExternalInput").ap()
            out = nc.dram_tensor("outT", [C, TL], F32, kind="ExternalOutput").ap()
            dr = lambda nm, shape, dt=F32, sp="Local": nc.dram_tensor(
                nm, shape, dt, addr_space=sp).ap()
            ln0 = dr("ln0b", [C, TL], BF16)
            states = dr("states", [128, 32])
            states_all = dr("states_all", [W * 128, 32], sp="Shared")
            x1 = dr("x1T", [C, TL])
            yT = dr("yT", [C, TL], BF16)
            x2 = dr("x2T", [C, TL])
            lnsu = dr("lnsu", [C, TL])
            x1h = dr("x1h", [C, TL], BF16)
            w2 = dr("w2", [16, 128, INTER], BF16)
            p6 = dr("p6", [128, 64 * 512], BF16)
            hs = dr("hsT", [C, TL])
            l = dr("lT", [C, TL])
            ast = dr("astT", [C, TL])
            mz = dr("mz", [128, 32])
            mz_all = dr("mz_all", [W * 128, 32], sp="Shared")
            logh = dr("loghT", [C, TL])
            stats = dr("stats", [1, 2])
            stats_all = dr("stats_all", [W, 2], sp="Shared")
            x3 = dr("x3T", [C, TL])
            ln4 = dr("ln4b", [C, TL], BF16)
            emit_l0a(nc, S, A, x, ln0, states, phase2=False)
            emit_allgather(nc, S, states, states_all)
            emit_l0a(nc, S, A, x, ln0, None, phase2=True, x1_ap=x1,
                     states_all_ap=states_all, y_ap=yT)
            emit_l0f(nc, S, A, x1, x2, lnsu, x1h, w2, p6)
            emit_l1a1(nc, S, A, x2, hs, l, ast, mz, None)
            emit_allgather(nc, S, mz, mz_all)
            emit_l1a2(nc, S, A, l, ast, mz_all, logh, stats)
            emit_allgather(nc, S, stats, stats_all)
            emit_l1a3(nc, S, A, x2, hs, logh, stats_all, x3)
            emit_l1f(nc, S, A, x3, ln4, out)
        else:
            raise KeyError(name)
    _PROG_CACHE[name] = nc
    return nc


def _run(name, sh, per_core_extra):
    nc = _build(name)
    maps = [{**sh, **per_core_extra[c]} for c in range(W)]
    res = run_bass_kernel_spmd(nc, maps, core_ids=list(range(W)))
    return res.results


FUSED = True


def kernel(**inputs):
    sh, pc = pack_inputs(inputs)
    core = [dict(pc[c]) for c in range(W)]

    if FUSED:
        r = _run("full", sh, core)
        out = np.empty((2, 8192, C), np.float32)
        for c in range(W):
            b, s = c // 4, c % 4
            out[b, s * TL:(s + 1) * TL, :] = np.asarray(r[c]["outT"]).T
        return out

    r = _run("l0a1", sh, core)
    ln0 = [np.asarray(r[c]["ln0b"]) for c in range(W)]
    states_all = np.concatenate([np.asarray(r[c]["states"]) for c in range(W)], axis=0)
    for c in range(W):
        core[c]["ln0b"] = ln0[c]
        core[c]["states_all"] = states_all

    r = _run("l0a2", sh, core)
    for c in range(W):
        core[c]["xT"] = np.asarray(r[c]["x1T"])

    r = _run("l0f", sh, core)
    for c in range(W):
        core[c]["xT"] = np.asarray(r[c]["x2T"])

    r = _run("l1a1", sh, core)
    mz_all = np.concatenate([np.asarray(r[c]["mz"]) for c in range(W)], axis=0)
    for c in range(W):
        core[c]["hsT"] = np.asarray(r[c]["hsT"])
        core[c]["lT"] = np.asarray(r[c]["lT"])
        core[c]["astT"] = np.asarray(r[c]["astT"])
        core[c]["mz_all"] = mz_all

    r = _run("l1a2", sh, core)
    stats_all = np.concatenate([np.asarray(r[c]["stats"]) for c in range(W)], axis=0)
    for c in range(W):
        core[c]["loghT"] = np.asarray(r[c]["loghT"])
        core[c]["stats_all"] = stats_all

    r = _run("l1a3", sh, core)
    for c in range(W):
        core[c]["xT"] = np.asarray(r[c]["x3T"])

    r = _run("l1f", sh, core)
    out = np.empty((2, 8192, C), np.float32)
    for c in range(W):
        b, s = c // 4, c % 4
        out[b, s * TL:(s + 1) * TL, :] = np.asarray(r[c]["outT"]).T
    return out


# version 08
# speedup vs baseline: 1.1414x; 1.1414x over previous
import math
import numpy as np
import ml_dtypes
from contextlib import ExitStack

import concourse.bass as bass
import concourse.mybir as mybir
from concourse.bass_utils import run_bass_kernel_spmd

F32 = mybir.dt.float32
BF16 = mybir.dt.bfloat16
U8 = mybir.dt.uint8
AF = mybir.ActivationFunctionType
ALU = mybir.AluOpType

C = 2048
TL = 2048
NT = 16
H = 16
W = 8
INTER = 8192
NEG = -1.0e30
EPS_LN = 1e-5
R2048 = 1.0 / math.sqrt(2048.0)
R8192 = 1.0 / math.sqrt(8192.0)
NBF = ml_dtypes.bfloat16

ENGINES = ("sync", "gpsimd", "tensor", "scalar", "vector")

_NMC = [0]


def _nm():
    _NMC[0] += 1
    return "t%d" % _NMC[0]


def _butterfly_plan(n, dists):
    slot = list(range(n))
    spare = n
    steps = []
    for dist in dists:
        stage = []
        for base in range(0, n, 2 * dist):
            for a in range(base, base + dist):
                b = a + dist
                pa, pb = slot[a], slot[b]
                stage.append((pa, pb, spare))
                slot[b] = spare
                spare = pb
        steps.append(stage)
    return steps, slot


def _pack_ktiles(w):
    K, M = w.shape
    nk, nj = K // 128, M // 128
    p = w.reshape(nk, 128, nj, 128).transpose(2, 1, 0, 3).reshape(nj, 128, K)
    return np.ascontiguousarray(p)


def _cols(v):
    return np.ascontiguousarray(v.reshape(-1, 128).T)


def _hadamard(n):
    Hm = np.array([[1.0]], dtype=np.float32)
    while Hm.shape[0] < n:
        Hm = np.block([[Hm, Hm], [Hm, -Hm]])
    return np.ascontiguousarray(Hm.astype(np.float32))


def pack_inputs(inp):
    f32 = np.float32
    sh = {}
    sh["wqkvP"] = _pack_ktiles(np.asarray(inp["L0_attn_w_qkv"], f32))
    sh["woutP"] = _pack_ktiles(np.asarray(inp["L0_attn_w_out"], f32))
    sh["wupP"] = _pack_ktiles(np.ascontiguousarray(np.asarray(inp["L0_ffn_W_up"], f32).T))
    sh["wgateP"] = _pack_ktiles(np.ascontiguousarray(np.asarray(inp["L0_ffn_W_gate"], f32).T))
    wdT = np.ascontiguousarray(np.asarray(inp["L0_ffn_W_down"], f32).T)
    sh["wdownP"] = _pack_ktiles(wdT)
    sh["wzP"] = _pack_ktiles(np.ascontiguousarray(np.asarray(inp["L1_attn_W_z"], f32).T))
    sh["whP"] = _pack_ktiles(np.ascontiguousarray(np.asarray(inp["L1_attn_W_h"], f32).T))
    sh["wbP"] = _pack_ktiles(np.ascontiguousarray(np.asarray(inp["L1_ffn_weight"], f32).T))

    sh["decay_row"] = np.ascontiguousarray(np.asarray(inp["L0_attn_decay_param"], f32).reshape(1, H))
    sh["su_ug_rows"] = np.ascontiguousarray(np.asarray(inp["L0_ffn_SU_ug"], f32).reshape(1, C))
    sh["su_dp_cols"] = _cols(np.asarray(inp["L0_ffn_SU_dp"], f32))
    sh["wsu_up_cols"] = _cols(np.asarray(inp["L0_ffn_Wscale_ug"], f32)[:INTER])
    sh["wsu_gt_cols"] = _cols(np.asarray(inp["L0_ffn_Wscale_ug"], f32)[INTER:])
    sh["wsd_cols"] = _cols(np.asarray(inp["L0_ffn_Wscale_dp"], f32))
    sh["bz_cols"] = _cols(np.asarray(inp["L1_attn_b_z"], f32))
    sh["bh_cols"] = _cols(np.asarray(inp["L1_attn_b_h"], f32))
    sh["lnw_cols"] = _cols(np.asarray(inp["L1_attn_ln_w"], f32))
    sh["lnb_cols"] = _cols(np.asarray(inp["L1_attn_ln_b"], f32))
    sh["h0_cols"] = _cols(np.asarray(inp["L1_attn_h0"], f32))
    sh["bias_cols"] = _cols(np.asarray(inp["L1_ffn_bias"], f32))

    sh["H128b"] = _hadamard(128).astype(NBF)
    sh["TRIb"] = np.ascontiguousarray(np.triu(np.ones((128, 128), f32)).astype(NBF))
    sh["ones1f"] = np.ones((1, 128), f32)
    sh["ones128b"] = np.ones((128, 128), NBF)
    sh["ones_row"] = np.ones((1, C), f32)

    x = np.asarray(inp["L0_attn_x"], f32)
    pc = []
    for c in range(W):
        b, s = c // 4, c % 4
        d = {}
        d["xT"] = np.ascontiguousarray(x[b, s * TL:(s + 1) * TL, :].T)
        cm = np.zeros((3, W), f32)
        m = np.zeros((W,), f32)
        for j in range(W):
            bj, sj = j // 4, j % 4
            if bj == b and sj < s:
                m[j] = 1.0
                g = s - 1 - sj
                if g < 3:
                    cm[g, j] = 1.0
        d["l0cm"] = np.ascontiguousarray(np.broadcast_to(cm.T.reshape(1, W * 3), (128, W * 3)))
        d["l1m"] = np.ascontiguousarray(np.broadcast_to(m.reshape(1, W), (128, W)))
        d["l1mneg"] = np.ascontiguousarray(np.broadcast_to(((m - 1.0) * 1e30).reshape(1, W), (128, W)))
        first = 1.0 if s == 0 else 0.0
        d["f_col"] = np.full((128, 1), first, f32)
        d["gneg_col"] = np.full((128, 1), (first - 1.0) * 1e30, f32)
        pc.append(d)
    return sh, pc


class Sems:
    def __init__(self, ctx, nc):
        self.nc = nc
        self.ctx = ctx
        self.sem = {e: ctx.enter_context(nc.semaphore(name="sem_" + e)) for e in ENGINES}
        self.cnt = {e: 0 for e in ENGINES}
        self.dpool = []
        self.dcnt = []

    def dma_pool(self, n):
        while len(self.dpool) < n:
            self.dpool.append(self.ctx.enter_context(
                self.nc.semaphore(name="dma%d" % len(self.dpool))))
            self.dcnt.append(0)
        return self.dpool


class Prog:

    def __init__(self, nc, S):
        self.nc = nc
        self.S = S
        self.steps = []
        self.last_writer = {}
        self.readers = {}

    def step(self, engine, fn, reads=(), writes=(), dma=False, fence=False,
             hazard=False):
        idx = len(self.steps)
        deps = set()
        for r in reads:
            k = id(r)
            if k in self.last_writer:
                deps.add(self.last_writer[k])
        for w in writes:
            k = id(w)
            if k in self.last_writer:
                deps.add(self.last_writer[k])
            for rd in self.readers.get(k, ()):
                deps.add(rd)
        for w in writes:
            k = id(w)
            self.last_writer[k] = idx
            self.readers[k] = []
        for r in reads:
            self.readers.setdefault(id(r), []).append(idx)
        if dma:
            key = ("d", id(writes[0]) if writes else id(reads[0]))
        else:
            key = engine
        self.steps.append([engine, fn, 16 if dma else 1, sorted(deps), key])

    def emit(self):
        S = self.S
        steps = self.steps
        slot = {}
        for st in steps:
            if not isinstance(st[4], str) and st[4] not in slot:
                slot[st[4]] = len(slot)
        pool = S.dma_pool(len(slot))
        cnt = dict(S.cnt)
        for k, s_ in slot.items():
            cnt[k] = S.dcnt[s_]
        after = []
        for st in steps:
            cnt[st[4]] += st[2]
            after.append(cnt[st[4]])

        def handle(key):
            return S.sem[key] if isinstance(key, str) else pool[slot[key]]

        waits = []
        marks = {}
        for i, (e, fn, inc, deps, key) in enumerate(steps):
            wl = {}
            for d in deps:
                kd = steps[d][4]
                v = after[d]
                if kd == e:
                    wl[kd] = max(wl.get(kd, 0), v)
                    continue
                m = marks.get((e, kd))
                if m is None:
                    m = S.cnt[kd] if isinstance(kd, str) else S.dcnt[slot[kd]]
                if v > m:
                    wl[kd] = max(wl.get(kd, 0), v)
            for kd, v in wl.items():
                marks[(e, kd)] = max(marks.get((e, kd), 0), v)
            waits.append(wl)

        dkeys = {}
        for i, st in enumerate(steps):
            if not isinstance(st[4], str):
                dkeys[st[4]] = after[i]
        qeng = "sync" if any(st[0] == "sync" for st in steps) else (
            "gpsimd" if any(st[0] == "gpsimd" for st in steps) else None)

        def mkbody(mine, eng_name):
            def body(eng):
                for i in mine:
                    fn, inc, key = steps[i][1], steps[i][2], steps[i][4]
                    for kd, v in sorted(waits[i].items(), key=lambda kv: str(kv[0])):
                        eng.wait_ge(handle(kd), v)
                    fn(eng).then_inc(handle(key), inc)
                if eng_name == qeng:
                    for kd, v in sorted(dkeys.items(), key=lambda kv: str(kv[0])):
                        eng.wait_ge(handle(kd), v)
            return body

        with self.nc.Block() as blk:
            for eng_name in ENGINES:
                mine = [i for i, st in enumerate(steps) if st[0] == eng_name]
                if not mine:
                    continue
                getattr(blk, eng_name)(mkbody(mine, eng_name))
        for st in steps:
            if isinstance(st[4], str):
                S.cnt[st[4]] += st[2]
        for k, s_ in slot.items():
            S.dcnt[s_] = dkeys[k]


def _tile(ap, g, t0=0, t1=TL):
    return ap[g * 128:(g + 1) * 128, t0:t1]


def emit_ln(nc, S, A, x_ap, out_bf=None, out_f32=None, su=False):
    with ExitStack() as ctx:
        sb = lambda shape, dt=F32: ctx.enter_context(nc.sbuf_tensor(_nm(), shape, dt))
        xs = [sb([128, TL]) for _ in range(2)]
        xb = [sb([128, TL], BF16) for _ in range(2)]
        sq = [sb([128, TL], BF16) for _ in range(2)]
        rows = sb([1, 6 * C])
        t1f = sb([128, TL])
        t2f = [sb([128, TL]) for _ in range(2)]
        t2b = [sb([128, TL], BF16) for _ in range(2)]
        o128 = sb([128, 128], BF16)
        onesr = sb([1, C])
        ps = ctx.enter_context(nc.psum_tensor(_nm(), [128, TL], F32))
        pss = ctx.enter_context(nc.psum_tensor(_nm(), [128, TL], F32))

        p = Prog(nc, S)
        p.step("sync", lambda e: e.dma_start(out=o128[:, :], in_=A["ones128b"]),
               writes=[o128], dma=True)
        srcrow = A["su_ug_rows"] if su else A["ones_row"]
        p.step("sync", lambda e, s=srcrow: e.dma_start(out=onesr[0:1, :], in_=s),
               writes=[onesr], dma=True)
        for g in range(NT):
            xg, xbg, sqg = xs[g % 2], xb[g % 2], sq[g % 2]
            p.step("sync", lambda e, xg=xg, g=g: e.dma_start(out=xg[:, :], in_=_tile(x_ap, g)),
                   writes=[xg], dma=True)
            p.step("gpsimd", lambda e, xg=xg, xbg=xbg: e.tensor_copy(xbg[:, :], xg[:, :]),
                   reads=[xg], writes=[xbg])
            p.step("scalar", lambda e, xg=xg, sqg=sqg: e.activation(sqg[:, :], xg[:, :], AF.Square),
                   reads=[xg], writes=[sqg])

            def mms(e, g=g, xbg=xbg, sqg=sqg):
                for n in range(4):
                    e.matmul(ps[:, n * 512:(n + 1) * 512], o128[:, :],
                             xbg[:, n * 512:(n + 1) * 512],
                             start=(g == 0), stop=(g == NT - 1))
                for n in range(4):
                    i = e.matmul(pss[:, n * 512:(n + 1) * 512], o128[:, :],
                                 sqg[:, n * 512:(n + 1) * 512],
                                 start=(g == 0), stop=(g == NT - 1))
                return i
            p.step("tensor", mms, reads=[o128, xbg, sqg], writes=[ps, pss])
        p.emit()

        p = Prog(nc, S)

        def rows1(e):
            e.tensor_scalar_mul(rows[0:1, 0:C], ps[0:1, :], 1.0 / C)
            e.tensor_scalar_mul(rows[0:1, C:2 * C], pss[0:1, :], 1.0 / C)
            e.tensor_mul(rows[0:1, 4 * C:5 * C], rows[0:1, 0:C], rows[0:1, 0:C])
            e.tensor_sub(rows[0:1, C:2 * C], rows[0:1, C:2 * C], rows[0:1, 4 * C:5 * C])
            return e.tensor_scalar_add(rows[0:1, 5 * C:6 * C], rows[0:1, C:2 * C], EPS_LN)
        p.step("vector", rows1, reads=[ps, pss], writes=[rows])

        def rows2(e):
            e.activation(rows[0:1, 5 * C:6 * C], rows[0:1, 5 * C:6 * C], AF.Ln)
            return e.activation(rows[0:1, 2 * C:3 * C], rows[0:1, 5 * C:6 * C],
                                AF.Exp, scale=-0.5)
        p.step("scalar", rows2, reads=[rows], writes=[rows])
        p.step("vector", lambda e: e.tensor_mul(rows[0:1, 3 * C:4 * C],
                                                rows[0:1, 0:C], rows[0:1, 2 * C:3 * C]),
               reads=[rows], writes=[rows])

        ngrp = NT if su else 1
        for g in range(NT):
            xg = xs[g % 2]
            of = t2f[g % 2]
            ob = t2b[g % 2]
            p.step("sync", lambda e, xg=xg, g=g: e.dma_start(out=xg[:, :], in_=_tile(x_ap, g)),
                   writes=[xg], dma=True)
            if g < ngrp:
                def bcast(e, g=g):
                    for n in range(4):
                        e.matmul(ps[:, n * 512:(n + 1) * 512],
                                 onesr[0:1, g * 128:(g + 1) * 128],
                                 rows[0:1, 2 * C + n * 512:2 * C + (n + 1) * 512],
                                 start=True, stop=True)
                    for n in range(4):
                        i = e.matmul(pss[:, n * 512:(n + 1) * 512],
                                     onesr[0:1, g * 128:(g + 1) * 128],
                                     rows[0:1, 3 * C + n * 512:3 * C + (n + 1) * 512],
                                     start=True, stop=True)
                    return i
                p.step("tensor", bcast, reads=[onesr, rows], writes=[ps, pss])

            def apply(e, xg=xg, of=of):
                e.scalar_tensor_tensor(t1f[:, :], xg[:, :], 1.0, ps[:, :],
                                       ALU.mult, ALU.mult)
                return e.tensor_sub(of[:, :], t1f[:, :], pss[:, :])
            p.step("vector", apply, reads=[xg, ps, pss], writes=[t1f, of])
            if out_bf is not None:
                p.step("vector", lambda e, of=of, ob=ob: e.tensor_copy(ob[:, :], of[:, :]),
                       reads=[of], writes=[ob])
                p.step("gpsimd", lambda e, ob=ob, g=g: e.dma_start(out=_tile(out_bf, g), in_=ob[:, :]),
                       reads=[ob], dma=True)
            if out_f32 is not None:
                p.step("gpsimd", lambda e, of=of, g=g: e.dma_start(out=_tile(out_f32, g), in_=of[:, :]),
                       reads=[of], dma=True)
        p.emit()


def _sigmoid_rows(p, drow, drow2, drow3):
    p.step("scalar", lambda e: e.activation(drow[0:1, 16:32], drow[0:1, 0:16],
                                            AF.Exp, scale=-1.0),
           reads=[drow], writes=[drow])
    p.step("vector", lambda e: e.tensor_scalar_add(drow3[0:1, 0:16],
                                                   drow[0:1, 16:32], 1.0),
           reads=[drow], writes=[drow3])
    p.step("vector", lambda e: e.reciprocal(drow2[0:1, 0:16], drow3[0:1, 0:16]),
           reads=[drow3], writes=[drow2], fence=True, hazard=True)


def _phi(p, psum, e_scr, out):
    p.step("vector", lambda e: e.tensor_scalar_min(e_scr[:, :], psum[:, :], 0.0),
           reads=[psum], writes=[e_scr])
    p.step("scalar", lambda e: e.activation(e_scr[:, :], e_scr[:, :], AF.Exp),
           reads=[e_scr], writes=[e_scr])
    p.step("vector", lambda e: e.scalar_tensor_tensor(out[:, :], psum[:, :], 1.0,
                                                      e_scr[:, :], ALU.add, ALU.max),
           reads=[psum, e_scr], writes=[out])


def _mm_ct(e, psum, wtile, lnb, nts=NT, tl=TL):
    nch = tl // 512
    for ct in range(nts):
        for n in range(4 if tl == TL else nch):
            i = e.matmul(psum[:, n * 512:(n + 1) * 512],
                         wtile[:, ct * 128:(ct + 1) * 128],
                         lnb[:, ct * tl + n * 512:ct * tl + (n + 1) * 512],
                         start=(ct == 0), stop=(ct == nts - 1))
    return i


def emit_l0a(nc, S, A, x_ap, ln0b_ap, states_ap, phase2, x1_ap=None,
             states_all_ap=None, y_ap=None):
    if not phase2:
        emit_ln(nc, S, A, x_ap, out_bf=ln0b_ap)
    with ExitStack() as ctx:
        sb = lambda shape, dt=F32: ctx.enter_context(nc.sbuf_tensor(_nm(), shape, dt))
        lnb = sb([128, NT * TL], BF16)
        wk = sb([128, C], BF16)
        wv = sb([128, C], BF16)
        wq = sb([128, C], BF16)
        kf = sb([128, TL])
        qf = sb([128, TL])
        kv = sb([128, TL])
        es = sb([128, TL])
        sckv = sb([128, TL])
        scks = sb([128, TL])
        qks = sb([128, TL], BF16)
        drow = sb([1, 64])
        drow2 = sb([1, 16])
        drow3 = sb([1, 16])
        denr = sb([1, TL])
        denr2 = sb([1, TL])
        dcols = sb([128, 16])
        stc = sb([128, 32])
        o1f = sb([1, 128])
        o128 = sb([128, 128], BF16)
        PA = ctx.enter_context(nc.psum_tensor(_nm(), [128, TL], F32))
        PB = ctx.enter_context(nc.psum_tensor(_nm(), [128, TL], F32))

        p = Prog(nc, S)
        for g in range(NT):
            p.step("sync", lambda e, g=g: e.dma_start(
                out=lnb[:, g * TL:(g + 1) * TL], in_=_tile(ln0b_ap, g)),
                writes=[lnb], dma=True)
        p.step("sync", lambda e: e.dma_start(out=drow[0:1, 0:16], in_=A["decay_row"]),
               writes=[drow], dma=True)
        p.step("sync", lambda e: e.dma_start(out=o1f[0:1, :], in_=A["ones1f"]),
               writes=[o1f], dma=True)
        p.step("sync", lambda e: e.dma_start(out=o128[:, :], in_=A["ones128b"]),
               writes=[o128], dma=True)
        _sigmoid_rows(p, drow, drow2, drow3)
        p.step("tensor", lambda e: e.matmul(PA[:, 0:16], o1f[0:1, :], drow2[0:1, 0:16],
                                            start=True, stop=True),
               reads=[o1f, drow2], writes=[PA])
        p.step("vector", lambda e: e.tensor_copy(dcols[:, :], PA[:, 0:16]),
               reads=[PA], writes=[dcols])

        if phase2:
            sall = sb([128, 8 * 32])
            cmks = sb([128, 24])
            dLc = sb([128, 16 * 4])
            Pkv = sb([128, 16])
            Pks = sb([128, 16])
            coef = sb([128, 16])
            ys = sb([128, TL], BF16)
            p.step("sync", lambda e: e.dma_start(
                out=sall[:, :], in_=states_all_ap.rearrange("(j p) s -> p j s", p=128)),
                writes=[sall], dma=True)
            p.step("sync", lambda e: e.dma_start(out=cmks[:, :], in_=A["l0cm"]),
                   writes=[cmks], dma=True)
            p.step("scalar", lambda e: e.activation(dLc[:, 16:32], dcols[:, :], AF.Ln),
                   reads=[dcols], writes=[dLc])

            p.step("scalar", lambda e: e.activation(dLc[:, 32:48], dLc[:, 16:32],
                                                    AF.Exp, scale=4096.0),
                   reads=[dLc], writes=[dLc])
            p.step("scalar", lambda e: e.activation(dLc[:, 16:32], dLc[:, 16:32],
                                                    AF.Exp, scale=2048.0),
                   reads=[dLc], writes=[dLc])
            p.step("gpsimd", lambda e: e.memset(dLc[:, 0:16], 1.0), writes=[dLc])
            p.step("gpsimd", lambda e: e.memset(Pkv[:, :], 0.0), writes=[Pkv])
            p.step("gpsimd", lambda e: e.memset(Pks[:, :], 0.0), writes=[Pks])
            for j in range(W):
                fr = [dLc, cmks, sall, Pkv, Pks, coef]
                fw = [coef, dLc, Pkv, Pks]
                for fs in [
                    lambda e, j=j: e.tensor_scalar(coef[:, :], dLc[:, 16:32],
                                                   cmks[:, 3 * j + 1:3 * j + 2], None, ALU.mult),
                    lambda e, j=j: e.tensor_scalar(dLc[:, 48:64], dLc[:, 32:48],
                                                   cmks[:, 3 * j + 2:3 * j + 3], None, ALU.mult),
                    lambda e: e.tensor_add(coef[:, :], coef[:, :], dLc[:, 48:64]),
                    lambda e, j=j: e.tensor_scalar(dLc[:, 48:64], dLc[:, 0:16],
                                                   cmks[:, 3 * j:3 * j + 1], None, ALU.mult),
                    lambda e: e.tensor_add(coef[:, :], coef[:, :], dLc[:, 48:64]),
                    lambda e, j=j: e.tensor_mul(dLc[:, 48:64], coef[:, :],
                                                sall[:, 32 * j:32 * j + 16]),
                    lambda e: e.tensor_add(Pkv[:, :], Pkv[:, :], dLc[:, 48:64]),
                    lambda e, j=j: e.tensor_mul(dLc[:, 48:64], coef[:, :],
                                                sall[:, 32 * j + 16:32 * j + 32]),
                    lambda e: e.tensor_add(Pks[:, :], Pks[:, :], dLc[:, 48:64]),
                ]:
                    p.step("vector", fs, reads=fr, writes=fw)

        for h in range(H):
            p.step("gpsimd", lambda e, h=h: e.dma_start(
                out=wk[:, :], in_=A["wqkvP"][16 + h, :, :]), writes=[wk], dma=True)
            p.step("gpsimd", lambda e, h=h: e.dma_start(
                out=wv[:, :], in_=A["wqkvP"][32 + h, :, :]), writes=[wv], dma=True)
            p.step("tensor", lambda e: _mm_ct(e, PA, wk, lnb),
                   reads=[wk, lnb], writes=[PA])
            p.step("tensor", lambda e: _mm_ct(e, PB, wv, lnb),
                   reads=[wv, lnb], writes=[PB])
            p.step("vector", lambda e: e.tensor_scalar_min(es[:, :], PA[:, :], 0.0),
                   reads=[PA], writes=[es])
            p.step("scalar", lambda e: e.activation(es[:, :], es[:, :], AF.Exp),
                   reads=[es], writes=[es])
            p.step("vector", lambda e: e.scalar_tensor_tensor(
                kf[:, :], PA[:, :], 1.0, es[:, :], ALU.add, ALU.max),
                reads=[PA, es], writes=[kf])
            p.step("vector", lambda e: e.tensor_mul(kv[:, :], kf[:, :], PB[:, :]),
                   reads=[kf, PB], writes=[kv])
            p.step("gpsimd", lambda e: e.memset(es[:, :], 1.0), writes=[es])
            p.step("vector", lambda e, h=h: e.tensor_scalar(
                es[:, :], es[:, :], dcols[:, h:h + 1], None, ALU.mult),
                reads=[es, dcols], writes=[es])
            if phase2:
                p.step("gpsimd", lambda e, h=h: e.dma_start(
                    out=wq[:, :], in_=A["wqkvP"][h, :, :]), writes=[wq], dma=True)
                p.step("tensor", lambda e: _mm_ct(e, PA, wq, lnb),
                       reads=[wq, lnb], writes=[PA])
                p.step("vector", lambda e, h=h: e.tensor_tensor_scan(
                    sckv[:, :], es[:, :], kv[:, :], Pkv[:, h:h + 1], ALU.mult, ALU.add),
                    reads=[es, kv, Pkv], writes=[sckv], fence=True, hazard=True)
                p.step("vector", lambda e, h=h: e.tensor_tensor_scan(
                    scks[:, :], es[:, :], kf[:, :], Pks[:, h:h + 1], ALU.mult, ALU.add),
                    reads=[es, kf, Pks], writes=[scks], fence=True, hazard=True)
                p.step("vector", lambda e: e.tensor_scalar_min(es[:, :], PA[:, :], 0.0),
                       reads=[PA], writes=[es])
                p.step("scalar", lambda e: e.activation(es[:, :], es[:, :], AF.Exp),
                       reads=[es], writes=[es])
                p.step("vector", lambda e: e.scalar_tensor_tensor(
                    qf[:, :], PA[:, :], 1.0, es[:, :], ALU.add, ALU.max),
                    reads=[PA, es], writes=[qf])
                p.step("vector", lambda e: e.tensor_mul(qks[:, :], qf[:, :], scks[:, :]),
                       reads=[qf, scks], writes=[qks])

                def denmm(e):
                    for n in range(4):
                        i = e.matmul(PB[:, n * 512:(n + 1) * 512], o128[:, :],
                                     qks[:, n * 512:(n + 1) * 512], start=True, stop=True)
                    return i
                p.step("tensor", denmm, reads=[o128, qks], writes=[PB])

                def denrow(e):
                    e.tensor_scalar_max(denr[0:1, :], PB[0:1, :], 1e-6)
                    return e.reciprocal(denr2[0:1, :], denr[0:1, :])
                p.step("vector", denrow, reads=[PB], writes=[denr, denr2], fence=True, hazard=True)

                def denbc(e):
                    for n in range(4):
                        i = e.matmul(PB[:, n * 512:(n + 1) * 512], o1f[0:1, :],
                                     denr2[0:1, n * 512:(n + 1) * 512],
                                     start=True, stop=True)
                    return i
                p.step("tensor", denbc, reads=[o1f, denr2], writes=[PB])
                p.step("vector", lambda e: e.tensor_mul(scks[:, :], qf[:, :], sckv[:, :]),
                       reads=[qf, sckv], writes=[scks])
                p.step("vector", lambda e: e.tensor_mul(ys[:, :], scks[:, :], PB[:, :]),
                       reads=[scks, PB], writes=[ys])
                p.step("sync", lambda e, h=h: e.dma_start(
                    out=_tile(y_ap, h), in_=ys[:, :]), reads=[ys, y_ap], dma=True)
            else:
                p.step("vector", lambda e: e.tensor_tensor_scan(
                    sckv[:, :], es[:, :], kv[:, :], 0.0, ALU.mult, ALU.add),
                    reads=[es, kv], writes=[sckv], fence=True, hazard=True)
                p.step("vector", lambda e, h=h: e.tensor_copy(
                    stc[:, h:h + 1], sckv[:, TL - 1:TL]), reads=[sckv], writes=[stc])
                p.step("vector", lambda e: e.tensor_tensor_scan(
                    scks[:, :], es[:, :], kf[:, :], 0.0, ALU.mult, ALU.add),
                    reads=[es, kf], writes=[scks], fence=True, hazard=True)
                p.step("vector", lambda e, h=h: e.tensor_copy(
                    stc[:, 16 + h:16 + h + 1], scks[:, TL - 1:TL]),
                    reads=[scks], writes=[stc])

        if not phase2:
            p.step("sync", lambda e: e.dma_start(out=states_ap, in_=stc[:, :]),
                   reads=[stc], dma=True)
            p.emit()
        else:
            p.emit()
            ybs = lnb
            p = Prog(nc, S)
            wo = wq
            xr = kf
            ot = kv
            for g in range(NT):
                p.step("sync", lambda e, g=g: e.dma_start(
                    out=ybs[:, g * TL:(g + 1) * TL], in_=_tile(y_ap, g)),
                    reads=[y_ap], writes=[ybs], dma=True)
            for j in range(NT):
                PX = PA if j % 2 == 0 else PB
                p.step("gpsimd", lambda e, j=j: e.dma_start(
                    out=wo[:, :], in_=A["woutP"][j, :, :]), writes=[wo], dma=True)
                p.step("sync", lambda e, j=j: e.dma_start(
                    out=xr[:, :], in_=_tile(x_ap, j)), writes=[xr], dma=True)
                p.step("tensor", lambda e, PX=PX: _mm_ct(e, PX, wo, ybs),
                       reads=[wo, ybs], writes=[PX])
                p.step("vector", lambda e, PX=PX: e.tensor_add(
                    ot[:, :], PX[:, :], xr[:, :]), reads=[PX, xr], writes=[ot])
                p.step("sync", lambda e, j=j: e.dma_start(
                    out=_tile(x1_ap, j), in_=ot[:, :]), reads=[ot], dma=True)
            p.emit()


def _mm_den(e, psum, o128, qks):
    for ct in range(1):
        pass
    for n in range(4):
        i = e.matmul(psum[:, n * 512:(n + 1) * 512], o128[:, :],
                     qks[:, n * 512:(n + 1) * 512], start=True, stop=True)
    return i


def emit_l0f(nc, S, A, x_ap, x2_ap, lnsu_ap, x1h_ap, w2_ap, p6_ap, x1c_ap=None):
    emit_ln(nc, S, A, x_ap, out_f32=lnsu_ap, su=True)

    with ExitStack() as ctx:
        sb = lambda shape, dt=F32: ctx.enter_context(nc.sbuf_tensor(_nm(), shape, dt))
        h128 = sb([128, 128], BF16)
        wdb = [sb([128, INTER], BF16) for _ in range(2)]
        w2s = [sb([128, INTER], BF16) for _ in range(2)]
        PW = ctx.enter_context(nc.psum_tensor(_nm(), [128, 2048], F32))
        p = Prog(nc, S)
        p.step("sync", lambda e: e.dma_start(out=h128[:, :], in_=A["H128b"]),
               writes=[h128], dma=True)
        for j in range(NT):
            wdj, w2j = wdb[j % 2], w2s[j % 2]
            p.step("gpsimd", lambda e, wdj=wdj, j=j: e.dma_start(
                out=wdj[:, :], in_=A["wdownP"][j, :, :]), writes=[wdj], dma=True)
            for q in range(4):
                def fold(e, wdj=wdj, q=q):
                    for s in range(16):
                        i = e.matmul(PW[:, s * 128:(s + 1) * 128], h128[:, :],
                                     wdj[:, (q * 16 + s) * 128:(q * 16 + s + 1) * 128],
                                     start=True, stop=True)
                    return i
                p.step("tensor", fold, reads=[h128, wdj], writes=[PW])
                p.step("scalar", lambda e, w2j=w2j, q=q: e.copy(
                    w2j[:, q * 2048:(q + 1) * 2048], PW[:, :]), reads=[PW], writes=[w2j])
            p.step("sync", lambda e, w2j=w2j, j=j: e.dma_start(
                out=w2_ap[j, :, :], in_=w2j[:, :]), reads=[w2j, w2_ap], dma=True)
        p.emit()

    bsteps16, _ = _butterfly_plan(16, [8, 4, 2, 1])
    with ExitStack() as ctx:
        sb = lambda shape, dt=F32: ctx.enter_context(nc.sbuf_tensor(_nm(), shape, dt))
        TH = TL // 2
        ftile = [sb([128, TH]) for _ in range(17)]
        x1c = sb([128, NT * TH], BF16)
        x1o = sb([128, NT * TH], BF16)
        h128 = sb([128, 128], BF16)
        PF = ctx.enter_context(nc.psum_tensor(_nm(), [128, TH], F32))
        PG = ctx.enter_context(nc.psum_tensor(_nm(), [128, TH], F32))
        for half in range(2):
            t0 = half * TH
            p = Prog(nc, S)
            if half == 0:
                p.step("sync", lambda e: e.dma_start(out=h128[:, :], in_=A["H128b"]),
                       writes=[h128], dma=True)
            for g in range(NT):
                p.step("sync", lambda e, g=g, t0=t0: e.dma_start(
                    out=ftile[g][:, :], in_=_tile(lnsu_ap, g, t0, t0 + TH)),
                    writes=[ftile[g]], dma=True)
            for si, stage in enumerate(bsteps16):
                last = si == len(bsteps16) - 1
                for pi, (pa, pb, sp) in enumerate(stage):
                    if not last:
                        def bf_(e, pa=pa, pb=pb, sp=sp):
                            e.tensor_sub(ftile[sp][:, :], ftile[pa][:, :], ftile[pb][:, :])
                            return e.tensor_add(ftile[pa][:, :], ftile[pa][:, :], ftile[pb][:, :])
                        p.step("vector", bf_, reads=[ftile[pa], ftile[pb]],
                               writes=[ftile[sp], ftile[pa]])
                    else:
                        la = 2 * pi
                        def bf2(e, pa=pa, pb=pb, la=la):
                            e.tensor_sub(x1c[:, (la + 1) * TH:(la + 2) * TH],
                                         ftile[pa][:, :], ftile[pb][:, :])
                            return e.tensor_add(x1c[:, la * TH:(la + 1) * TH],
                                                ftile[pa][:, :], ftile[pb][:, :])
                        p.step("vector", bf2, reads=[ftile[pa], ftile[pb]], writes=[x1c])
            if x1c_ap is not None:
                for a in range(NT):
                    p.step("sync", lambda e, a=a, t0=t0: e.dma_start(
                        out=_tile(x1c_ap, a, t0, t0 + TH), in_=x1c[:, a * TH:(a + 1) * TH]),
                        reads=[x1c], dma=True)
            for a in range(NT):
                PX = PF if a % 2 == 0 else PG
                def h128mm(e, a=a, PX=PX):
                    for n in range(TH // 512):
                        i = e.matmul(PX[:, n * 512:(n + 1) * 512], h128[:, :],
                                     x1c[:, a * TH + n * 512:a * TH + (n + 1) * 512],
                                     start=True, stop=True)
                    return i
                p.step("tensor", h128mm, reads=[h128, x1c], writes=[PX])
                p.step("scalar", lambda e, a=a, PX=PX: e.copy(
                    x1o[:, a * TH:(a + 1) * TH], PX[:, :]), reads=[PX], writes=[x1o])
                p.step("sync", lambda e, a=a, t0=t0: e.dma_start(
                    out=_tile(x1h_ap, a, t0, t0 + TH), in_=x1o[:, a * TH:(a + 1) * TH]),
                    reads=[x1o, x1h_ap], dma=True)
            p.emit()

    bsteps64, _ = _butterfly_plan(64, [32, 16, 8, 4, 2, 1])
    TC = 512
    with ExitStack() as ctx:
        sb = lambda shape, dt=F32: ctx.enter_context(nc.sbuf_tensor(_nm(), shape, dt))
        sup = sb([128, 64])
        sgt = sb([128, 64])
        sdn = sb([128, 16])
        sud = sb([128, 64])
        p = Prog(nc, S)
        p.step("sync", lambda e: e.dma_start(out=sup[:, :], in_=A["wsu_up_cols"]),
               writes=[sup], dma=True)
        p.step("sync", lambda e: e.dma_start(out=sgt[:, :], in_=A["wsu_gt_cols"]),
               writes=[sgt], dma=True)
        p.step("sync", lambda e: e.dma_start(out=sdn[:, :], in_=A["wsd_cols"]),
               writes=[sdn], dma=True)
        p.step("sync", lambda e: e.dma_start(out=sud[:, :], in_=A["su_dp_cols"]),
               writes=[sud], dma=True)
        p.step("vector", lambda e: e.tensor_scalar_mul(sup[:, :], sup[:, :], R2048),
               reads=[sup], writes=[sup])
        p.step("vector", lambda e: e.tensor_scalar_mul(sgt[:, :], sgt[:, :], R2048),
               reads=[sgt], writes=[sgt])
        p.step("vector", lambda e: e.tensor_scalar_mul(sdn[:, :], sdn[:, :], R8192),
               reads=[sdn], writes=[sdn])
        p.emit()

        for tc in range(TL // TC):
            t0 = tc * TC
            with ExitStack() as c1:
                sb1 = lambda shape, dt=F32: c1.enter_context(nc.sbuf_tensor(_nm(), shape, dt))
                x1s = sb1([128, NT * TC], BF16)
                wub = [sb1([128, C], BF16) for _ in range(2)]
                wgb = [sb1([128, C], BF16) for _ in range(2)]
                us = [sb1([128, TC]) for _ in range(2)]
                gs = [sb1([128, TC]) for _ in range(2)]
                pF = [sb1([128, TC]) for _ in range(65)]
                bnc = sb1([128, 2 * TC], BF16)
                PU = [c1.enter_context(nc.psum_tensor(_nm(), [128, TC], F32))
                      for _ in range(2)]
                PG2 = [c1.enter_context(nc.psum_tensor(_nm(), [128, TC], F32))
                       for _ in range(2)]
                p = Prog(nc, S)
                p.step("sync", lambda e, t0=t0: e.dma_start(
                    out=x1s[:, :],
                    in_=x1h_ap.rearrange("(g p) t -> p g t", p=128)[:, :, t0:t0 + TC]),
                    reads=[x1h_ap], writes=[x1s], dma=True)
                for j in range(64):
                    wuj, wgj = wub[j % 2], wgb[j % 2]
                    PUj, PGj = PU[j % 2], PG2[j % 2]
                    usj, gsj = us[j % 2], gs[j % 2]
                    p.step("gpsimd", lambda e, wuj=wuj, j=j: e.dma_start(
                        out=wuj[:, :], in_=A["wupP"][j, :, :]), writes=[wuj], dma=True)
                    p.step("gpsimd", lambda e, wgj=wgj, j=j: e.dma_start(
                        out=wgj[:, :], in_=A["wgateP"][j, :, :]), writes=[wgj], dma=True)
                    p.step("tensor", lambda e, wuj=wuj, PUj=PUj: _mm_ct(e, PUj, wuj, x1s, tl=TC),
                           reads=[wuj, x1s], writes=[PUj])
                    p.step("tensor", lambda e, wgj=wgj, PGj=PGj: _mm_ct(e, PGj, wgj, x1s, tl=TC),
                           reads=[wgj, x1s], writes=[PGj])
                    p.step("scalar", lambda e, j=j, PUj=PUj, usj=usj: e.activation(
                        usj[:, :], PUj[:, :], AF.Identity, scale=sup[:, j:j + 1]),
                        reads=[PUj, sup], writes=[usj])
                    p.step("scalar", lambda e, j=j, PGj=PGj, gsj=gsj: e.activation(
                        gsj[:, :], PGj[:, :], AF.Silu, scale=sgt[:, j:j + 1]),
                        reads=[PGj, sgt], writes=[gsj])
                    p.step("vector", lambda e, j=j, gsj=gsj, usj=usj: e.scalar_tensor_tensor(
                        pF[j][:, :], gsj[:, :], sud[:, j:j + 1], usj[:, :],
                        ALU.mult, ALU.mult),
                        reads=[gsj, sud, usj], writes=[pF[j]])
                for si, stage in enumerate(bsteps64):
                    last = si == len(bsteps64) - 1
                    for pi, (pa, pb, sp) in enumerate(stage):
                        if not last:
                            def bf_(e, pa=pa, pb=pb, sp=sp):
                                e.tensor_sub(pF[sp][:, :], pF[pa][:, :], pF[pb][:, :])
                                return e.tensor_add(pF[pa][:, :], pF[pa][:, :], pF[pb][:, :])
                            p.step("vector", bf_, reads=[pF[pa], pF[pb]],
                                   writes=[pF[sp], pF[pa]])
                        else:
                            la = 2 * pi
                            def bf2(e, pa=pa, pb=pb):
                                e.tensor_sub(bnc[:, TC:2 * TC], pF[pa][:, :], pF[pb][:, :])
                                return e.tensor_add(bnc[:, 0:TC], pF[pa][:, :], pF[pb][:, :])
                            p.step("vector", bf2, reads=[pF[pa], pF[pb]], writes=[bnc])
                            p.step("sync", lambda e, la=la: e.dma_start(
                                out=p6_ap[:, la * TC:(la + 2) * TC], in_=bnc[:, :]),
                                reads=[bnc, p6_ap], dma=True)
                p.emit()

            with ExitStack() as c2:
                sb2 = lambda shape, dt=F32: c2.enter_context(nc.sbuf_tensor(_nm(), shape, dt))
                p6s = sb2([128, 64 * TC], BF16)
                w2b = [sb2([128, INTER], BF16) for _ in range(2)]
                us2 = sb2([128, TC])
                xr = [sb2([128, TC]) for _ in range(2)]
                ot = [sb2([128, TC]) for _ in range(2)]
                PO = c2.enter_context(nc.psum_tensor(_nm(), [128, TC], F32))
                PO2 = c2.enter_context(nc.psum_tensor(_nm(), [128, TC], F32))
                p = Prog(nc, S)
                p.step("sync", lambda e: e.dma_start(out=p6s[:, :], in_=p6_ap),
                       reads=[p6_ap], writes=[p6s], dma=True)
                for j in range(NT):
                    w2j = w2b[j % 2]
                    xrj, otj = xr[j % 2], ot[j % 2]
                    PX = PO if j % 2 == 0 else PO2
                    p.step("gpsimd", lambda e, w2j=w2j, j=j: e.dma_start(
                        out=w2j[:, :], in_=w2_ap[j, :, :]),
                        reads=[w2_ap], writes=[w2j], dma=True)
                    p.step("sync", lambda e, xrj=xrj, j=j, t0=t0: e.dma_start(
                        out=xrj[:, :], in_=_tile(x_ap, j, t0, t0 + TC)),
                        writes=[xrj], dma=True)

                    def wdmm(e, w2j=w2j, PX=PX):
                        for s in range(64):
                            i = e.matmul(PX[:, :], w2j[:, s * 128:(s + 1) * 128],
                                         p6s[:, s * TC:(s + 1) * TC],
                                         start=(s == 0), stop=(s == 63))
                        return i
                    p.step("tensor", wdmm, reads=[w2j, p6s], writes=[PX])
                    p.step("scalar", lambda e, j=j, PX=PX: e.activation(
                        us2[:, :], PX[:, :], AF.Identity, scale=sdn[:, j:j + 1]),
                        reads=[PX, sdn], writes=[us2])
                    p.step("vector", lambda e, xrj=xrj, otj=otj: e.tensor_add(
                        otj[:, :], us2[:, :], xrj[:, :]), reads=[us2, xrj], writes=[otj])
                    p.step("sync", lambda e, otj=otj, j=j, t0=t0: e.dma_start(
                        out=_tile(x2_ap, j, t0, t0 + TC), in_=otj[:, :]),
                        reads=[otj], dma=True)
                p.emit()


def emit_l1a1(nc, S, A, x_ap, hs_ap, l_ap, ast_ap, mz_ap, pcA):
    emit_ln(nc, S, A, x_ap, out_f32=hs_ap)
    with ExitStack() as ctx:
        sb = lambda shape, dt=F32: ctx.enter_context(nc.sbuf_tensor(_nm(), shape, dt))
        hsb = sb([128, NT * TL], BF16)
        wz = sb([128, C], BF16)
        wh = sb([128, C], BF16)
        k1 = sb([128, TL])
        z2 = sb([128, TL])
        sp_ = sb([128, TL])
        scr = sb([128, TL])
        lcb = sb([128, TL], BF16)
        lzt = sb([128, TL])
        u_s = sb([128, TL])
        ast = sb([128, TL])
        R_s = sb([128, TL])
        d_s = sb([128, TL])
        b_s = sb([128, TL])
        Z_s = sb([128, TL])
        l_s = sb([128, TL])
        msk = sb([128, TL], U8)
        bzc = sb([128, 16])
        bhc = sb([128, 16])
        h0c = sb([128, 16])
        u0c = sb([128, 16])
        tnc = sb([128, 16])
        fcol = sb([128, 1])
        gcol = sb([128, 1])
        rin = sb([128, 16])
        cst = sb([128, 2])
        carry = sb([1, TL])
        crow = sb([1, TL])
        mzc = sb([128, 32])
        trib = sb([128, 128], BF16)
        o1f = sb([1, 128])
        o128 = sb([128, 128], BF16)
        PA = ctx.enter_context(nc.psum_tensor(_nm(), [128, TL], F32))
        PB = ctx.enter_context(nc.psum_tensor(_nm(), [128, TL], F32))

        p = Prog(nc, S)
        p.step("gpsimd", lambda e: e.dma_start(
            out=hsb[:, :], in_=hs_ap.rearrange("(g p) t -> p g t", p=128)),
            writes=[hsb], dma=True)
        for nm, dst in (("bz_cols", bzc), ("bh_cols", bhc), ("h0_cols", h0c),
                        ("TRIb", trib), ("ones1f", o1f), ("ones128b", o128),
                        ("f_col", fcol), ("gneg_col", gcol)):
            p.step("sync", lambda e, nm=nm, dst=dst: e.dma_start(out=dst[:, :], in_=A[nm]),
                   writes=[dst], dma=True)
        p.step("gpsimd", lambda e: e.memset(carry[0:1, :], 0.0), writes=[carry])
        p.step("gpsimd", lambda e: e.memset(cst[:, 0:1], 0.5), writes=[cst])
        p.step("gpsimd", lambda e: e.memset(cst[:, 1:2], 1.0), writes=[cst])

        for fs in [
            lambda e: e.activation(u0c[:, :], h0c[:, :], AF.Relu),
            lambda e: e.activation(u0c[:, :], u0c[:, :], AF.Ln, bias=cst[:, 0:1]),
            lambda e: e.activation(tnc[:, :], h0c[:, :], AF.Abs),
            lambda e: e.activation(tnc[:, :], tnc[:, :], AF.Exp, scale=-1.0),
            lambda e: e.activation(tnc[:, :], tnc[:, :], AF.Ln, bias=cst[:, 1:2]),
        ]:
            p.step("scalar", fs, reads=[h0c, cst, u0c, tnc], writes=[u0c, tnc])

        for fs in [
            lambda e: e.scalar_tensor_tensor(tnc[:, :], h0c[:, :], 0.0, tnc[:, :],
                                             ALU.max, ALU.add),
            lambda e: e.tensor_sub(tnc[:, :], h0c[:, :], tnc[:, :]),
            lambda e: e.tensor_single_scalar(rin[:, :], h0c[:, :], 0.0, ALU.is_ge),
        ]:
            p.step("vector", fs, reads=[h0c, tnc], writes=[tnc, rin])
        p.step("vector", lambda e: e.tensor_copy(msk[:, 0:16], rin[:, :]),
               reads=[rin], writes=[msk])
        p.step("vector", lambda e: e.select(u0c[:, :], msk[:, 0:16], u0c[:, :], tnc[:, :]),
               reads=[msk, u0c, tnc], writes=[u0c])
        p.step("vector", lambda e: e.tensor_scalar(rin[:, :], u0c[:, :], fcol[:, 0:1],
                                                   None, ALU.mult),
               reads=[u0c, fcol], writes=[rin])
        p.step("vector", lambda e: e.tensor_scalar(rin[:, :], rin[:, :], gcol[:, 0:1],
                                                   None, ALU.add),
               reads=[rin, gcol], writes=[rin])

        for j in range(NT):
            p.step("gpsimd", lambda e, j=j: e.dma_start(
                out=wz[:, :], in_=A["wzP"][j, :, :]), writes=[wz], dma=True)
            p.step("gpsimd", lambda e, j=j: e.dma_start(
                out=wh[:, :], in_=A["whP"][j, :, :]), writes=[wh], dma=True)
            p.step("tensor", lambda e: _mm_ct(e, PA, wz, hsb),
                   reads=[wz, hsb], writes=[PA])
            p.step("tensor", lambda e: _mm_ct(e, PB, wh, hsb),
                   reads=[wh, hsb], writes=[PB])
            p.step("scalar", lambda e, j=j: e.activation(
                k1[:, :], PA[:, :], AF.Identity, bias=bzc[:, j:j + 1]),
                reads=[PA, bzc], writes=[k1])
            p.step("scalar", lambda e, j=j: e.activation(
                z2[:, :], PB[:, :], AF.Identity, bias=bhc[:, j:j + 1]),
                reads=[PB, bhc], writes=[z2])

            def spa(e):
                e.activation(scr[:, :], k1[:, :], AF.Abs)
                e.activation(scr[:, :], scr[:, :], AF.Exp, scale=-1.0)
                return e.activation(scr[:, :], scr[:, :], AF.Ln, bias=cst[:, 1:2])
            p.step("scalar", spa, reads=[k1, cst], writes=[scr])
            p.step("vector", lambda e: e.scalar_tensor_tensor(
                sp_[:, :], k1[:, :], 0.0, scr[:, :], ALU.max, ALU.add),
                reads=[k1, scr], writes=[sp_])
            p.step("vector", lambda e: e.tensor_scalar_mul(lcb[:, :], sp_[:, :], -1.0),
                   reads=[sp_], writes=[lcb])
            def gla(e):
                e.activation(lzt[:, :], z2[:, :], AF.Relu)
                e.activation(lzt[:, :], lzt[:, :], AF.Ln, bias=cst[:, 0:1])
                e.activation(scr[:, :], z2[:, :], AF.Abs)
                e.activation(scr[:, :], scr[:, :], AF.Exp, scale=-1.0)
                return e.activation(scr[:, :], scr[:, :], AF.Ln, bias=cst[:, 1:2])
            p.step("scalar", gla, reads=[z2, cst], writes=[lzt, scr])

            def glb(e):
                e.scalar_tensor_tensor(scr[:, :], z2[:, :], 0.0, scr[:, :], ALU.max, ALU.add)
                e.tensor_sub(scr[:, :], z2[:, :], scr[:, :])
                e.tensor_single_scalar(msk[:, :], z2[:, :], 0.0, ALU.is_ge)
                return e.select(lzt[:, :], msk[:, :], lzt[:, :], scr[:, :])
            p.step("vector", glb, reads=[z2, scr, lzt], writes=[scr, msk, lzt])

            def lz2(e):
                e.tensor_sub(u_s[:, :], k1[:, :], sp_[:, :])
                return e.tensor_add(lzt[:, :], lzt[:, :], u_s[:, :])
            p.step("vector", lz2, reads=[k1, sp_, lzt], writes=[lzt, u_s])

            def amm(e):
                for n in range(4):
                    e.matmul(PA[:, n * 512:(n + 1) * 512], trib[:, :],
                             lcb[:, n * 512:(n + 1) * 512], start=True, stop=True)
                for n in range(4):
                    i = e.matmul(PB[:, n * 512:(n + 1) * 512], o1f[0:1, :],
                                 carry[0:1, n * 512:(n + 1) * 512], start=True, stop=True)
                return i
            p.step("tensor", amm, reads=[trib, lcb, o1f, carry], writes=[PA, PB])

            p.step("scalar", lambda e: e.copy(ast[:, :], PA[:, :]),
                   reads=[PA], writes=[ast])
            p.step("vector", lambda e: e.tensor_add(ast[:, :], ast[:, :], PB[:, :]),
                   reads=[ast, PB], writes=[ast])
            p.step("vector", lambda e: e.tensor_sub(u_s[:, :], lzt[:, :], ast[:, :]),
                   reads=[lzt, ast], writes=[u_s])
            p.step("sync", lambda e, j=j: e.dma_start(out=_tile(ast_ap, j), in_=ast[:, :]),
                   reads=[ast], dma=True)

            def cmm(e):
                for ct in range(1):
                    pass
                for n in range(4):
                    i = e.matmul(PA[:, n * 512:(n + 1) * 512], o128[:, :],
                                 lcb[:, n * 512:(n + 1) * 512], start=True, stop=True)
                return i
            p.step("tensor", cmm, reads=[o128, lcb], writes=[PA])
            p.step("vector", lambda e: e.tensor_add(carry[0:1, :], carry[0:1, :], PA[0:1, :]),
                   reads=[PA, carry], writes=[carry])

            p.step("gpsimd", lambda e: e.memset(scr[:, :], 0.0), writes=[scr])
            p.step("vector", lambda e, j=j: e.tensor_tensor_scan(
                R_s[:, :], u_s[:, :], scr[:, :], rin[:, j:j + 1], ALU.max, ALU.add),
                reads=[u_s, scr, rin], writes=[R_s], fence=True, hazard=True)

            def dsh(e, j=j):
                e.tensor_sub(d_s[:, 1:TL], R_s[:, 0:TL - 1], R_s[:, 1:TL])
                e.tensor_scalar(d_s[:, 0:1], R_s[:, 0:1], -1.0, rin[:, j:j + 1],
                                ALU.mult, ALU.add)
                return e.tensor_sub(b_s[:, :], u_s[:, :], R_s[:, :])
            p.step("vector", dsh, reads=[R_s, u_s, rin], writes=[d_s, b_s])

            def exps(e):
                e.activation(d_s[:, :], d_s[:, :], AF.Exp)
                return e.activation(b_s[:, :], b_s[:, :], AF.Exp)
            p.step("scalar", exps, reads=[d_s, b_s], writes=[d_s, b_s])
            p.step("vector", lambda e, j=j: e.tensor_tensor_scan(
                Z_s[:, :], d_s[:, :], b_s[:, :], fcol[:, 0:1], ALU.mult, ALU.add),
                reads=[d_s, b_s, fcol], writes=[Z_s], fence=True, hazard=True)
            p.step("scalar", lambda e: e.activation(l_s[:, :], Z_s[:, :], AF.Ln),
                   reads=[Z_s], writes=[l_s])
            p.step("vector", lambda e: e.tensor_add(l_s[:, :], l_s[:, :], R_s[:, :]),
                   reads=[R_s, l_s], writes=[l_s])
            p.step("sync", lambda e, j=j: e.dma_start(out=_tile(l_ap, j), in_=l_s[:, :]),
                   reads=[l_s], dma=True)
            p.step("vector", lambda e, j=j: e.tensor_copy(mzc[:, j:j + 1], R_s[:, TL - 1:TL]),
                   reads=[R_s], writes=[mzc])
            p.step("vector", lambda e, j=j: e.tensor_copy(mzc[:, 16 + j:17 + j], Z_s[:, TL - 1:TL]),
                   reads=[Z_s], writes=[mzc])
        p.step("sync", lambda e: e.dma_start(out=mz_ap, in_=mzc[:, :]),
               reads=[mzc], dma=True)
        p.emit()


def emit_l1a2(nc, S, A, l_ap, ast_ap, mzall_ap, logh_ap, stats_ap):
    with ExitStack() as ctx:
        sb = lambda shape, dt=F32: ctx.enter_context(nc.sbuf_tensor(_nm(), shape, dt))
        sall = sb([128, 8 * 32])
        l1m = sb([128, 8])
        l1mn = sb([128, 8])
        fcol = sb([128, 1])
        PM = sb([128, 16])
        PZ = sb([128, 16])
        Me = sb([128, 16])
        Ze = sb([128, 16])
        Mn = sb([128, 16])
        t1 = sb([128, 16])
        t2 = sb([128, 16])
        Pcol = sb([128, 16])
        h0c = sb([128, 16])
        u0c = sb([128, 16])
        tnc = sb([128, 16])
        msk16 = sb([128, 16], U8)
        cst = sb([128, 2])
        l_s = sb([128, TL])
        ast = sb([128, TL])
        dl = sb([128, TL])
        mx = sb([128, TL])
        scr = sb([128, TL])
        acc = sb([128, 34])
        o1c = sb([128, 1])
        rowst = sb([1, 40])
        PS = ctx.enter_context(nc.psum_tensor(_nm(), [1, 512], F32))

        p = Prog(nc, S)
        p.step("sync", lambda e: e.dma_start(
            out=sall[:, :], in_=mzall_ap.rearrange("(j p) s -> p j s", p=128)),
            writes=[sall], dma=True)
        p.step("sync", lambda e: e.dma_start(out=l1m[:, :], in_=A["l1m"]),
               writes=[l1m], dma=True)
        p.step("sync", lambda e: e.dma_start(out=l1mn[:, :], in_=A["l1mneg"]),
               writes=[l1mn], dma=True)
        p.step("sync", lambda e: e.dma_start(out=fcol[:, :], in_=A["f_col"]),
               writes=[fcol], dma=True)
        p.step("sync", lambda e: e.dma_start(out=h0c[:, :], in_=A["h0_cols"]),
               writes=[h0c], dma=True)
        p.step("gpsimd", lambda e: e.memset(PM[:, :], NEG), writes=[PM])
        p.step("gpsimd", lambda e: e.memset(PZ[:, :], 0.0), writes=[PZ])
        p.step("gpsimd", lambda e: e.memset(o1c[:, :], 1.0), writes=[o1c])
        p.step("gpsimd", lambda e: e.memset(cst[:, 0:1], 0.5), writes=[cst])
        p.step("gpsimd", lambda e: e.memset(cst[:, 1:2], 1.0), writes=[cst])
        for j in range(W):
            fr = [sall, l1m, l1mn, PM, Me, Ze, Mn]
            fw = [Me, Ze, Mn, t1, t2]
            for fs in [
                lambda e, j=j: e.tensor_scalar(Me[:, :], sall[:, 32 * j:32 * j + 16],
                                               l1m[:, j:j + 1], l1mn[:, j:j + 1],
                                               ALU.mult, ALU.add),
                lambda e, j=j: e.tensor_scalar(Ze[:, :], sall[:, 32 * j + 16:32 * j + 32],
                                               l1m[:, j:j + 1], None, ALU.mult),
                lambda e: e.tensor_max(Mn[:, :], PM[:, :], Me[:, :]),
                lambda e: e.tensor_sub(t1[:, :], PM[:, :], Mn[:, :]),
                lambda e: e.tensor_sub(t2[:, :], Me[:, :], Mn[:, :]),
            ]:
                p.step("vector", fs, reads=fr, writes=fw)

            def f2(e):
                e.activation(t1[:, :], t1[:, :], AF.Exp)
                return e.activation(t2[:, :], t2[:, :], AF.Exp)
            p.step("scalar", f2, reads=[t1, t2], writes=[t1, t2])

            for fs in [
                lambda e: e.tensor_mul(PZ[:, :], PZ[:, :], t1[:, :]),
                lambda e: e.tensor_mul(t2[:, :], Ze[:, :], t2[:, :]),
                lambda e: e.tensor_add(PZ[:, :], PZ[:, :], t2[:, :]),
                lambda e: e.tensor_copy(PM[:, :], Mn[:, :]),
            ]:
                p.step("vector", fs, reads=[PZ, t1, t2, Ze, Mn], writes=[PZ, PM, t2])
        p.step("vector", lambda e: e.tensor_scalar_max(PZ[:, :], PZ[:, :], 1e-38),
               reads=[PZ], writes=[PZ])
        p.step("scalar", lambda e: e.activation(Pcol[:, :], PZ[:, :], AF.Ln),
               reads=[PZ], writes=[Pcol])
        p.step("vector", lambda e: e.tensor_add(Pcol[:, :], Pcol[:, :], PM[:, :]),
               reads=[Pcol, PM], writes=[Pcol])

        for fs in [
            lambda e: e.activation(u0c[:, :], h0c[:, :], AF.Relu),
            lambda e: e.activation(u0c[:, :], u0c[:, :], AF.Ln, bias=cst[:, 0:1]),
            lambda e: e.activation(tnc[:, :], h0c[:, :], AF.Abs),
            lambda e: e.activation(tnc[:, :], tnc[:, :], AF.Exp, scale=-1.0),
            lambda e: e.activation(tnc[:, :], tnc[:, :], AF.Ln, bias=cst[:, 1:2]),
        ]:
            p.step("scalar", fs, reads=[h0c, cst, u0c, tnc], writes=[u0c, tnc])

        for fs in [
            lambda e: e.scalar_tensor_tensor(tnc[:, :], h0c[:, :], 0.0, tnc[:, :],
                                             ALU.max, ALU.add),
            lambda e: e.tensor_sub(tnc[:, :], h0c[:, :], tnc[:, :]),
            lambda e: e.tensor_single_scalar(msk16[:, :], h0c[:, :], 0.0, ALU.is_ge),
            lambda e: e.select(u0c[:, :], msk16[:, :], u0c[:, :], tnc[:, :]),
            lambda e: e.tensor_scalar(u0c[:, :], u0c[:, :], fcol[:, 0:1], None, ALU.mult),
        ]:
            p.step("vector", fs, reads=[h0c, tnc, msk16, u0c, fcol],
                   writes=[tnc, msk16, u0c])
        p.step("scalar", lambda e: e.activation(scr[:, 0:16], u0c[:, :], AF.Copy,
                                                accum_out=acc[:, 32:33]),
               reads=[u0c], writes=[scr, acc])
        p.step("scalar", lambda e: e.activation(scr[:, 0:16], u0c[:, :], AF.Square,
                                                accum_out=acc[:, 33:34]),
               reads=[u0c], writes=[scr, acc])

        for j in range(NT):
            p.step("sync", lambda e, j=j: e.dma_start(out=l_s[:, :], in_=_tile(l_ap, j)),
                   writes=[l_s], dma=True)
            p.step("sync", lambda e, j=j: e.dma_start(out=ast[:, :], in_=_tile(ast_ap, j)),
                   writes=[ast], dma=True)

            def lae(e, j=j):
                e.tensor_scalar(dl[:, :], l_s[:, :], Pcol[:, j:j + 1], None, ALU.subtract)
                return e.tensor_scalar(mx[:, :], l_s[:, :], Pcol[:, j:j + 1], None, ALU.max)
            p.step("vector", lae, reads=[l_s, Pcol], writes=[dl, mx])

            def lex(e):
                e.activation(dl[:, :], dl[:, :], AF.Abs)
                e.activation(dl[:, :], dl[:, :], AF.Exp, scale=-1.0)
                return e.activation(dl[:, :], dl[:, :], AF.Ln, bias=cst[:, 1:2])
            p.step("scalar", lex, reads=[dl, cst], writes=[dl])

            def lsum(e):
                e.tensor_add(mx[:, :], mx[:, :], dl[:, :])
                return e.tensor_add(mx[:, :], mx[:, :], ast[:, :])
            p.step("vector", lsum, reads=[mx, dl, ast], writes=[mx])
            p.step("sync", lambda e, j=j: e.dma_start(out=_tile(logh_ap, j), in_=mx[:, :]),
                   reads=[mx], dma=True)
            p.step("scalar", lambda e, j=j: e.activation(
                scr[:, :], mx[:, :], AF.Copy, accum_out=acc[:, j:j + 1]),
                reads=[mx], writes=[scr, acc])
            p.step("scalar", lambda e, j=j: e.activation(
                scr[:, :], mx[:, :], AF.Square, accum_out=acc[:, 16 + j:17 + j]),
                reads=[mx], writes=[scr, acc])

        def statmm(e):
            return e.matmul(PS[0:1, 0:34], o1c[:, 0:1], acc[:, :], start=True, stop=True)
        p.step("tensor", statmm, reads=[o1c, acc], writes=[PS])

        for fs in [
            lambda e: e.reduce_sum(rowst[0:1, 0:1], PS[0:1, 0:16], axis=mybir.AxisListType.X),
            lambda e: e.reduce_sum(rowst[0:1, 1:2], PS[0:1, 16:32], axis=mybir.AxisListType.X),
            lambda e: e.tensor_add(rowst[0:1, 0:1], rowst[0:1, 0:1], PS[0:1, 32:33]),
            lambda e: e.tensor_add(rowst[0:1, 1:2], rowst[0:1, 1:2], PS[0:1, 33:34]),
        ]:
            p.step("vector", fs, reads=[PS, rowst], writes=[rowst])
        p.step("sync", lambda e: e.dma_start(out=stats_ap, in_=rowst[0:1, 0:2]),
               reads=[rowst], dma=True)
        p.emit()


def emit_l1a3(nc, S, A, x_ap, hs_ap, logh_ap, statsall_ap, x3_ap):
    NTOT = 2.0 * 8193.0 * 2048.0
    with ExitStack() as ctx:
        sb = lambda shape, dt=F32: ctx.enter_context(nc.sbuf_tensor(_nm(), shape, dt))
        stall = sb([8, 2])
        rowst = sb([1, 16])
        o8 = sb([8, 1])
        o1f = sb([1, 128])
        sccol = sb([128, 2])
        x2a = sb([128, TL])
        lg = [sb([128, TL]) for _ in range(2)]
        hsx = [sb([128, TL]) for _ in range(2)]
        x2b = [sb([128, TL], BF16) for _ in range(2)]
        sqb = [sb([128, TL], BF16) for _ in range(2)]
        rows = sb([1, 6 * C])
        t1f = sb([128, TL])
        t2f = sb([128, TL])
        ret = sb([128, TL])
        xr = sb([128, TL])
        lnw = sb([128, 16])
        lnb_ = sb([128, 16])
        o128 = sb([128, 128], BF16)
        ps = ctx.enter_context(nc.psum_tensor(_nm(), [128, TL], F32))
        pss = ctx.enter_context(nc.psum_tensor(_nm(), [128, TL], F32))

        p = Prog(nc, S)
        p.step("sync", lambda e: e.dma_start(out=stall[:, :], in_=statsall_ap),
               writes=[stall], dma=True)
        p.step("sync", lambda e: e.dma_start(out=o1f[0:1, :], in_=A["ones1f"]),
               writes=[o1f], dma=True)
        p.step("sync", lambda e: e.dma_start(out=o128[:, :], in_=A["ones128b"]),
               writes=[o128], dma=True)
        p.step("sync", lambda e: e.dma_start(out=lnw[:, :], in_=A["lnw_cols"]),
               writes=[lnw], dma=True)
        p.step("sync", lambda e: e.dma_start(out=lnb_[:, :], in_=A["lnb_cols"]),
               writes=[lnb_], dma=True)
        p.step("gpsimd", lambda e: e.memset(o8[:, :], 1.0), writes=[o8])
        p.step("tensor", lambda e: e.matmul(ps[0:1, 0:2], o8[:, 0:1], stall[:, :],
                                            start=True, stop=True),
               reads=[o8, stall], writes=[ps])

        for fs in [
            lambda e: e.tensor_scalar_mul(rowst[0:1, 0:1], ps[0:1, 0:1], 1.0 / NTOT),
            lambda e: e.tensor_mul(rowst[0:1, 2:3], rowst[0:1, 0:1], rowst[0:1, 0:1]),
            lambda e: e.tensor_scalar_mul(rowst[0:1, 2:3], rowst[0:1, 2:3], NTOT),
            lambda e: e.tensor_sub(rowst[0:1, 3:4], ps[0:1, 1:2], rowst[0:1, 2:3]),
            lambda e: e.tensor_scalar_mul(rowst[0:1, 3:4], rowst[0:1, 3:4],
                                          1.0 / (NTOT - 1.0)),
        ]:
            p.step("vector", fs, reads=[ps, rowst], writes=[rowst])

        p.step("scalar", lambda e: e.activation(rowst[0:1, 3:4], rowst[0:1, 3:4], AF.Ln),
               reads=[rowst], writes=[rowst])
        p.step("scalar", lambda e: e.activation(rowst[0:1, 4:5], rowst[0:1, 3:4],
                                                AF.Exp, scale=-0.5),
               reads=[rowst], writes=[rowst])

        p.step("vector", lambda e: e.tensor_mul(rowst[0:1, 5:6], rowst[0:1, 0:1],
                                                rowst[0:1, 4:5]),
               reads=[rowst], writes=[rowst])
        p.step("vector", lambda e: e.tensor_scalar_mul(rowst[0:1, 5:6], rowst[0:1, 5:6],
                                                       -1.0),
               reads=[rowst], writes=[rowst])
        p.step("tensor", lambda e: e.matmul(ps[:, 2:4], o1f[0:1, :], rowst[0:1, 4:6],
                                            start=True, stop=True),
               reads=[o1f, rowst], writes=[ps])
        p.step("vector", lambda e: e.tensor_copy(sccol[:, :], ps[:, 2:4]),
               reads=[ps], writes=[sccol])

        for g in range(NT):
            lgg, hsg = lg[g % 2], hsx[g % 2]
            x2bg, sqg = x2b[g % 2], sqb[g % 2]
            p.step("sync", lambda e, lgg=lgg, g=g: e.dma_start(
                out=lgg[:, :], in_=_tile(logh_ap, g)), writes=[lgg], dma=True)
            p.step("sync", lambda e, hsg=hsg, g=g: e.dma_start(
                out=hsg[:, :], in_=_tile(hs_ap, g)), writes=[hsg], dma=True)
            p.step("scalar", lambda e, lgg=lgg: e.activation(
                x2a[:, :], lgg[:, :], AF.Exp, scale=sccol[:, 0:1], bias=sccol[:, 1:2]),
                reads=[lgg, sccol], writes=[x2a])
            p.step("vector", lambda e, hsg=hsg: e.tensor_add(
                x2a[:, :], x2a[:, :], hsg[:, :]), reads=[x2a, hsg], writes=[x2a])
            p.step("gpsimd", lambda e, x2bg=x2bg: e.tensor_copy(x2bg[:, :], x2a[:, :]),
                   reads=[x2a], writes=[x2bg])
            p.step("scalar", lambda e, sqg=sqg: e.activation(sqg[:, :], x2a[:, :], AF.Square),
                   reads=[x2a], writes=[sqg])

            def mms(e, g=g, x2bg=x2bg, sqg=sqg):
                for n in range(4):
                    e.matmul(ps[:, n * 512:(n + 1) * 512], o128[:, :],
                             x2bg[:, n * 512:(n + 1) * 512],
                             start=(g == 0), stop=(g == NT - 1))
                for n in range(4):
                    i = e.matmul(pss[:, n * 512:(n + 1) * 512], o128[:, :],
                                 sqg[:, n * 512:(n + 1) * 512],
                                 start=(g == 0), stop=(g == NT - 1))
                return i
            p.step("tensor", mms, reads=[o128, x2bg, sqg], writes=[ps, pss])

        def rows1(e):
            e.tensor_scalar_mul(rows[0:1, 0:C], ps[0:1, :], 1.0 / C)
            e.tensor_scalar_mul(rows[0:1, C:2 * C], pss[0:1, :], 1.0 / C)
            e.tensor_mul(rows[0:1, 4 * C:5 * C], rows[0:1, 0:C], rows[0:1, 0:C])
            e.tensor_sub(rows[0:1, C:2 * C], rows[0:1, C:2 * C], rows[0:1, 4 * C:5 * C])
            return e.tensor_scalar_add(rows[0:1, 5 * C:6 * C], rows[0:1, C:2 * C], EPS_LN)
        p.step("vector", rows1, reads=[ps, pss], writes=[rows])

        def rows2(e):
            e.activation(rows[0:1, 5 * C:6 * C], rows[0:1, 5 * C:6 * C], AF.Ln)
            return e.activation(rows[0:1, 2 * C:3 * C], rows[0:1, 5 * C:6 * C],
                                AF.Exp, scale=-0.5)
        p.step("scalar", rows2, reads=[rows], writes=[rows])
        p.step("vector", lambda e: e.tensor_mul(rows[0:1, 3 * C:4 * C],
                                                rows[0:1, 0:C], rows[0:1, 2 * C:3 * C]),
               reads=[rows], writes=[rows])

        def bcast(e):
            for n in range(4):
                e.matmul(ps[:, n * 512:(n + 1) * 512], o1f[0:1, :],
                         rows[0:1, 2 * C + n * 512:2 * C + (n + 1) * 512],
                         start=True, stop=True)
            for n in range(4):
                i = e.matmul(pss[:, n * 512:(n + 1) * 512], o1f[0:1, :],
                             rows[0:1, 3 * C + n * 512:3 * C + (n + 1) * 512],
                             start=True, stop=True)
            return i
        p.step("tensor", bcast, reads=[o1f, rows], writes=[ps, pss])

        for g in range(NT):
            lgg, hsg = lg[g % 2], hsx[g % 2]
            p.step("sync", lambda e, lgg=lgg, g=g: e.dma_start(
                out=lgg[:, :], in_=_tile(logh_ap, g)), writes=[lgg], dma=True)
            p.step("sync", lambda e, hsg=hsg, g=g: e.dma_start(
                out=hsg[:, :], in_=_tile(hs_ap, g)), writes=[hsg], dma=True)
            p.step("sync", lambda e, g=g: e.dma_start(
                out=xr[:, :], in_=_tile(x_ap, g)), writes=[xr], dma=True)
            p.step("scalar", lambda e, lgg=lgg: e.activation(
                x2a[:, :], lgg[:, :], AF.Exp, scale=sccol[:, 0:1], bias=sccol[:, 1:2]),
                reads=[lgg, sccol], writes=[x2a])
            p.step("vector", lambda e, hsg=hsg: e.tensor_add(
                x2a[:, :], x2a[:, :], hsg[:, :]), reads=[x2a, hsg], writes=[x2a])

            def apply(e):
                e.scalar_tensor_tensor(t1f[:, :], x2a[:, :], 1.0, ps[:, :],
                                       ALU.mult, ALU.mult)
                return e.tensor_sub(t2f[:, :], t1f[:, :], pss[:, :])
            p.step("vector", apply, reads=[x2a, ps, pss], writes=[t1f, t2f])
            p.step("scalar", lambda e, g=g: e.activation(
                ret[:, :], t2f[:, :], AF.Identity, scale=lnw[:, g:g + 1], bias=lnb_[:, g:g + 1]),
                reads=[t2f, lnw, lnb_], writes=[ret])
            p.step("vector", lambda e: e.tensor_add(ret[:, :], ret[:, :], xr[:, :]),
                   reads=[ret, xr], writes=[ret])
            p.step("sync", lambda e, g=g: e.dma_start(
                out=_tile(x3_ap, g), in_=ret[:, :]), reads=[ret], dma=True)
        p.emit()


def emit_l1f(nc, S, A, x_ap, ln4b_ap, out_ap):
    emit_ln(nc, S, A, x_ap, out_bf=ln4b_ap)
    with ExitStack() as ctx:
        sb = lambda shape, dt=F32: ctx.enter_context(nc.sbuf_tensor(_nm(), shape, dt))
        lnb = sb([128, NT * TL], BF16)
        wbf = [sb([128, C]) for _ in range(2)]
        awb = sb([128, C], BF16)
        sgb = sb([128, C], BF16)
        sgf = sb([128, C])
        arow = sb([1, 128])
        acol = sb([128, 1])
        bcol = sb([128, 16])
        o128 = sb([128, 128], BF16)
        o11 = sb([1, 1])
        ot = [sb([128, TL]) for _ in range(2)]
        xr = [sb([128, TL]) for _ in range(2)]
        PA = ctx.enter_context(nc.psum_tensor(_nm(), [128, TL], F32))
        PAL = ctx.enter_context(nc.psum_tensor(_nm(), [128, 128], F32))
        PAC = ctx.enter_context(nc.psum_tensor(_nm(), [128, 1], F32))

        p = Prog(nc, S)
        p.step("sync", lambda e: e.dma_start(out=o128[:, :], in_=A["ones128b"]),
               writes=[o128], dma=True)
        p.step("sync", lambda e: e.dma_start(out=bcol[:, :], in_=A["bias_cols"]),
               writes=[bcol], dma=True)
        p.step("gpsimd", lambda e: e.memset(o11[:, :], 1.0), writes=[o11])
        for g in range(NT):
            p.step("sync", lambda e, g=g: e.dma_start(
                out=lnb[:, g * TL:(g + 1) * TL], in_=_tile(ln4b_ap, g)),
                writes=[lnb], dma=True)
        for j in range(NT):
            wbj = wbf[j % 2]
            PX = PA
            otj, xrj = ot[j % 2], xr[j % 2]
            p.step("sync", lambda e, wbj=wbj, j=j: e.dma_start(
                out=wbj[:, :], in_=A["wbP"][j, :, :]), writes=[wbj], dma=True)
            p.step("scalar", lambda e, wbj=wbj: e.activation(awb[:, :], wbj[:, :], AF.Abs),
                   reads=[wbj], writes=[awb])

            def amm(e):
                for ct in range(NT):
                    i = e.matmul(PAL[:, :], o128[:, :], awb[:, ct * 128:(ct + 1) * 128],
                                 start=(ct == 0), stop=(ct == NT - 1))
                return i
            p.step("tensor", amm, reads=[o128, awb], writes=[PAL])
            p.step("vector", lambda e: e.tensor_scalar_mul(arow[0:1, :], PAL[0:1, :], 1.0 / C),
                   reads=[PAL], writes=[arow])
            p.step("tensor", lambda e: e.matmul(PAC[:, :], arow[0:1, :], o11[0:1, :],
                                                start=True, stop=True),
                   reads=[arow, o11], writes=[PAC])
            p.step("vector", lambda e: e.tensor_copy(acol[:, :], PAC[:, :]),
                   reads=[PAC], writes=[acol])

            def sgn(e, wbj=wbj):
                e.tensor_single_scalar(sgf[:, :], wbj[:, :], 0.0, ALU.is_ge)
                return e.tensor_scalar(sgb[:, :], sgf[:, :], 2.0, -1.0, ALU.mult, ALU.add)
            p.step("vector", sgn, reads=[wbj], writes=[sgf, sgb])
            p.step("tensor", lambda e, PX=PX: _mm_ct(e, PX, sgb, lnb),
                   reads=[sgb, lnb], writes=[PX])
            p.step("sync", lambda e, xrj=xrj, j=j: e.dma_start(
                out=xrj[:, :], in_=_tile(x_ap, j)), writes=[xrj], dma=True)
            p.step("scalar", lambda e, PX=PX, otj=otj, j=j: e.activation(
                otj[:, :], PX[:, :], AF.Identity, scale=acol[:, 0:1], bias=bcol[:, j:j + 1]),
                reads=[PX, acol, bcol], writes=[otj])
            p.step("vector", lambda e, otj=otj, xrj=xrj: e.tensor_add(
                otj[:, :], otj[:, :], xrj[:, :]), reads=[otj, xrj], writes=[otj])
            p.step("sync", lambda e, otj=otj, j=j: e.dma_start(
                out=_tile(out_ap, j), in_=otj[:, :]), reads=[otj], dma=True)
        p.emit()


def emit_allgather(nc, S, in_ap, out_ap):
    p = Prog(nc, S)
    p.step("gpsimd", lambda e: e.collective_compute(
        "AllGather", mybir.AluOpType.bypass,
        replica_groups=[list(range(W))],
        ins=[in_ap], outs=[out_ap]),
        reads=[in_ap], writes=[out_ap])
    p.step("gpsimd", lambda e: e.engine_nop(), reads=[out_ap])
    p.emit()


def _mk_nc():
    return bass.Bass("TRN2", target_bir_lowering=False, debug=False)


_SHAPES = {
    "wqkvP": ([48, 128, C], F32), "woutP": ([16, 128, C], F32),
    "wupP": ([64, 128, C], F32), "wgateP": ([64, 128, C], F32),
    "wdownP": ([16, 128, INTER], F32), "wzP": ([16, 128, C], F32),
    "whP": ([16, 128, C], F32), "wbP": ([16, 128, C], F32),
    "decay_row": ([1, H], F32), "su_ug_rows": ([1, C], F32),
    "su_dp_cols": ([128, 64], F32), "wsu_up_cols": ([128, 64], F32),
    "wsu_gt_cols": ([128, 64], F32), "wsd_cols": ([128, 16], F32),
    "bz_cols": ([128, 16], F32), "bh_cols": ([128, 16], F32),
    "lnw_cols": ([128, 16], F32), "lnb_cols": ([128, 16], F32),
    "h0_cols": ([128, 16], F32), "bias_cols": ([128, 16], F32),
    "ones1f": ([1, 128], F32), "ones_row": ([1, C], F32),
    "l0cm": ([128, 24], F32), "l1m": ([128, 8], F32), "l1mneg": ([128, 8], F32),
    "f_col": ([128, 1], F32), "gneg_col": ([128, 1], F32),
    "H128b": ([128, 128], BF16), "TRIb": ([128, 128], BF16),
    "ones128b": ([128, 128], BF16),
}


class LazyAps(dict):
    def __init__(self, nc):
        super().__init__()
        self.nc = nc

    def __missing__(self, k):
        shape, dt = _SHAPES[k]
        ap = self.nc.dram_tensor(k, shape, dt, kind="ExternalInput").ap()
        self[k] = ap
        return ap


def _shared_aps(nc):
    return LazyAps(nc)


_PROG_CACHE = {}


def _build(name):
    if name in _PROG_CACHE:
        return _PROG_CACHE[name]
    nc = _mk_nc()
    A = _shared_aps(nc)
    with ExitStack() as ctx:
        S = Sems(ctx, nc)
        if name == "l0a1":
            x = nc.dram_tensor("xT", [C, TL], F32, kind="ExternalInput").ap()
            ln0 = nc.dram_tensor("ln0b", [C, TL], BF16, kind="ExternalOutput").ap()
            st = nc.dram_tensor("states", [128, 32], F32, kind="ExternalOutput").ap()
            emit_l0a(nc, S, A, x, ln0, st, phase2=False)
        elif name in ("l0a2", "l0a2dbg"):
            x = nc.dram_tensor("xT", [C, TL], F32, kind="ExternalInput").ap()
            ln0 = nc.dram_tensor("ln0b", [C, TL], BF16, kind="ExternalInput").ap()
            sall = nc.dram_tensor("states_all", [W * 128, 32], F32, kind="ExternalInput").ap()
            x1 = nc.dram_tensor("x1T", [C, TL], F32, kind="ExternalOutput").ap()
            yT = nc.dram_tensor("yT", [C, TL], BF16,
                                kind="ExternalOutput" if name == "l0a2dbg" else "Internal").ap()
            emit_l0a(nc, S, A, x, ln0, None, phase2=True, x1_ap=x1, states_all_ap=sall,
                     y_ap=yT)
        elif name in ("l0f", "l0fdbg"):
            x = nc.dram_tensor("xT", [C, TL], F32, kind="ExternalInput").ap()
            x2 = nc.dram_tensor("x2T", [C, TL], F32, kind="ExternalOutput").ap()
            lnsu = nc.dram_tensor("lnsu", [C, TL], F32,
                                  kind="ExternalOutput" if name == "l0fdbg" else "Internal").ap()
            x1h = nc.dram_tensor("x1h", [C, TL], BF16,
                                 kind="ExternalOutput" if name == "l0fdbg" else "Internal").ap()
            x1c_d = (nc.dram_tensor("x1c_dbg", [C, TL], BF16, kind="ExternalOutput").ap()
                     if name == "l0fdbg" else None)
            w2 = nc.dram_tensor("w2", [16, 128, INTER], BF16).ap()
            p6 = nc.dram_tensor("p6", [128, 64 * 512], BF16).ap()
            emit_l0f(nc, S, A, x, x2, lnsu, x1h, w2, p6,
                     x1c_ap=(x1c_d if name == "l0fdbg" else None))
        elif name == "l1a1":
            x = nc.dram_tensor("xT", [C, TL], F32, kind="ExternalInput").ap()
            hs = nc.dram_tensor("hsT", [C, TL], F32, kind="ExternalOutput").ap()
            l = nc.dram_tensor("lT", [C, TL], F32, kind="ExternalOutput").ap()
            ast = nc.dram_tensor("astT", [C, TL], F32, kind="ExternalOutput").ap()
            mz = nc.dram_tensor("mz", [128, 32], F32, kind="ExternalOutput").ap()
            emit_l1a1(nc, S, A, x, hs, l, ast, mz, None)
        elif name == "l1a2":
            l = nc.dram_tensor("lT", [C, TL], F32, kind="ExternalInput").ap()
            ast = nc.dram_tensor("astT", [C, TL], F32, kind="ExternalInput").ap()
            mzall = nc.dram_tensor("mz_all", [W * 128, 32], F32, kind="ExternalInput").ap()
            logh = nc.dram_tensor("loghT", [C, TL], F32, kind="ExternalOutput").ap()
            stats = nc.dram_tensor("stats", [1, 2], F32, kind="ExternalOutput").ap()
            emit_l1a2(nc, S, A, l, ast, mzall, logh, stats)
        elif name == "l1a3":
            x = nc.dram_tensor("xT", [C, TL], F32, kind="ExternalInput").ap()
            hs = nc.dram_tensor("hsT", [C, TL], F32, kind="ExternalInput").ap()
            logh = nc.dram_tensor("loghT", [C, TL], F32, kind="ExternalInput").ap()
            stall = nc.dram_tensor("stats_all", [W, 2], F32, kind="ExternalInput").ap()
            x3 = nc.dram_tensor("x3T", [C, TL], F32, kind="ExternalOutput").ap()
            emit_l1a3(nc, S, A, x, hs, logh, stall, x3)
        elif name == "l1f":
            x = nc.dram_tensor("xT", [C, TL], F32, kind="ExternalInput").ap()
            ln4 = nc.dram_tensor("ln4b", [C, TL], BF16).ap()
            out = nc.dram_tensor("outT", [C, TL], F32, kind="ExternalOutput").ap()
            emit_l1f(nc, S, A, x, ln4, out)
        elif name == "full":
            x = nc.dram_tensor("xT", [C, TL], F32, kind="ExternalInput").ap()
            out = nc.dram_tensor("outT", [C, TL], F32, kind="ExternalOutput").ap()
            dr = lambda nm, shape, dt=F32, sp="Local": nc.dram_tensor(
                nm, shape, dt, addr_space=sp).ap()
            ln0 = dr("ln0b", [C, TL], BF16)
            states = dr("states", [128, 32])
            states_all = dr("states_all", [W * 128, 32], sp="Shared")
            x1 = dr("x1T", [C, TL])
            yT = dr("yT", [C, TL], BF16)
            x2 = dr("x2T", [C, TL])
            lnsu = dr("lnsu", [C, TL])
            x1h = dr("x1h", [C, TL], BF16)
            w2 = dr("w2", [16, 128, INTER], BF16)
            p6 = dr("p6", [128, 64 * 512], BF16)
            hs = dr("hsT", [C, TL])
            l = dr("lT", [C, TL])
            ast = dr("astT", [C, TL])
            mz = dr("mz", [128, 32])
            mz_all = dr("mz_all", [W * 128, 32], sp="Shared")
            logh = dr("loghT", [C, TL])
            stats = dr("stats", [1, 2])
            stats_all = dr("stats_all", [W, 2], sp="Shared")
            x3 = dr("x3T", [C, TL])
            ln4 = dr("ln4b", [C, TL], BF16)
            emit_l0a(nc, S, A, x, ln0, states, phase2=False)
            emit_allgather(nc, S, states, states_all)
            emit_l0a(nc, S, A, x, ln0, None, phase2=True, x1_ap=x1,
                     states_all_ap=states_all, y_ap=yT)
            emit_l0f(nc, S, A, x1, x2, lnsu, x1h, w2, p6)
            emit_l1a1(nc, S, A, x2, hs, l, ast, mz, None)
            emit_allgather(nc, S, mz, mz_all)
            emit_l1a2(nc, S, A, l, ast, mz_all, logh, stats)
            emit_allgather(nc, S, stats, stats_all)
            emit_l1a3(nc, S, A, x2, hs, logh, stats_all, x3)
            emit_l1f(nc, S, A, x3, ln4, out)
        else:
            raise KeyError(name)
    _PROG_CACHE[name] = nc
    return nc


def _run(name, sh, per_core_extra):
    nc = _build(name)
    maps = [{**sh, **per_core_extra[c]} for c in range(W)]
    res = run_bass_kernel_spmd(nc, maps, core_ids=list(range(W)))
    return res.results


FUSED = True


def kernel(**inputs):
    sh, pc = pack_inputs(inputs)
    core = [dict(pc[c]) for c in range(W)]

    if FUSED:
        r = _run("full", sh, core)
        out = np.empty((2, 8192, C), np.float32)
        for c in range(W):
            b, s = c // 4, c % 4
            out[b, s * TL:(s + 1) * TL, :] = np.asarray(r[c]["outT"]).T
        return out

    r = _run("l0a1", sh, core)
    ln0 = [np.asarray(r[c]["ln0b"]) for c in range(W)]
    states_all = np.concatenate([np.asarray(r[c]["states"]) for c in range(W)], axis=0)
    for c in range(W):
        core[c]["ln0b"] = ln0[c]
        core[c]["states_all"] = states_all

    r = _run("l0a2", sh, core)
    for c in range(W):
        core[c]["xT"] = np.asarray(r[c]["x1T"])

    r = _run("l0f", sh, core)
    for c in range(W):
        core[c]["xT"] = np.asarray(r[c]["x2T"])

    r = _run("l1a1", sh, core)
    mz_all = np.concatenate([np.asarray(r[c]["mz"]) for c in range(W)], axis=0)
    for c in range(W):
        core[c]["hsT"] = np.asarray(r[c]["hsT"])
        core[c]["lT"] = np.asarray(r[c]["lT"])
        core[c]["astT"] = np.asarray(r[c]["astT"])
        core[c]["mz_all"] = mz_all

    r = _run("l1a2", sh, core)
    stats_all = np.concatenate([np.asarray(r[c]["stats"]) for c in range(W)], axis=0)
    for c in range(W):
        core[c]["loghT"] = np.asarray(r[c]["loghT"])
        core[c]["stats_all"] = stats_all

    r = _run("l1a3", sh, core)
    for c in range(W):
        core[c]["xT"] = np.asarray(r[c]["x3T"])

    r = _run("l1f", sh, core)
    out = np.empty((2, 8192, C), np.float32)
    for c in range(W):
        b, s = c // 4, c % 4
        out[b, s * TL:(s + 1) * TL, :] = np.asarray(r[c]["outT"]).T
    return out


# version 10
# speedup vs baseline: 1.1475x; 1.0054x over previous
import math
import numpy as np
import ml_dtypes
from contextlib import ExitStack

import concourse.bass as bass
import concourse.mybir as mybir
from concourse.bass_utils import run_bass_kernel_spmd

F32 = mybir.dt.float32
BF16 = mybir.dt.bfloat16
U8 = mybir.dt.uint8
AF = mybir.ActivationFunctionType
ALU = mybir.AluOpType

C = 2048
TL = 2048
NT = 16
H = 16
W = 8
INTER = 8192
NEG = -1.0e30
EPS_LN = 1e-5
R2048 = 1.0 / math.sqrt(2048.0)
R8192 = 1.0 / math.sqrt(8192.0)
NBF = ml_dtypes.bfloat16

ENGINES = ("sync", "gpsimd", "tensor", "scalar", "vector")

_NMC = [0]


def _nm():
    _NMC[0] += 1
    return "t%d" % _NMC[0]


def _butterfly_plan(n, dists):
    slot = list(range(n))
    spare = n
    steps = []
    for dist in dists:
        stage = []
        for base in range(0, n, 2 * dist):
            for a in range(base, base + dist):
                b = a + dist
                pa, pb = slot[a], slot[b]
                stage.append((pa, pb, spare))
                slot[b] = spare
                spare = pb
        steps.append(stage)
    return steps, slot


def _pack_ktiles(w):
    K, M = w.shape
    nk, nj = K // 128, M // 128
    p = w.reshape(nk, 128, nj, 128).transpose(2, 1, 0, 3).reshape(nj, 128, K)
    return np.ascontiguousarray(p)


def _cols(v):
    return np.ascontiguousarray(v.reshape(-1, 128).T)


def _hadamard(n):
    Hm = np.array([[1.0]], dtype=np.float32)
    while Hm.shape[0] < n:
        Hm = np.block([[Hm, Hm], [Hm, -Hm]])
    return np.ascontiguousarray(Hm.astype(np.float32))


def pack_inputs(inp):
    f32 = np.float32
    sh = {}
    sh["wqkvP"] = _pack_ktiles(np.asarray(inp["L0_attn_w_qkv"], f32))
    sh["woutP"] = _pack_ktiles(np.asarray(inp["L0_attn_w_out"], f32))
    sh["wupP"] = _pack_ktiles(np.ascontiguousarray(np.asarray(inp["L0_ffn_W_up"], f32).T))
    sh["wgateP"] = _pack_ktiles(np.ascontiguousarray(np.asarray(inp["L0_ffn_W_gate"], f32).T))
    wdT = np.ascontiguousarray(np.asarray(inp["L0_ffn_W_down"], f32).T)
    sh["wdownP"] = _pack_ktiles(wdT)
    sh["wzP"] = _pack_ktiles(np.ascontiguousarray(np.asarray(inp["L1_attn_W_z"], f32).T))
    sh["whP"] = _pack_ktiles(np.ascontiguousarray(np.asarray(inp["L1_attn_W_h"], f32).T))
    sh["wbP"] = _pack_ktiles(np.ascontiguousarray(np.asarray(inp["L1_ffn_weight"], f32).T))

    sh["decay_row"] = np.ascontiguousarray(np.asarray(inp["L0_attn_decay_param"], f32).reshape(1, H))
    sh["su_ug_rows"] = np.ascontiguousarray(np.asarray(inp["L0_ffn_SU_ug"], f32).reshape(1, C))
    sh["su_dp_cols"] = _cols(np.asarray(inp["L0_ffn_SU_dp"], f32))
    sh["wsu_up_cols"] = _cols(np.asarray(inp["L0_ffn_Wscale_ug"], f32)[:INTER])
    sh["wsu_gt_cols"] = _cols(np.asarray(inp["L0_ffn_Wscale_ug"], f32)[INTER:])
    sh["wsd_cols"] = _cols(np.asarray(inp["L0_ffn_Wscale_dp"], f32))
    sh["bz_cols"] = _cols(np.asarray(inp["L1_attn_b_z"], f32))
    sh["bh_cols"] = _cols(np.asarray(inp["L1_attn_b_h"], f32))
    sh["lnw_cols"] = _cols(np.asarray(inp["L1_attn_ln_w"], f32))
    sh["lnb_cols"] = _cols(np.asarray(inp["L1_attn_ln_b"], f32))
    sh["h0_cols"] = _cols(np.asarray(inp["L1_attn_h0"], f32))
    sh["bias_cols"] = _cols(np.asarray(inp["L1_ffn_bias"], f32))

    sh["H128b"] = _hadamard(128).astype(NBF)
    sh["TRIb"] = np.ascontiguousarray(np.triu(np.ones((128, 128), f32)).astype(NBF))
    sh["ones1f"] = np.ones((1, 128), f32)
    sh["ones128b"] = np.ones((128, 128), NBF)
    sh["ones_row"] = np.ones((1, C), f32)

    x = np.asarray(inp["L0_attn_x"], f32)
    pc = []
    for c in range(W):
        b, s = c // 4, c % 4
        d = {}
        d["xT"] = np.ascontiguousarray(x[b, s * TL:(s + 1) * TL, :].T)
        cm = np.zeros((3, W), f32)
        m = np.zeros((W,), f32)
        for j in range(W):
            bj, sj = j // 4, j % 4
            if bj == b and sj < s:
                m[j] = 1.0
                g = s - 1 - sj
                if g < 3:
                    cm[g, j] = 1.0
        d["l0cm"] = np.ascontiguousarray(np.broadcast_to(cm.T.reshape(1, W * 3), (128, W * 3)))
        d["l1m"] = np.ascontiguousarray(np.broadcast_to(m.reshape(1, W), (128, W)))
        d["l1mneg"] = np.ascontiguousarray(np.broadcast_to(((m - 1.0) * 1e30).reshape(1, W), (128, W)))
        first = 1.0 if s == 0 else 0.0
        d["f_col"] = np.full((128, 1), first, f32)
        d["gneg_col"] = np.full((128, 1), (first - 1.0) * 1e30, f32)
        pc.append(d)
    return sh, pc


class Sems:
    def __init__(self, ctx, nc):
        self.nc = nc
        self.ctx = ctx
        self.sem = {e: ctx.enter_context(nc.semaphore(name="sem_" + e)) for e in ENGINES}
        self.cnt = {e: 0 for e in ENGINES}
        self.dpool = []
        self.dcnt = []

    def dma_pool(self, n):
        while len(self.dpool) < n:
            self.dpool.append(self.ctx.enter_context(
                self.nc.semaphore(name="dma%d" % len(self.dpool))))
            self.dcnt.append(0)
        return self.dpool


class Prog:

    def __init__(self, nc, S):
        self.nc = nc
        self.S = S
        self.steps = []
        self.last_writer = {}
        self.readers = {}

    def step(self, engine, fn, reads=(), writes=(), dma=False, fence=False,
             hazard=False):
        idx = len(self.steps)
        deps = set()
        for r in reads:
            k = id(r)
            if k in self.last_writer:
                deps.add(self.last_writer[k])
        for w in writes:
            k = id(w)
            if k in self.last_writer:
                deps.add(self.last_writer[k])
            for rd in self.readers.get(k, ()):
                deps.add(rd)
        for w in writes:
            k = id(w)
            self.last_writer[k] = idx
            self.readers[k] = []
        for r in reads:
            self.readers.setdefault(id(r), []).append(idx)
        if dma:
            key = ("d", id(writes[0]) if writes else id(reads[0]))
        else:
            key = engine
        self.steps.append([engine, fn, 16 if dma else 1, sorted(deps), key])

    def emit(self):
        S = self.S
        steps = self.steps
        slot = {}
        for st in steps:
            if not isinstance(st[4], str) and st[4] not in slot:
                slot[st[4]] = len(slot)
        pool = S.dma_pool(len(slot))
        cnt = dict(S.cnt)
        for k, s_ in slot.items():
            cnt[k] = S.dcnt[s_]
        after = []
        for st in steps:
            cnt[st[4]] += st[2]
            after.append(cnt[st[4]])

        def handle(key):
            return S.sem[key] if isinstance(key, str) else pool[slot[key]]

        waits = []
        marks = {}
        for i, (e, fn, inc, deps, key) in enumerate(steps):
            wl = {}
            for d in deps:
                kd = steps[d][4]
                v = after[d]
                if kd == e:
                    wl[kd] = max(wl.get(kd, 0), v)
                    continue
                m = marks.get((e, kd))
                if m is None:
                    m = S.cnt[kd] if isinstance(kd, str) else S.dcnt[slot[kd]]
                if v > m:
                    wl[kd] = max(wl.get(kd, 0), v)
            for kd, v in wl.items():
                marks[(e, kd)] = max(marks.get((e, kd), 0), v)
            waits.append(wl)

        dkeys = {}
        for i, st in enumerate(steps):
            if not isinstance(st[4], str):
                dkeys[st[4]] = after[i]
        qeng = "sync" if any(st[0] == "sync" for st in steps) else (
            "gpsimd" if any(st[0] == "gpsimd" for st in steps) else None)

        def mkbody(mine, eng_name):
            def body(eng):
                for i in mine:
                    fn, inc, key = steps[i][1], steps[i][2], steps[i][4]
                    for kd, v in sorted(waits[i].items(), key=lambda kv: str(kv[0])):
                        eng.wait_ge(handle(kd), v)
                    fn(eng).then_inc(handle(key), inc)
                if eng_name == qeng:
                    for kd, v in sorted(dkeys.items(), key=lambda kv: str(kv[0])):
                        eng.wait_ge(handle(kd), v)
            return body

        with self.nc.Block() as blk:
            for eng_name in ENGINES:
                mine = [i for i, st in enumerate(steps) if st[0] == eng_name]
                if not mine:
                    continue
                getattr(blk, eng_name)(mkbody(mine, eng_name))
        for st in steps:
            if isinstance(st[4], str):
                S.cnt[st[4]] += st[2]
        for k, s_ in slot.items():
            S.dcnt[s_] = dkeys[k]


def _tile(ap, g, t0=0, t1=TL):
    return ap[g * 128:(g + 1) * 128, t0:t1]


def emit_ln(nc, S, A, x_ap, out_bf=None, out_f32=None, su=False):
    with ExitStack() as ctx:
        sb = lambda shape, dt=F32: ctx.enter_context(nc.sbuf_tensor(_nm(), shape, dt))
        xs = [sb([128, TL]) for _ in range(2)]
        xb = [sb([128, TL], BF16) for _ in range(2)]
        sq = [sb([128, TL], BF16) for _ in range(2)]
        rows = sb([1, 6 * C])
        t1f = sb([128, TL])
        t2f = [sb([128, TL]) for _ in range(2)]
        t2b = [sb([128, TL], BF16) for _ in range(2)]
        o128 = sb([128, 128], BF16)
        onesr = sb([1, C])
        ps = ctx.enter_context(nc.psum_tensor(_nm(), [128, TL], F32))
        pss = ctx.enter_context(nc.psum_tensor(_nm(), [128, TL], F32))

        p = Prog(nc, S)
        p.step("sync", lambda e: e.dma_start(out=o128[:, :], in_=A["ones128b"]),
               writes=[o128], dma=True)
        srcrow = A["su_ug_rows"] if su else A["ones_row"]
        p.step("sync", lambda e, s=srcrow: e.dma_start(out=onesr[0:1, :], in_=s),
               writes=[onesr], dma=True)
        for g in range(NT):
            xg, xbg, sqg = xs[g % 2], xb[g % 2], sq[g % 2]
            p.step("sync", lambda e, xg=xg, g=g: e.dma_start(out=xg[:, :], in_=_tile(x_ap, g)),
                   writes=[xg], dma=True)
            p.step("gpsimd", lambda e, xg=xg, xbg=xbg: e.tensor_copy(xbg[:, :], xg[:, :]),
                   reads=[xg], writes=[xbg])
            p.step("scalar", lambda e, xg=xg, sqg=sqg: e.activation(sqg[:, :], xg[:, :], AF.Square),
                   reads=[xg], writes=[sqg])

            def mms(e, g=g, xbg=xbg, sqg=sqg):
                for n in range(4):
                    e.matmul(ps[:, n * 512:(n + 1) * 512], o128[:, :],
                             xbg[:, n * 512:(n + 1) * 512],
                             start=(g == 0), stop=(g == NT - 1))
                for n in range(4):
                    i = e.matmul(pss[:, n * 512:(n + 1) * 512], o128[:, :],
                                 sqg[:, n * 512:(n + 1) * 512],
                                 start=(g == 0), stop=(g == NT - 1))
                return i
            p.step("tensor", mms, reads=[o128, xbg, sqg], writes=[ps, pss])
        p.emit()

        p = Prog(nc, S)

        def rows1(e):
            e.tensor_scalar_mul(rows[0:1, 0:C], ps[0:1, :], 1.0 / C)
            e.tensor_scalar_mul(rows[0:1, C:2 * C], pss[0:1, :], 1.0 / C)
            e.tensor_mul(rows[0:1, 4 * C:5 * C], rows[0:1, 0:C], rows[0:1, 0:C])
            e.tensor_sub(rows[0:1, C:2 * C], rows[0:1, C:2 * C], rows[0:1, 4 * C:5 * C])
            return e.tensor_scalar_add(rows[0:1, 5 * C:6 * C], rows[0:1, C:2 * C], EPS_LN)
        p.step("vector", rows1, reads=[ps, pss], writes=[rows])

        def rows2(e):
            e.activation(rows[0:1, 5 * C:6 * C], rows[0:1, 5 * C:6 * C], AF.Ln)
            return e.activation(rows[0:1, 2 * C:3 * C], rows[0:1, 5 * C:6 * C],
                                AF.Exp, scale=-0.5)
        p.step("scalar", rows2, reads=[rows], writes=[rows])
        p.step("vector", lambda e: e.tensor_mul(rows[0:1, 3 * C:4 * C],
                                                rows[0:1, 0:C], rows[0:1, 2 * C:3 * C]),
               reads=[rows], writes=[rows])

        ngrp = NT if su else 1
        for g in range(NT):
            xg = xs[g % 2]
            of = t2f[g % 2]
            ob = t2b[g % 2]
            p.step("sync", lambda e, xg=xg, g=g: e.dma_start(out=xg[:, :], in_=_tile(x_ap, g)),
                   writes=[xg], dma=True)
            if g < ngrp:
                def bcast(e, g=g):
                    for n in range(4):
                        e.matmul(ps[:, n * 512:(n + 1) * 512],
                                 onesr[0:1, g * 128:(g + 1) * 128],
                                 rows[0:1, 2 * C + n * 512:2 * C + (n + 1) * 512],
                                 start=True, stop=True)
                    for n in range(4):
                        i = e.matmul(pss[:, n * 512:(n + 1) * 512],
                                     onesr[0:1, g * 128:(g + 1) * 128],
                                     rows[0:1, 3 * C + n * 512:3 * C + (n + 1) * 512],
                                     start=True, stop=True)
                    return i
                p.step("tensor", bcast, reads=[onesr, rows], writes=[ps, pss])

            def apply(e, xg=xg, of=of):
                e.scalar_tensor_tensor(t1f[:, :], xg[:, :], 1.0, ps[:, :],
                                       ALU.mult, ALU.mult)
                return e.tensor_sub(of[:, :], t1f[:, :], pss[:, :])
            p.step("vector", apply, reads=[xg, ps, pss], writes=[t1f, of])
            if out_bf is not None:
                p.step("vector", lambda e, of=of, ob=ob: e.tensor_copy(ob[:, :], of[:, :]),
                       reads=[of], writes=[ob])
                p.step("gpsimd", lambda e, ob=ob, g=g: e.dma_start(out=_tile(out_bf, g), in_=ob[:, :]),
                       reads=[ob], dma=True)
            if out_f32 is not None:
                p.step("gpsimd", lambda e, of=of, g=g: e.dma_start(out=_tile(out_f32, g), in_=of[:, :]),
                       reads=[of], dma=True)
        p.emit()


def _sigmoid_rows(p, drow, drow2, drow3):
    p.step("scalar", lambda e: e.activation(drow[0:1, 16:32], drow[0:1, 0:16],
                                            AF.Exp, scale=-1.0),
           reads=[drow], writes=[drow])
    p.step("vector", lambda e: e.tensor_scalar_add(drow3[0:1, 0:16],
                                                   drow[0:1, 16:32], 1.0),
           reads=[drow], writes=[drow3])
    p.step("vector", lambda e: e.reciprocal(drow2[0:1, 0:16], drow3[0:1, 0:16]),
           reads=[drow3], writes=[drow2], fence=True, hazard=True)


def _phi(p, psum, e_scr, out):
    p.step("vector", lambda e: e.tensor_scalar_min(e_scr[:, :], psum[:, :], 0.0),
           reads=[psum], writes=[e_scr])
    p.step("scalar", lambda e: e.activation(e_scr[:, :], e_scr[:, :], AF.Exp),
           reads=[e_scr], writes=[e_scr])
    p.step("vector", lambda e: e.scalar_tensor_tensor(out[:, :], psum[:, :], 1.0,
                                                      e_scr[:, :], ALU.add, ALU.max),
           reads=[psum, e_scr], writes=[out])


def _mm_ct(e, psum, wtile, lnb, nts=NT, tl=TL, t0=0, tw=None, row=None):
    tw = tl if tw is None else tw
    row = tl if row is None else row
    nch = tw // 512
    for ct in range(nts):
        for n in range(nch):
            i = e.matmul(psum[:, n * 512:(n + 1) * 512],
                         wtile[:, ct * 128:(ct + 1) * 128],
                         lnb[:, ct * row + t0 + n * 512:ct * row + t0 + (n + 1) * 512],
                         start=(ct == 0), stop=(ct == nts - 1))
    return i


def emit_l0a(nc, S, A, x_ap, ln0b_ap, states_ap, phase2, x1_ap=None,
             states_all_ap=None, y_ap=None):
    if not phase2:
        emit_ln(nc, S, A, x_ap, out_bf=ln0b_ap)
    with ExitStack() as ctx:
        sb = lambda shape, dt=F32: ctx.enter_context(nc.sbuf_tensor(_nm(), shape, dt))
        lnb = sb([128, NT * TL], BF16)
        wk = sb([128, C], BF16)
        wv = sb([128, C], BF16)
        wq = sb([128, C], BF16)
        TH2_ = TL // 2
        kf2 = [sb([128, TH2_]) for _ in range(2)]
        qf2 = [sb([128, TH2_]) for _ in range(2)]
        kv2 = [sb([128, TH2_]) for _ in range(2)]
        es2 = [sb([128, TH2_]) for _ in range(2)]
        sckv2 = [sb([128, TH2_]) for _ in range(2)]
        scks2 = [sb([128, TH2_]) for _ in range(2)]
        qks2 = [sb([128, TH2_], BF16) for _ in range(2)]
        ys2 = [sb([128, TH2_], BF16) for _ in range(2)]
        drow = sb([1, 64])
        drow2 = sb([1, 16])
        drow3 = sb([1, 16])
        denr = sb([1, TL])
        denr2 = sb([1, TL])
        dcols = sb([128, 16])
        stc = sb([128, 32])
        o1f = sb([1, 128])
        o128 = sb([128, 128], BF16)
        PS4 = [ctx.enter_context(nc.psum_tensor(_nm(), [128, TH2_], F32))
               for _ in range(4)]
        PA = PS4[0]
        PQ2 = [PS4[0], PS4[2]]

        p = Prog(nc, S)
        for g in range(NT):
            p.step("sync", lambda e, g=g: e.dma_start(
                out=lnb[:, g * TL:(g + 1) * TL], in_=_tile(ln0b_ap, g)),
                writes=[lnb], dma=True)
        p.step("sync", lambda e: e.dma_start(out=drow[0:1, 0:16], in_=A["decay_row"]),
               writes=[drow], dma=True)
        p.step("sync", lambda e: e.dma_start(out=o1f[0:1, :], in_=A["ones1f"]),
               writes=[o1f], dma=True)
        p.step("sync", lambda e: e.dma_start(out=o128[:, :], in_=A["ones128b"]),
               writes=[o128], dma=True)
        _sigmoid_rows(p, drow, drow2, drow3)
        p.step("tensor", lambda e: e.matmul(PA[:, 0:16], o1f[0:1, :], drow2[0:1, 0:16],
                                            start=True, stop=True),
               reads=[o1f, drow2], writes=[PA])
        p.step("vector", lambda e: e.tensor_copy(dcols[:, :], PA[:, 0:16]),
               reads=[PA], writes=[dcols])

        if phase2:
            sall = sb([128, 8 * 32])
            cmks = sb([128, 24])
            dLc = sb([128, 16 * 4])
            Pkv = sb([128, 16])
            Pks = sb([128, 16])
            coef = sb([128, 16])
            ys = sb([128, TL], BF16)
            p.step("sync", lambda e: e.dma_start(
                out=sall[:, :], in_=states_all_ap.rearrange("(j p) s -> p j s", p=128)),
                writes=[sall], dma=True)
            p.step("sync", lambda e: e.dma_start(out=cmks[:, :], in_=A["l0cm"]),
                   writes=[cmks], dma=True)
            p.step("scalar", lambda e: e.activation(dLc[:, 16:32], dcols[:, :], AF.Ln),
                   reads=[dcols], writes=[dLc])

            p.step("scalar", lambda e: e.activation(dLc[:, 32:48], dLc[:, 16:32],
                                                    AF.Exp, scale=4096.0),
                   reads=[dLc], writes=[dLc])
            p.step("scalar", lambda e: e.activation(dLc[:, 16:32], dLc[:, 16:32],
                                                    AF.Exp, scale=2048.0),
                   reads=[dLc], writes=[dLc])
            p.step("gpsimd", lambda e: e.memset(dLc[:, 0:16], 1.0), writes=[dLc])
            p.step("gpsimd", lambda e: e.memset(Pkv[:, :], 0.0), writes=[Pkv])
            p.step("gpsimd", lambda e: e.memset(Pks[:, :], 0.0), writes=[Pks])
            for j in range(W):
                fr = [dLc, cmks, sall, Pkv, Pks, coef]
                fw = [coef, dLc, Pkv, Pks]
                for fs in [
                    lambda e, j=j: e.tensor_scalar(coef[:, :], dLc[:, 16:32],
                                                   cmks[:, 3 * j + 1:3 * j + 2], None, ALU.mult),
                    lambda e, j=j: e.tensor_scalar(dLc[:, 48:64], dLc[:, 32:48],
                                                   cmks[:, 3 * j + 2:3 * j + 3], None, ALU.mult),
                    lambda e: e.tensor_add(coef[:, :], coef[:, :], dLc[:, 48:64]),
                    lambda e, j=j: e.tensor_scalar(dLc[:, 48:64], dLc[:, 0:16],
                                                   cmks[:, 3 * j:3 * j + 1], None, ALU.mult),
                    lambda e: e.tensor_add(coef[:, :], coef[:, :], dLc[:, 48:64]),
                    lambda e, j=j: e.tensor_mul(dLc[:, 48:64], coef[:, :],
                                                sall[:, 32 * j:32 * j + 16]),
                    lambda e: e.tensor_add(Pkv[:, :], Pkv[:, :], dLc[:, 48:64]),
                    lambda e, j=j: e.tensor_mul(dLc[:, 48:64], coef[:, :],
                                                sall[:, 32 * j + 16:32 * j + 32]),
                    lambda e: e.tensor_add(Pks[:, :], Pks[:, :], dLc[:, 48:64]),
                ]:
                    p.step("vector", fs, reads=fr, writes=fw)

        TH2 = TL // 2
        for h in range(H):
            p.step("gpsimd", lambda e, h=h: e.dma_start(
                out=wk[:, :], in_=A["wqkvP"][16 + h, :, :]), writes=[wk], dma=True)
            p.step("gpsimd", lambda e, h=h: e.dma_start(
                out=wv[:, :], in_=A["wqkvP"][32 + h, :, :]), writes=[wv], dma=True)
            if phase2:
                p.step("gpsimd", lambda e, h=h: e.dma_start(
                    out=wq[:, :], in_=A["wqkvP"][h, :, :]), writes=[wq], dma=True)
            for hf in range(2):
                t0 = hf * TH2
                PK, PV = PS4[2 * hf], PS4[2 * hf + 1]
                kfh = kf2[hf]
                esh = es2[hf]
                kvh = kv2[hf]
                sckvh, scksh = sckv2[hf], scks2[hf]
                p.step("tensor", lambda e, PK=PK, t0=t0: _mm_ct(
                    e, PK, wk, lnb, t0=t0, tw=TH2), reads=[wk, lnb], writes=[PK])
                p.step("tensor", lambda e, PV=PV, t0=t0: _mm_ct(
                    e, PV, wv, lnb, t0=t0, tw=TH2), reads=[wv, lnb], writes=[PV])
                p.step("vector", lambda e, PK=PK, esh=esh: e.tensor_scalar_min(
                    esh[:, :], PK[:, :], 0.0), reads=[PK], writes=[esh])
                p.step("scalar", lambda e, esh=esh: e.activation(esh[:, :], esh[:, :], AF.Exp),
                       reads=[esh], writes=[esh])
                p.step("vector", lambda e, PK=PK, esh=esh, kfh=kfh: e.scalar_tensor_tensor(
                    kfh[:, :], PK[:, :], 1.0, esh[:, :], ALU.add, ALU.max),
                    reads=[PK, esh], writes=[kfh])
                p.step("vector", lambda e, kfh=kfh, PV=PV, kvh=kvh: e.tensor_mul(
                    kvh[:, :], kfh[:, :], PV[:, :]), reads=[kfh, PV], writes=[kvh])
                p.step("gpsimd", lambda e, esh=esh: e.memset(esh[:, :], 1.0), writes=[esh])
                p.step("vector", lambda e, h=h, esh=esh: e.tensor_scalar(
                    esh[:, :], esh[:, :], dcols[:, h:h + 1], None, ALU.mult),
                    reads=[esh, dcols], writes=[esh])
                if phase2:
                    PQ = PQ2[hf]
                    qfh = qf2[hf]
                    p.step("tensor", lambda e, PQ=PQ, t0=t0: _mm_ct(
                        e, PQ, wq, lnb, t0=t0, tw=TH2), reads=[wq, lnb], writes=[PQ])
                    init_kv = Pkv[:, h:h + 1] if hf == 0 else sckv2[0][:, TH2 - 1:TH2]
                    init_ks = Pks[:, h:h + 1] if hf == 0 else scks2[0][:, TH2 - 1:TH2]
                    rd_kv = [esh, kvh, Pkv] + ([sckv2[0]] if hf else [])
                    rd_ks = [esh, kfh, Pks] + ([scks2[0]] if hf else [])
                    p.step("vector", lambda e, esh=esh, kvh=kvh, sckvh=sckvh,
                           init_kv=init_kv: e.tensor_tensor_scan(
                        sckvh[:, :], esh[:, :], kvh[:, :], init_kv, ALU.mult, ALU.add),
                        reads=rd_kv, writes=[sckvh], fence=True, hazard=True)
                    p.step("vector", lambda e, esh=esh, kfh=kfh, scksh=scksh,
                           init_ks=init_ks: e.tensor_tensor_scan(
                        scksh[:, :], esh[:, :], kfh[:, :], init_ks, ALU.mult, ALU.add),
                        reads=rd_ks, writes=[scksh], fence=True, hazard=True)
                    p.step("vector", lambda e, PQ=PQ, esh=esh: e.tensor_scalar_min(
                        esh[:, :], PQ[:, :], 0.0), reads=[PQ], writes=[esh])
                    p.step("scalar", lambda e, esh=esh: e.activation(
                        esh[:, :], esh[:, :], AF.Exp), reads=[esh], writes=[esh])
                    p.step("vector", lambda e, PQ=PQ, esh=esh, qfh=qfh: e.scalar_tensor_tensor(
                        qfh[:, :], PQ[:, :], 1.0, esh[:, :], ALU.add, ALU.max),
                        reads=[PQ, esh], writes=[qfh])
                    qksh = qks2[hf]
                    p.step("vector", lambda e, qfh=qfh, scksh=scksh, qksh=qksh: e.tensor_mul(
                        qksh[:, :], qfh[:, :], scksh[:, :]),
                        reads=[qfh, scksh], writes=[qksh])
                    PD = PQ

                    def denmm(e, qksh=qksh, PD=PD):
                        for n in range(TH2 // 512):
                            i = e.matmul(PD[:, n * 512:(n + 1) * 512], o128[:, :],
                                         qksh[:, n * 512:(n + 1) * 512],
                                         start=True, stop=True)
                        return i
                    p.step("tensor", denmm, reads=[o128, qksh], writes=[PD])
                    dr1, dr2 = denr, denr2

                    def denrow(e, PD=PD, hf=hf):
                        e.tensor_scalar_max(dr1[0:1, hf * TH2:(hf + 1) * TH2], PD[0:1, :], 1e-6)
                        return e.reciprocal(dr2[0:1, hf * TH2:(hf + 1) * TH2],
                                            dr1[0:1, hf * TH2:(hf + 1) * TH2])
                    p.step("vector", denrow, reads=[PD], writes=[denr, denr2],
                           fence=True, hazard=True)

                    def denbc(e, PD=PD, hf=hf):
                        for n in range(TH2 // 512):
                            i = e.matmul(PD[:, n * 512:(n + 1) * 512], o1f[0:1, :],
                                         denr2[0:1, hf * TH2 + n * 512:hf * TH2 + (n + 1) * 512],
                                         start=True, stop=True)
                        return i
                    p.step("tensor", denbc, reads=[o1f, denr2], writes=[PD])
                    p.step("vector", lambda e, qfh=qfh, sckvh=sckvh, scksh=scksh: e.tensor_mul(
                        scksh[:, :], qfh[:, :], sckvh[:, :]),
                        reads=[qfh, sckvh], writes=[scksh])
                    ysh = ys2[hf]
                    p.step("vector", lambda e, scksh=scksh, PD=PD, ysh=ysh: e.tensor_mul(
                        ysh[:, :], scksh[:, :], PD[:, :]), reads=[scksh, PD], writes=[ysh])
                    p.step("sync", lambda e, h=h, t0=t0, ysh=ysh: e.dma_start(
                        out=_tile(y_ap, h, t0, t0 + TH2), in_=ysh[:, :]),
                        reads=[ysh, y_ap], dma=True)
                else:
                    init_kv = 0.0 if hf == 0 else sckv2[0][:, TH2 - 1:TH2]
                    init_ks = 0.0 if hf == 0 else scks2[0][:, TH2 - 1:TH2]
                    rd_kv = [esh, kvh] + ([sckv2[0]] if hf else [])
                    rd_ks = [esh, kfh] + ([scks2[0]] if hf else [])
                    p.step("vector", lambda e, esh=esh, kvh=kvh, sckvh=sckvh,
                           init_kv=init_kv: e.tensor_tensor_scan(
                        sckvh[:, :], esh[:, :], kvh[:, :], init_kv, ALU.mult, ALU.add),
                        reads=rd_kv, writes=[sckvh], fence=True, hazard=True)
                    p.step("vector", lambda e, esh=esh, kfh=kfh, scksh=scksh,
                           init_ks=init_ks: e.tensor_tensor_scan(
                        scksh[:, :], esh[:, :], kfh[:, :], init_ks, ALU.mult, ALU.add),
                        reads=rd_ks, writes=[scksh], fence=True, hazard=True)
                    if hf == 1:
                        p.step("vector", lambda e, h=h, sckvh=sckvh: e.tensor_copy(
                            stc[:, h:h + 1], sckvh[:, TH2 - 1:TH2]),
                            reads=[sckvh], writes=[stc])
                        p.step("vector", lambda e, h=h, scksh=scksh: e.tensor_copy(
                            stc[:, 16 + h:16 + h + 1], scksh[:, TH2 - 1:TH2]),
                            reads=[scksh], writes=[stc])

        if not phase2:
            p.step("sync", lambda e: e.dma_start(out=states_ap, in_=stc[:, :]),
                   reads=[stc], dma=True)
            p.emit()
        else:
            p.emit()
            ybs = lnb
            p = Prog(nc, S)
            wo = [wk, wv]
            for g in range(NT):
                p.step("sync", lambda e, g=g: e.dma_start(
                    out=ybs[:, g * TL:(g + 1) * TL], in_=_tile(y_ap, g)),
                    reads=[y_ap], writes=[ybs], dma=True)
            for j in range(NT):
                woj = wo[j % 2]
                p.step("gpsimd", lambda e, j=j, woj=woj: e.dma_start(
                    out=woj[:, :], in_=A["woutP"][j, :, :]), writes=[woj], dma=True)
                for hf in range(2):
                    t0 = hf * TH2
                    PX = PS4[(2 * j + hf) % 4]
                    xr = kf2[hf]
                    ot = kv2[hf]
                    p.step("sync", lambda e, j=j, t0=t0, xr=xr: e.dma_start(
                        out=xr[:, :], in_=_tile(x_ap, j, t0, t0 + TH2)),
                        writes=[xr], dma=True)
                    p.step("tensor", lambda e, PX=PX, t0=t0, woj=woj: _mm_ct(
                        e, PX, woj, ybs, t0=t0, tw=TH2),
                        reads=[woj, ybs], writes=[PX])
                    p.step("vector", lambda e, PX=PX, xr=xr, ot=ot: e.tensor_add(
                        ot[:, :], PX[:, :], xr[:, :]), reads=[PX, xr], writes=[ot])
                    p.step("sync", lambda e, j=j, t0=t0, ot=ot: e.dma_start(
                        out=_tile(x1_ap, j, t0, t0 + TH2), in_=ot[:, :]),
                        reads=[ot], dma=True)
            p.emit()


def _mm_den(e, psum, o128, qks):
    for ct in range(1):
        pass
    for n in range(4):
        i = e.matmul(psum[:, n * 512:(n + 1) * 512], o128[:, :],
                     qks[:, n * 512:(n + 1) * 512], start=True, stop=True)
    return i


def emit_l0f(nc, S, A, x_ap, x2_ap, lnsu_ap, x1h_ap, w2_ap, p6_ap, x1c_ap=None):
    emit_ln(nc, S, A, x_ap, out_f32=lnsu_ap, su=True)

    with ExitStack() as ctx:
        sb = lambda shape, dt=F32: ctx.enter_context(nc.sbuf_tensor(_nm(), shape, dt))
        h128 = sb([128, 128], BF16)
        wdb = [sb([128, INTER], BF16) for _ in range(2)]
        w2s = [sb([128, INTER], BF16) for _ in range(2)]
        PW = ctx.enter_context(nc.psum_tensor(_nm(), [128, 2048], F32))
        p = Prog(nc, S)
        p.step("sync", lambda e: e.dma_start(out=h128[:, :], in_=A["H128b"]),
               writes=[h128], dma=True)
        for j in range(NT):
            wdj, w2j = wdb[j % 2], w2s[j % 2]
            p.step("gpsimd", lambda e, wdj=wdj, j=j: e.dma_start(
                out=wdj[:, :], in_=A["wdownP"][j, :, :]), writes=[wdj], dma=True)
            for q in range(4):
                def fold(e, wdj=wdj, q=q):
                    for s in range(16):
                        i = e.matmul(PW[:, s * 128:(s + 1) * 128], h128[:, :],
                                     wdj[:, (q * 16 + s) * 128:(q * 16 + s + 1) * 128],
                                     start=True, stop=True)
                    return i
                p.step("tensor", fold, reads=[h128, wdj], writes=[PW])
                p.step("scalar", lambda e, w2j=w2j, q=q: e.copy(
                    w2j[:, q * 2048:(q + 1) * 2048], PW[:, :]), reads=[PW], writes=[w2j])
            p.step("sync", lambda e, w2j=w2j, j=j: e.dma_start(
                out=w2_ap[j, :, :], in_=w2j[:, :]), reads=[w2j, w2_ap], dma=True)
        p.emit()

    bsteps16, _ = _butterfly_plan(16, [8, 4, 2, 1])
    with ExitStack() as ctx:
        sb = lambda shape, dt=F32: ctx.enter_context(nc.sbuf_tensor(_nm(), shape, dt))
        TH = TL // 2
        ftile = [sb([128, TH]) for _ in range(17)]
        x1c = sb([128, NT * TH], BF16)
        x1o = sb([128, NT * TH], BF16)
        h128 = sb([128, 128], BF16)
        PF = ctx.enter_context(nc.psum_tensor(_nm(), [128, TH], F32))
        PG = ctx.enter_context(nc.psum_tensor(_nm(), [128, TH], F32))
        for half in range(2):
            t0 = half * TH
            p = Prog(nc, S)
            if half == 0:
                p.step("sync", lambda e: e.dma_start(out=h128[:, :], in_=A["H128b"]),
                       writes=[h128], dma=True)
            for g in range(NT):
                p.step("sync", lambda e, g=g, t0=t0: e.dma_start(
                    out=ftile[g][:, :], in_=_tile(lnsu_ap, g, t0, t0 + TH)),
                    writes=[ftile[g]], dma=True)
            for si, stage in enumerate(bsteps16):
                last = si == len(bsteps16) - 1
                for pi, (pa, pb, sp) in enumerate(stage):
                    if not last:
                        def bf_(e, pa=pa, pb=pb, sp=sp):
                            e.tensor_sub(ftile[sp][:, :], ftile[pa][:, :], ftile[pb][:, :])
                            return e.tensor_add(ftile[pa][:, :], ftile[pa][:, :], ftile[pb][:, :])
                        p.step("vector", bf_, reads=[ftile[pa], ftile[pb]],
                               writes=[ftile[sp], ftile[pa]])
                    else:
                        la = 2 * pi
                        def bf2(e, pa=pa, pb=pb, la=la):
                            e.tensor_sub(x1c[:, (la + 1) * TH:(la + 2) * TH],
                                         ftile[pa][:, :], ftile[pb][:, :])
                            return e.tensor_add(x1c[:, la * TH:(la + 1) * TH],
                                                ftile[pa][:, :], ftile[pb][:, :])
                        p.step("vector", bf2, reads=[ftile[pa], ftile[pb]], writes=[x1c])
            if x1c_ap is not None:
                for a in range(NT):
                    p.step("sync", lambda e, a=a, t0=t0: e.dma_start(
                        out=_tile(x1c_ap, a, t0, t0 + TH), in_=x1c[:, a * TH:(a + 1) * TH]),
                        reads=[x1c], dma=True)
            for a in range(NT):
                PX = PF if a % 2 == 0 else PG
                def h128mm(e, a=a, PX=PX):
                    for n in range(TH // 512):
                        i = e.matmul(PX[:, n * 512:(n + 1) * 512], h128[:, :],
                                     x1c[:, a * TH + n * 512:a * TH + (n + 1) * 512],
                                     start=True, stop=True)
                    return i
                p.step("tensor", h128mm, reads=[h128, x1c], writes=[PX])
                p.step("scalar", lambda e, a=a, PX=PX: e.copy(
                    x1o[:, a * TH:(a + 1) * TH], PX[:, :]), reads=[PX], writes=[x1o])
                p.step("sync", lambda e, a=a, t0=t0: e.dma_start(
                    out=_tile(x1h_ap, a, t0, t0 + TH), in_=x1o[:, a * TH:(a + 1) * TH]),
                    reads=[x1o, x1h_ap], dma=True)
            p.emit()

    bsteps64, _ = _butterfly_plan(64, [32, 16, 8, 4, 2, 1])
    TC = 512
    with ExitStack() as ctx:
        sb = lambda shape, dt=F32: ctx.enter_context(nc.sbuf_tensor(_nm(), shape, dt))
        sup = sb([128, 64])
        sgt = sb([128, 64])
        sdn = sb([128, 16])
        sud = sb([128, 64])
        p = Prog(nc, S)
        p.step("sync", lambda e: e.dma_start(out=sup[:, :], in_=A["wsu_up_cols"]),
               writes=[sup], dma=True)
        p.step("sync", lambda e: e.dma_start(out=sgt[:, :], in_=A["wsu_gt_cols"]),
               writes=[sgt], dma=True)
        p.step("sync", lambda e: e.dma_start(out=sdn[:, :], in_=A["wsd_cols"]),
               writes=[sdn], dma=True)
        p.step("sync", lambda e: e.dma_start(out=sud[:, :], in_=A["su_dp_cols"]),
               writes=[sud], dma=True)
        p.step("vector", lambda e: e.tensor_scalar_mul(sup[:, :], sup[:, :], R2048),
               reads=[sup], writes=[sup])
        p.step("vector", lambda e: e.tensor_scalar_mul(sgt[:, :], sgt[:, :], R2048),
               reads=[sgt], writes=[sgt])
        p.step("vector", lambda e: e.tensor_scalar_mul(sdn[:, :], sdn[:, :], R8192),
               reads=[sdn], writes=[sdn])
        p.emit()

        for tc in range(TL // TC):
            t0 = tc * TC
            with ExitStack() as c1:
                sb1 = lambda shape, dt=F32: c1.enter_context(nc.sbuf_tensor(_nm(), shape, dt))
                x1s = sb1([128, NT * TC], BF16)
                wub = [sb1([128, C], BF16) for _ in range(2)]
                wgb = [sb1([128, C], BF16) for _ in range(2)]
                us = [sb1([128, TC]) for _ in range(2)]
                gs = [sb1([128, TC]) for _ in range(2)]
                pF = [sb1([128, TC]) for _ in range(65)]
                bnc = sb1([128, 2 * TC], BF16)
                PU = [c1.enter_context(nc.psum_tensor(_nm(), [128, TC], F32))
                      for _ in range(2)]
                PG2 = [c1.enter_context(nc.psum_tensor(_nm(), [128, TC], F32))
                       for _ in range(2)]
                p = Prog(nc, S)
                p.step("sync", lambda e, t0=t0: e.dma_start(
                    out=x1s[:, :],
                    in_=x1h_ap.rearrange("(g p) t -> p g t", p=128)[:, :, t0:t0 + TC]),
                    reads=[x1h_ap], writes=[x1s], dma=True)
                for j in range(64):
                    wuj, wgj = wub[j % 2], wgb[j % 2]
                    PUj, PGj = PU[j % 2], PG2[j % 2]
                    usj, gsj = us[j % 2], gs[j % 2]
                    p.step("gpsimd", lambda e, wuj=wuj, j=j: e.dma_start(
                        out=wuj[:, :], in_=A["wupP"][j, :, :]), writes=[wuj], dma=True)
                    p.step("gpsimd", lambda e, wgj=wgj, j=j: e.dma_start(
                        out=wgj[:, :], in_=A["wgateP"][j, :, :]), writes=[wgj], dma=True)
                    p.step("tensor", lambda e, wuj=wuj, PUj=PUj: _mm_ct(e, PUj, wuj, x1s, tl=TC),
                           reads=[wuj, x1s], writes=[PUj])
                    p.step("tensor", lambda e, wgj=wgj, PGj=PGj: _mm_ct(e, PGj, wgj, x1s, tl=TC),
                           reads=[wgj, x1s], writes=[PGj])
                    p.step("scalar", lambda e, j=j, PUj=PUj, usj=usj: e.activation(
                        usj[:, :], PUj[:, :], AF.Identity, scale=sup[:, j:j + 1]),
                        reads=[PUj, sup], writes=[usj])
                    p.step("scalar", lambda e, j=j, PGj=PGj, gsj=gsj: e.activation(
                        gsj[:, :], PGj[:, :], AF.Silu, scale=sgt[:, j:j + 1]),
                        reads=[PGj, sgt], writes=[gsj])
                    p.step("vector", lambda e, j=j, gsj=gsj, usj=usj: e.scalar_tensor_tensor(
                        pF[j][:, :], gsj[:, :], sud[:, j:j + 1], usj[:, :],
                        ALU.mult, ALU.mult),
                        reads=[gsj, sud, usj], writes=[pF[j]])
                for si, stage in enumerate(bsteps64):
                    last = si == len(bsteps64) - 1
                    for pi, (pa, pb, sp) in enumerate(stage):
                        if not last:
                            def bf_(e, pa=pa, pb=pb, sp=sp):
                                e.tensor_sub(pF[sp][:, :], pF[pa][:, :], pF[pb][:, :])
                                return e.tensor_add(pF[pa][:, :], pF[pa][:, :], pF[pb][:, :])
                            p.step("vector", bf_, reads=[pF[pa], pF[pb]],
                                   writes=[pF[sp], pF[pa]])
                        else:
                            la = 2 * pi
                            def bf2(e, pa=pa, pb=pb):
                                e.tensor_sub(bnc[:, TC:2 * TC], pF[pa][:, :], pF[pb][:, :])
                                return e.tensor_add(bnc[:, 0:TC], pF[pa][:, :], pF[pb][:, :])
                            p.step("vector", bf2, reads=[pF[pa], pF[pb]], writes=[bnc])
                            p.step("sync", lambda e, la=la: e.dma_start(
                                out=p6_ap[:, la * TC:(la + 2) * TC], in_=bnc[:, :]),
                                reads=[bnc, p6_ap], dma=True)
                p.emit()

            with ExitStack() as c2:
                sb2 = lambda shape, dt=F32: c2.enter_context(nc.sbuf_tensor(_nm(), shape, dt))
                p6s = sb2([128, 64 * TC], BF16)
                w2b = [sb2([128, INTER], BF16) for _ in range(2)]
                us2 = sb2([128, TC])
                xr = [sb2([128, TC]) for _ in range(2)]
                ot = [sb2([128, TC]) for _ in range(2)]
                PO = c2.enter_context(nc.psum_tensor(_nm(), [128, TC], F32))
                PO2 = c2.enter_context(nc.psum_tensor(_nm(), [128, TC], F32))
                p = Prog(nc, S)
                p.step("sync", lambda e: e.dma_start(out=p6s[:, :], in_=p6_ap),
                       reads=[p6_ap], writes=[p6s], dma=True)
                for j in range(NT):
                    w2j = w2b[j % 2]
                    xrj, otj = xr[j % 2], ot[j % 2]
                    PX = PO if j % 2 == 0 else PO2
                    p.step("gpsimd", lambda e, w2j=w2j, j=j: e.dma_start(
                        out=w2j[:, :], in_=w2_ap[j, :, :]),
                        reads=[w2_ap], writes=[w2j], dma=True)
                    p.step("sync", lambda e, xrj=xrj, j=j, t0=t0: e.dma_start(
                        out=xrj[:, :], in_=_tile(x_ap, j, t0, t0 + TC)),
                        writes=[xrj], dma=True)

                    def wdmm(e, w2j=w2j, PX=PX):
                        for s in range(64):
                            i = e.matmul(PX[:, :], w2j[:, s * 128:(s + 1) * 128],
                                         p6s[:, s * TC:(s + 1) * TC],
                                         start=(s == 0), stop=(s == 63))
                        return i
                    p.step("tensor", wdmm, reads=[w2j, p6s], writes=[PX])
                    p.step("scalar", lambda e, j=j, PX=PX: e.activation(
                        us2[:, :], PX[:, :], AF.Identity, scale=sdn[:, j:j + 1]),
                        reads=[PX, sdn], writes=[us2])
                    p.step("vector", lambda e, xrj=xrj, otj=otj: e.tensor_add(
                        otj[:, :], us2[:, :], xrj[:, :]), reads=[us2, xrj], writes=[otj])
                    p.step("sync", lambda e, otj=otj, j=j, t0=t0: e.dma_start(
                        out=_tile(x2_ap, j, t0, t0 + TC), in_=otj[:, :]),
                        reads=[otj], dma=True)
                p.emit()


def emit_l1a1(nc, S, A, x_ap, hs_ap, l_ap, ast_ap, mz_ap, pcA):
    emit_ln(nc, S, A, x_ap, out_f32=hs_ap)
    with ExitStack() as ctx:
        sb = lambda shape, dt=F32: ctx.enter_context(nc.sbuf_tensor(_nm(), shape, dt))
        hsb = sb([128, NT * TL], BF16)
        wz = sb([128, C], BF16)
        wh = sb([128, C], BF16)
        k1 = sb([128, TL])
        z2 = sb([128, TL])
        sp_ = sb([128, TL])
        scr = sb([128, TL])
        lcb = sb([128, TL], BF16)
        lzt = sb([128, TL])
        u_s = sb([128, TL])
        ast = sb([128, TL])
        R_s = sb([128, TL])
        d_s = sb([128, TL])
        b_s = sb([128, TL])
        Z_s = sb([128, TL])
        l_s = sb([128, TL])
        msk = sb([128, TL], U8)
        bzc = sb([128, 16])
        bhc = sb([128, 16])
        h0c = sb([128, 16])
        u0c = sb([128, 16])
        tnc = sb([128, 16])
        fcol = sb([128, 1])
        gcol = sb([128, 1])
        rin = sb([128, 16])
        cst = sb([128, 2])
        carry = sb([1, TL])
        crow = sb([1, TL])
        mzc = sb([128, 32])
        trib = sb([128, 128], BF16)
        o1f = sb([1, 128])
        o128 = sb([128, 128], BF16)
        PA = ctx.enter_context(nc.psum_tensor(_nm(), [128, TL], F32))
        PB = ctx.enter_context(nc.psum_tensor(_nm(), [128, TL], F32))

        p = Prog(nc, S)
        p.step("gpsimd", lambda e: e.dma_start(
            out=hsb[:, :], in_=hs_ap.rearrange("(g p) t -> p g t", p=128)),
            writes=[hsb], dma=True)
        for nm, dst in (("bz_cols", bzc), ("bh_cols", bhc), ("h0_cols", h0c),
                        ("TRIb", trib), ("ones1f", o1f), ("ones128b", o128),
                        ("f_col", fcol), ("gneg_col", gcol)):
            p.step("sync", lambda e, nm=nm, dst=dst: e.dma_start(out=dst[:, :], in_=A[nm]),
                   writes=[dst], dma=True)
        p.step("gpsimd", lambda e: e.memset(carry[0:1, :], 0.0), writes=[carry])
        p.step("gpsimd", lambda e: e.memset(cst[:, 0:1], 0.5), writes=[cst])
        p.step("gpsimd", lambda e: e.memset(cst[:, 1:2], 1.0), writes=[cst])

        for fs in [
            lambda e: e.activation(u0c[:, :], h0c[:, :], AF.Relu),
            lambda e: e.activation(u0c[:, :], u0c[:, :], AF.Ln, bias=cst[:, 0:1]),
            lambda e: e.activation(tnc[:, :], h0c[:, :], AF.Abs),
            lambda e: e.activation(tnc[:, :], tnc[:, :], AF.Exp, scale=-1.0),
            lambda e: e.activation(tnc[:, :], tnc[:, :], AF.Ln, bias=cst[:, 1:2]),
        ]:
            p.step("scalar", fs, reads=[h0c, cst, u0c, tnc], writes=[u0c, tnc])

        for fs in [
            lambda e: e.scalar_tensor_tensor(tnc[:, :], h0c[:, :], 0.0, tnc[:, :],
                                             ALU.max, ALU.add),
            lambda e: e.tensor_sub(tnc[:, :], h0c[:, :], tnc[:, :]),
            lambda e: e.tensor_single_scalar(rin[:, :], h0c[:, :], 0.0, ALU.is_ge),
        ]:
            p.step("vector", fs, reads=[h0c, tnc], writes=[tnc, rin])
        p.step("vector", lambda e: e.tensor_copy(msk[:, 0:16], rin[:, :]),
               reads=[rin], writes=[msk])
        p.step("vector", lambda e: e.select(u0c[:, :], msk[:, 0:16], u0c[:, :], tnc[:, :]),
               reads=[msk, u0c, tnc], writes=[u0c])
        p.step("vector", lambda e: e.tensor_scalar(rin[:, :], u0c[:, :], fcol[:, 0:1],
                                                   None, ALU.mult),
               reads=[u0c, fcol], writes=[rin])
        p.step("vector", lambda e: e.tensor_scalar(rin[:, :], rin[:, :], gcol[:, 0:1],
                                                   None, ALU.add),
               reads=[rin, gcol], writes=[rin])

        for j in range(NT):
            p.step("gpsimd", lambda e, j=j: e.dma_start(
                out=wz[:, :], in_=A["wzP"][j, :, :]), writes=[wz], dma=True)
            p.step("gpsimd", lambda e, j=j: e.dma_start(
                out=wh[:, :], in_=A["whP"][j, :, :]), writes=[wh], dma=True)
            p.step("tensor", lambda e: _mm_ct(e, PA, wz, hsb),
                   reads=[wz, hsb], writes=[PA])
            p.step("tensor", lambda e: _mm_ct(e, PB, wh, hsb),
                   reads=[wh, hsb], writes=[PB])
            p.step("scalar", lambda e, j=j: e.activation(
                k1[:, :], PA[:, :], AF.Identity, bias=bzc[:, j:j + 1]),
                reads=[PA, bzc], writes=[k1])
            p.step("scalar", lambda e, j=j: e.activation(
                z2[:, :], PB[:, :], AF.Identity, bias=bhc[:, j:j + 1]),
                reads=[PB, bhc], writes=[z2])

            def spa(e):
                e.activation(scr[:, :], k1[:, :], AF.Abs)
                e.activation(scr[:, :], scr[:, :], AF.Exp, scale=-1.0)
                return e.activation(scr[:, :], scr[:, :], AF.Ln, bias=cst[:, 1:2])
            p.step("scalar", spa, reads=[k1, cst], writes=[scr])
            p.step("vector", lambda e: e.scalar_tensor_tensor(
                sp_[:, :], k1[:, :], 0.0, scr[:, :], ALU.max, ALU.add),
                reads=[k1, scr], writes=[sp_])
            p.step("vector", lambda e: e.tensor_scalar_mul(lcb[:, :], sp_[:, :], -1.0),
                   reads=[sp_], writes=[lcb])
            def gla(e):
                e.activation(lzt[:, :], z2[:, :], AF.Relu)
                e.activation(lzt[:, :], lzt[:, :], AF.Ln, bias=cst[:, 0:1])
                e.activation(scr[:, :], z2[:, :], AF.Abs)
                e.activation(scr[:, :], scr[:, :], AF.Exp, scale=-1.0)
                return e.activation(scr[:, :], scr[:, :], AF.Ln, bias=cst[:, 1:2])
            p.step("scalar", gla, reads=[z2, cst], writes=[lzt, scr])

            def glb(e):
                e.scalar_tensor_tensor(scr[:, :], z2[:, :], 0.0, scr[:, :], ALU.max, ALU.add)
                e.tensor_sub(scr[:, :], z2[:, :], scr[:, :])
                e.tensor_single_scalar(msk[:, :], z2[:, :], 0.0, ALU.is_ge)
                return e.select(lzt[:, :], msk[:, :], lzt[:, :], scr[:, :])
            p.step("vector", glb, reads=[z2, scr, lzt], writes=[scr, msk, lzt])

            def lz2(e):
                e.tensor_sub(u_s[:, :], k1[:, :], sp_[:, :])
                return e.tensor_add(lzt[:, :], lzt[:, :], u_s[:, :])
            p.step("vector", lz2, reads=[k1, sp_, lzt], writes=[lzt, u_s])

            def amm(e):
                for n in range(4):
                    e.matmul(PA[:, n * 512:(n + 1) * 512], trib[:, :],
                             lcb[:, n * 512:(n + 1) * 512], start=True, stop=True)
                for n in range(4):
                    i = e.matmul(PB[:, n * 512:(n + 1) * 512], o1f[0:1, :],
                                 carry[0:1, n * 512:(n + 1) * 512], start=True, stop=True)
                return i
            p.step("tensor", amm, reads=[trib, lcb, o1f, carry], writes=[PA, PB])

            p.step("scalar", lambda e: e.copy(ast[:, :], PA[:, :]),
                   reads=[PA], writes=[ast])
            p.step("vector", lambda e: e.tensor_add(ast[:, :], ast[:, :], PB[:, :]),
                   reads=[ast, PB], writes=[ast])
            p.step("vector", lambda e: e.tensor_sub(u_s[:, :], lzt[:, :], ast[:, :]),
                   reads=[lzt, ast], writes=[u_s])
            p.step("sync", lambda e, j=j: e.dma_start(out=_tile(ast_ap, j), in_=ast[:, :]),
                   reads=[ast], dma=True)

            def cmm(e):
                for ct in range(1):
                    pass
                for n in range(4):
                    i = e.matmul(PA[:, n * 512:(n + 1) * 512], o128[:, :],
                                 lcb[:, n * 512:(n + 1) * 512], start=True, stop=True)
                return i
            p.step("tensor", cmm, reads=[o128, lcb], writes=[PA])
            p.step("vector", lambda e: e.tensor_add(carry[0:1, :], carry[0:1, :], PA[0:1, :]),
                   reads=[PA, carry], writes=[carry])

            p.step("gpsimd", lambda e: e.memset(scr[:, :], 0.0), writes=[scr])
            p.step("vector", lambda e, j=j: e.tensor_tensor_scan(
                R_s[:, :], u_s[:, :], scr[:, :], rin[:, j:j + 1], ALU.max, ALU.add),
                reads=[u_s, scr, rin], writes=[R_s], fence=True, hazard=True)

            def dsh(e, j=j):
                e.tensor_sub(d_s[:, 1:TL], R_s[:, 0:TL - 1], R_s[:, 1:TL])
                e.tensor_scalar(d_s[:, 0:1], R_s[:, 0:1], -1.0, rin[:, j:j + 1],
                                ALU.mult, ALU.add)
                return e.tensor_sub(b_s[:, :], u_s[:, :], R_s[:, :])
            p.step("vector", dsh, reads=[R_s, u_s, rin], writes=[d_s, b_s])

            def exps(e):
                e.activation(d_s[:, :], d_s[:, :], AF.Exp)
                return e.activation(b_s[:, :], b_s[:, :], AF.Exp)
            p.step("scalar", exps, reads=[d_s, b_s], writes=[d_s, b_s])
            p.step("vector", lambda e, j=j: e.tensor_tensor_scan(
                Z_s[:, :], d_s[:, :], b_s[:, :], fcol[:, 0:1], ALU.mult, ALU.add),
                reads=[d_s, b_s, fcol], writes=[Z_s], fence=True, hazard=True)
            p.step("scalar", lambda e: e.activation(l_s[:, :], Z_s[:, :], AF.Ln),
                   reads=[Z_s], writes=[l_s])
            p.step("vector", lambda e: e.tensor_add(l_s[:, :], l_s[:, :], R_s[:, :]),
                   reads=[R_s, l_s], writes=[l_s])
            p.step("sync", lambda e, j=j: e.dma_start(out=_tile(l_ap, j), in_=l_s[:, :]),
                   reads=[l_s], dma=True)
            p.step("vector", lambda e, j=j: e.tensor_copy(mzc[:, j:j + 1], R_s[:, TL - 1:TL]),
                   reads=[R_s], writes=[mzc])
            p.step("vector", lambda e, j=j: e.tensor_copy(mzc[:, 16 + j:17 + j], Z_s[:, TL - 1:TL]),
                   reads=[Z_s], writes=[mzc])
        p.step("sync", lambda e: e.dma_start(out=mz_ap, in_=mzc[:, :]),
               reads=[mzc], dma=True)
        p.emit()


def emit_l1a2(nc, S, A, l_ap, ast_ap, mzall_ap, logh_ap, stats_ap):
    with ExitStack() as ctx:
        sb = lambda shape, dt=F32: ctx.enter_context(nc.sbuf_tensor(_nm(), shape, dt))
        sall = sb([128, 8 * 32])
        l1m = sb([128, 8])
        l1mn = sb([128, 8])
        fcol = sb([128, 1])
        PM = sb([128, 16])
        PZ = sb([128, 16])
        Me = sb([128, 16])
        Ze = sb([128, 16])
        Mn = sb([128, 16])
        t1 = sb([128, 16])
        t2 = sb([128, 16])
        Pcol = sb([128, 16])
        h0c = sb([128, 16])
        u0c = sb([128, 16])
        tnc = sb([128, 16])
        msk16 = sb([128, 16], U8)
        cst = sb([128, 2])
        l_s = [sb([128, TL]) for _ in range(2)]
        ast = [sb([128, TL]) for _ in range(2)]
        dl = [sb([128, TL]) for _ in range(2)]
        mx = [sb([128, TL]) for _ in range(2)]
        scr = [sb([128, TL]) for _ in range(2)]
        acc = sb([128, 34])
        o1c = sb([128, 1])
        rowst = sb([1, 40])
        PS = ctx.enter_context(nc.psum_tensor(_nm(), [1, 512], F32))

        p = Prog(nc, S)
        p.step("sync", lambda e: e.dma_start(
            out=sall[:, :], in_=mzall_ap.rearrange("(j p) s -> p j s", p=128)),
            writes=[sall], dma=True)
        p.step("sync", lambda e: e.dma_start(out=l1m[:, :], in_=A["l1m"]),
               writes=[l1m], dma=True)
        p.step("sync", lambda e: e.dma_start(out=l1mn[:, :], in_=A["l1mneg"]),
               writes=[l1mn], dma=True)
        p.step("sync", lambda e: e.dma_start(out=fcol[:, :], in_=A["f_col"]),
               writes=[fcol], dma=True)
        p.step("sync", lambda e: e.dma_start(out=h0c[:, :], in_=A["h0_cols"]),
               writes=[h0c], dma=True)
        p.step("gpsimd", lambda e: e.memset(PM[:, :], NEG), writes=[PM])
        p.step("gpsimd", lambda e: e.memset(PZ[:, :], 0.0), writes=[PZ])
        p.step("gpsimd", lambda e: e.memset(o1c[:, :], 1.0), writes=[o1c])
        p.step("gpsimd", lambda e: e.memset(cst[:, 0:1], 0.5), writes=[cst])
        p.step("gpsimd", lambda e: e.memset(cst[:, 1:2], 1.0), writes=[cst])
        for j in range(W):
            fr = [sall, l1m, l1mn, PM, Me, Ze, Mn]
            fw = [Me, Ze, Mn, t1, t2]
            for fs in [
                lambda e, j=j: e.tensor_scalar(Me[:, :], sall[:, 32 * j:32 * j + 16],
                                               l1m[:, j:j + 1], l1mn[:, j:j + 1],
                                               ALU.mult, ALU.add),
                lambda e, j=j: e.tensor_scalar(Ze[:, :], sall[:, 32 * j + 16:32 * j + 32],
                                               l1m[:, j:j + 1], None, ALU.mult),
                lambda e: e.tensor_max(Mn[:, :], PM[:, :], Me[:, :]),
                lambda e: e.tensor_sub(t1[:, :], PM[:, :], Mn[:, :]),
                lambda e: e.tensor_sub(t2[:, :], Me[:, :], Mn[:, :]),
            ]:
                p.step("vector", fs, reads=fr, writes=fw)

            def f2(e):
                e.activation(t1[:, :], t1[:, :], AF.Exp)
                return e.activation(t2[:, :], t2[:, :], AF.Exp)
            p.step("scalar", f2, reads=[t1, t2], writes=[t1, t2])

            for fs in [
                lambda e: e.tensor_mul(PZ[:, :], PZ[:, :], t1[:, :]),
                lambda e: e.tensor_mul(t2[:, :], Ze[:, :], t2[:, :]),
                lambda e: e.tensor_add(PZ[:, :], PZ[:, :], t2[:, :]),
                lambda e: e.tensor_copy(PM[:, :], Mn[:, :]),
            ]:
                p.step("vector", fs, reads=[PZ, t1, t2, Ze, Mn], writes=[PZ, PM, t2])
        p.step("vector", lambda e: e.tensor_scalar_max(PZ[:, :], PZ[:, :], 1e-38),
               reads=[PZ], writes=[PZ])
        p.step("scalar", lambda e: e.activation(Pcol[:, :], PZ[:, :], AF.Ln),
               reads=[PZ], writes=[Pcol])
        p.step("vector", lambda e: e.tensor_add(Pcol[:, :], Pcol[:, :], PM[:, :]),
               reads=[Pcol, PM], writes=[Pcol])

        for fs in [
            lambda e: e.activation(u0c[:, :], h0c[:, :], AF.Relu),
            lambda e: e.activation(u0c[:, :], u0c[:, :], AF.Ln, bias=cst[:, 0:1]),
            lambda e: e.activation(tnc[:, :], h0c[:, :], AF.Abs),
            lambda e: e.activation(tnc[:, :], tnc[:, :], AF.Exp, scale=-1.0),
            lambda e: e.activation(tnc[:, :], tnc[:, :], AF.Ln, bias=cst[:, 1:2]),
        ]:
            p.step("scalar", fs, reads=[h0c, cst, u0c, tnc], writes=[u0c, tnc])

        for fs in [
            lambda e: e.scalar_tensor_tensor(tnc[:, :], h0c[:, :], 0.0, tnc[:, :],
                                             ALU.max, ALU.add),
            lambda e: e.tensor_sub(tnc[:, :], h0c[:, :], tnc[:, :]),
            lambda e: e.tensor_single_scalar(msk16[:, :], h0c[:, :], 0.0, ALU.is_ge),
            lambda e: e.select(u0c[:, :], msk16[:, :], u0c[:, :], tnc[:, :]),
            lambda e: e.tensor_scalar(u0c[:, :], u0c[:, :], fcol[:, 0:1], None, ALU.mult),
        ]:
            p.step("vector", fs, reads=[h0c, tnc, msk16, u0c, fcol],
                   writes=[tnc, msk16, u0c])
        p.step("scalar", lambda e: e.activation(scr[0][:, 0:16], u0c[:, :], AF.Copy,
                                                accum_out=acc[:, 32:33]),
               reads=[u0c], writes=[scr[0], acc])
        p.step("scalar", lambda e: e.activation(scr[0][:, 0:16], u0c[:, :], AF.Square,
                                                accum_out=acc[:, 33:34]),
               reads=[u0c], writes=[scr[0], acc])

        for j in range(NT):
            lj, aj, dj, mj, sj = (l_s[j % 2], ast[j % 2], dl[j % 2], mx[j % 2],
                                  scr[j % 2])
            p.step("sync", lambda e, j=j, lj=lj: e.dma_start(out=lj[:, :], in_=_tile(l_ap, j)),
                   writes=[lj], dma=True)
            p.step("sync", lambda e, j=j, aj=aj: e.dma_start(out=aj[:, :], in_=_tile(ast_ap, j)),
                   writes=[aj], dma=True)

            def lae(e, j=j, lj=lj, dj=dj, mj=mj):
                e.tensor_scalar(dj[:, :], lj[:, :], Pcol[:, j:j + 1], None, ALU.subtract)
                return e.tensor_scalar(mj[:, :], lj[:, :], Pcol[:, j:j + 1], None, ALU.max)
            p.step("vector", lae, reads=[lj, Pcol], writes=[dj, mj])

            def lex(e, dj=dj):
                e.activation(dj[:, :], dj[:, :], AF.Abs)
                e.activation(dj[:, :], dj[:, :], AF.Exp, scale=-1.0)
                return e.activation(dj[:, :], dj[:, :], AF.Ln, bias=cst[:, 1:2])
            p.step("scalar", lex, reads=[dj, cst], writes=[dj])

            def lsum(e, mj=mj, dj=dj, aj=aj):
                e.tensor_add(mj[:, :], mj[:, :], dj[:, :])
                return e.tensor_add(mj[:, :], mj[:, :], aj[:, :])
            p.step("vector", lsum, reads=[mj, dj, aj], writes=[mj])
            p.step("sync", lambda e, j=j, mj=mj: e.dma_start(out=_tile(logh_ap, j), in_=mj[:, :]),
                   reads=[mj], dma=True)
            p.step("scalar", lambda e, j=j, mj=mj, sj=sj: e.activation(
                sj[:, :], mj[:, :], AF.Copy, accum_out=acc[:, j:j + 1]),
                reads=[mj], writes=[sj, acc])
            p.step("scalar", lambda e, j=j, mj=mj, sj=sj: e.activation(
                sj[:, :], mj[:, :], AF.Square, accum_out=acc[:, 16 + j:17 + j]),
                reads=[mj], writes=[sj, acc])

        def statmm(e):
            return e.matmul(PS[0:1, 0:34], o1c[:, 0:1], acc[:, :], start=True, stop=True)
        p.step("tensor", statmm, reads=[o1c, acc], writes=[PS])

        for fs in [
            lambda e: e.reduce_sum(rowst[0:1, 0:1], PS[0:1, 0:16], axis=mybir.AxisListType.X),
            lambda e: e.reduce_sum(rowst[0:1, 1:2], PS[0:1, 16:32], axis=mybir.AxisListType.X),
            lambda e: e.tensor_add(rowst[0:1, 0:1], rowst[0:1, 0:1], PS[0:1, 32:33]),
            lambda e: e.tensor_add(rowst[0:1, 1:2], rowst[0:1, 1:2], PS[0:1, 33:34]),
        ]:
            p.step("vector", fs, reads=[PS, rowst], writes=[rowst])
        p.step("sync", lambda e: e.dma_start(out=stats_ap, in_=rowst[0:1, 0:2]),
               reads=[rowst], dma=True)
        p.emit()


def emit_l1a3(nc, S, A, x_ap, hs_ap, logh_ap, statsall_ap, x3_ap):
    NTOT = 2.0 * 8193.0 * 2048.0
    with ExitStack() as ctx:
        sb = lambda shape, dt=F32: ctx.enter_context(nc.sbuf_tensor(_nm(), shape, dt))
        stall = sb([8, 2])
        rowst = sb([1, 16])
        o8 = sb([8, 1])
        o1f = sb([1, 128])
        sccol = sb([128, 2])
        x2a = sb([128, TL])
        lg = [sb([128, TL]) for _ in range(2)]
        hsx = [sb([128, TL]) for _ in range(2)]
        x2b = [sb([128, TL], BF16) for _ in range(2)]
        sqb = [sb([128, TL], BF16) for _ in range(2)]
        rows = sb([1, 6 * C])
        t1f = sb([128, TL])
        t2f = sb([128, TL])
        ret = sb([128, TL])
        xr = sb([128, TL])
        lnw = sb([128, 16])
        lnb_ = sb([128, 16])
        o128 = sb([128, 128], BF16)
        ps = ctx.enter_context(nc.psum_tensor(_nm(), [128, TL], F32))
        pss = ctx.enter_context(nc.psum_tensor(_nm(), [128, TL], F32))

        p = Prog(nc, S)
        p.step("sync", lambda e: e.dma_start(out=stall[:, :], in_=statsall_ap),
               writes=[stall], dma=True)
        p.step("sync", lambda e: e.dma_start(out=o1f[0:1, :], in_=A["ones1f"]),
               writes=[o1f], dma=True)
        p.step("sync", lambda e: e.dma_start(out=o128[:, :], in_=A["ones128b"]),
               writes=[o128], dma=True)
        p.step("sync", lambda e: e.dma_start(out=lnw[:, :], in_=A["lnw_cols"]),
               writes=[lnw], dma=True)
        p.step("sync", lambda e: e.dma_start(out=lnb_[:, :], in_=A["lnb_cols"]),
               writes=[lnb_], dma=True)
        p.step("gpsimd", lambda e: e.memset(o8[:, :], 1.0), writes=[o8])
        p.step("tensor", lambda e: e.matmul(ps[0:1, 0:2], o8[:, 0:1], stall[:, :],
                                            start=True, stop=True),
               reads=[o8, stall], writes=[ps])

        for fs in [
            lambda e: e.tensor_scalar_mul(rowst[0:1, 0:1], ps[0:1, 0:1], 1.0 / NTOT),
            lambda e: e.tensor_mul(rowst[0:1, 2:3], rowst[0:1, 0:1], rowst[0:1, 0:1]),
            lambda e: e.tensor_scalar_mul(rowst[0:1, 2:3], rowst[0:1, 2:3], NTOT),
            lambda e: e.tensor_sub(rowst[0:1, 3:4], ps[0:1, 1:2], rowst[0:1, 2:3]),
            lambda e: e.tensor_scalar_mul(rowst[0:1, 3:4], rowst[0:1, 3:4],
                                          1.0 / (NTOT - 1.0)),
        ]:
            p.step("vector", fs, reads=[ps, rowst], writes=[rowst])

        p.step("scalar", lambda e: e.activation(rowst[0:1, 3:4], rowst[0:1, 3:4], AF.Ln),
               reads=[rowst], writes=[rowst])
        p.step("scalar", lambda e: e.activation(rowst[0:1, 4:5], rowst[0:1, 3:4],
                                                AF.Exp, scale=-0.5),
               reads=[rowst], writes=[rowst])

        p.step("vector", lambda e: e.tensor_mul(rowst[0:1, 5:6], rowst[0:1, 0:1],
                                                rowst[0:1, 4:5]),
               reads=[rowst], writes=[rowst])
        p.step("vector", lambda e: e.tensor_scalar_mul(rowst[0:1, 5:6], rowst[0:1, 5:6],
                                                       -1.0),
               reads=[rowst], writes=[rowst])
        p.step("tensor", lambda e: e.matmul(ps[:, 2:4], o1f[0:1, :], rowst[0:1, 4:6],
                                            start=True, stop=True),
               reads=[o1f, rowst], writes=[ps])
        p.step("vector", lambda e: e.tensor_copy(sccol[:, :], ps[:, 2:4]),
               reads=[ps], writes=[sccol])

        for g in range(NT):
            lgg, hsg = lg[g % 2], hsx[g % 2]
            x2bg, sqg = x2b[g % 2], sqb[g % 2]
            p.step("sync", lambda e, lgg=lgg, g=g: e.dma_start(
                out=lgg[:, :], in_=_tile(logh_ap, g)), writes=[lgg], dma=True)
            p.step("sync", lambda e, hsg=hsg, g=g: e.dma_start(
                out=hsg[:, :], in_=_tile(hs_ap, g)), writes=[hsg], dma=True)
            p.step("scalar", lambda e, lgg=lgg: e.activation(
                x2a[:, :], lgg[:, :], AF.Exp, scale=sccol[:, 0:1], bias=sccol[:, 1:2]),
                reads=[lgg, sccol], writes=[x2a])
            p.step("vector", lambda e, hsg=hsg: e.tensor_add(
                x2a[:, :], x2a[:, :], hsg[:, :]), reads=[x2a, hsg], writes=[x2a])
            p.step("gpsimd", lambda e, x2bg=x2bg: e.tensor_copy(x2bg[:, :], x2a[:, :]),
                   reads=[x2a], writes=[x2bg])
            p.step("scalar", lambda e, sqg=sqg: e.activation(sqg[:, :], x2a[:, :], AF.Square),
                   reads=[x2a], writes=[sqg])

            def mms(e, g=g, x2bg=x2bg, sqg=sqg):
                for n in range(4):
                    e.matmul(ps[:, n * 512:(n + 1) * 512], o128[:, :],
                             x2bg[:, n * 512:(n + 1) * 512],
                             start=(g == 0), stop=(g == NT - 1))
                for n in range(4):
                    i = e.matmul(pss[:, n * 512:(n + 1) * 512], o128[:, :],
                                 sqg[:, n * 512:(n + 1) * 512],
                                 start=(g == 0), stop=(g == NT - 1))
                return i
            p.step("tensor", mms, reads=[o128, x2bg, sqg], writes=[ps, pss])

        def rows1(e):
            e.tensor_scalar_mul(rows[0:1, 0:C], ps[0:1, :], 1.0 / C)
            e.tensor_scalar_mul(rows[0:1, C:2 * C], pss[0:1, :], 1.0 / C)
            e.tensor_mul(rows[0:1, 4 * C:5 * C], rows[0:1, 0:C], rows[0:1, 0:C])
            e.tensor_sub(rows[0:1, C:2 * C], rows[0:1, C:2 * C], rows[0:1, 4 * C:5 * C])
            return e.tensor_scalar_add(rows[0:1, 5 * C:6 * C], rows[0:1, C:2 * C], EPS_LN)
        p.step("vector", rows1, reads=[ps, pss], writes=[rows])

        def rows2(e):
            e.activation(rows[0:1, 5 * C:6 * C], rows[0:1, 5 * C:6 * C], AF.Ln)
            return e.activation(rows[0:1, 2 * C:3 * C], rows[0:1, 5 * C:6 * C],
                                AF.Exp, scale=-0.5)
        p.step("scalar", rows2, reads=[rows], writes=[rows])
        p.step("vector", lambda e: e.tensor_mul(rows[0:1, 3 * C:4 * C],
                                                rows[0:1, 0:C], rows[0:1, 2 * C:3 * C]),
               reads=[rows], writes=[rows])

        def bcast(e):
            for n in range(4):
                e.matmul(ps[:, n * 512:(n + 1) * 512], o1f[0:1, :],
                         rows[0:1, 2 * C + n * 512:2 * C + (n + 1) * 512],
                         start=True, stop=True)
            for n in range(4):
                i = e.matmul(pss[:, n * 512:(n + 1) * 512], o1f[0:1, :],
                             rows[0:1, 3 * C + n * 512:3 * C + (n + 1) * 512],
                             start=True, stop=True)
            return i
        p.step("tensor", bcast, reads=[o1f, rows], writes=[ps, pss])

        for g in range(NT):
            lgg, hsg = lg[g % 2], hsx[g % 2]
            p.step("sync", lambda e, lgg=lgg, g=g: e.dma_start(
                out=lgg[:, :], in_=_tile(logh_ap, g)), writes=[lgg], dma=True)
            p.step("sync", lambda e, hsg=hsg, g=g: e.dma_start(
                out=hsg[:, :], in_=_tile(hs_ap, g)), writes=[hsg], dma=True)
            p.step("sync", lambda e, g=g: e.dma_start(
                out=xr[:, :], in_=_tile(x_ap, g)), writes=[xr], dma=True)
            p.step("scalar", lambda e, lgg=lgg: e.activation(
                x2a[:, :], lgg[:, :], AF.Exp, scale=sccol[:, 0:1], bias=sccol[:, 1:2]),
                reads=[lgg, sccol], writes=[x2a])
            p.step("vector", lambda e, hsg=hsg: e.tensor_add(
                x2a[:, :], x2a[:, :], hsg[:, :]), reads=[x2a, hsg], writes=[x2a])

            def apply(e):
                e.scalar_tensor_tensor(t1f[:, :], x2a[:, :], 1.0, ps[:, :],
                                       ALU.mult, ALU.mult)
                return e.tensor_sub(t2f[:, :], t1f[:, :], pss[:, :])
            p.step("vector", apply, reads=[x2a, ps, pss], writes=[t1f, t2f])
            p.step("scalar", lambda e, g=g: e.activation(
                ret[:, :], t2f[:, :], AF.Identity, scale=lnw[:, g:g + 1], bias=lnb_[:, g:g + 1]),
                reads=[t2f, lnw, lnb_], writes=[ret])
            p.step("vector", lambda e: e.tensor_add(ret[:, :], ret[:, :], xr[:, :]),
                   reads=[ret, xr], writes=[ret])
            p.step("sync", lambda e, g=g: e.dma_start(
                out=_tile(x3_ap, g), in_=ret[:, :]), reads=[ret], dma=True)
        p.emit()


def emit_l1f(nc, S, A, x_ap, ln4b_ap, out_ap):
    emit_ln(nc, S, A, x_ap, out_bf=ln4b_ap)
    with ExitStack() as ctx:
        sb = lambda shape, dt=F32: ctx.enter_context(nc.sbuf_tensor(_nm(), shape, dt))
        lnb = sb([128, NT * TL], BF16)
        wbf = [sb([128, C]) for _ in range(2)]
        awb = sb([128, C], BF16)
        sgb = sb([128, C], BF16)
        sgf = sb([128, C])
        arow = sb([1, 128])
        acol = sb([128, 1])
        bcol = sb([128, 16])
        o128 = sb([128, 128], BF16)
        o11 = sb([1, 1])
        ot = [sb([128, TL]) for _ in range(2)]
        xr = [sb([128, TL]) for _ in range(2)]
        PA = ctx.enter_context(nc.psum_tensor(_nm(), [128, TL], F32))
        PAL = ctx.enter_context(nc.psum_tensor(_nm(), [128, 128], F32))
        PAC = ctx.enter_context(nc.psum_tensor(_nm(), [128, 1], F32))

        p = Prog(nc, S)
        p.step("sync", lambda e: e.dma_start(out=o128[:, :], in_=A["ones128b"]),
               writes=[o128], dma=True)
        p.step("sync", lambda e: e.dma_start(out=bcol[:, :], in_=A["bias_cols"]),
               writes=[bcol], dma=True)
        p.step("gpsimd", lambda e: e.memset(o11[:, :], 1.0), writes=[o11])
        for g in range(NT):
            p.step("sync", lambda e, g=g: e.dma_start(
                out=lnb[:, g * TL:(g + 1) * TL], in_=_tile(ln4b_ap, g)),
                writes=[lnb], dma=True)
        for j in range(NT):
            wbj = wbf[j % 2]
            PX = PA
            otj, xrj = ot[j % 2], xr[j % 2]
            p.step("sync", lambda e, wbj=wbj, j=j: e.dma_start(
                out=wbj[:, :], in_=A["wbP"][j, :, :]), writes=[wbj], dma=True)
            p.step("scalar", lambda e, wbj=wbj: e.activation(awb[:, :], wbj[:, :], AF.Abs),
                   reads=[wbj], writes=[awb])

            def amm(e):
                for ct in range(NT):
                    i = e.matmul(PAL[:, :], o128[:, :], awb[:, ct * 128:(ct + 1) * 128],
                                 start=(ct == 0), stop=(ct == NT - 1))
                return i
            p.step("tensor", amm, reads=[o128, awb], writes=[PAL])
            p.step("vector", lambda e: e.tensor_scalar_mul(arow[0:1, :], PAL[0:1, :], 1.0 / C),
                   reads=[PAL], writes=[arow])
            p.step("tensor", lambda e: e.matmul(PAC[:, :], arow[0:1, :], o11[0:1, :],
                                                start=True, stop=True),
                   reads=[arow, o11], writes=[PAC])
            p.step("vector", lambda e: e.tensor_copy(acol[:, :], PAC[:, :]),
                   reads=[PAC], writes=[acol])

            def sgn(e, wbj=wbj):
                e.tensor_single_scalar(sgf[:, :], wbj[:, :], 0.0, ALU.is_ge)
                return e.tensor_scalar(sgb[:, :], sgf[:, :], 2.0, -1.0, ALU.mult, ALU.add)
            p.step("vector", sgn, reads=[wbj], writes=[sgf, sgb])
            p.step("tensor", lambda e, PX=PX: _mm_ct(e, PX, sgb, lnb),
                   reads=[sgb, lnb], writes=[PX])
            p.step("sync", lambda e, xrj=xrj, j=j: e.dma_start(
                out=xrj[:, :], in_=_tile(x_ap, j)), writes=[xrj], dma=True)
            p.step("scalar", lambda e, PX=PX, otj=otj, j=j: e.activation(
                otj[:, :], PX[:, :], AF.Identity, scale=acol[:, 0:1], bias=bcol[:, j:j + 1]),
                reads=[PX, acol, bcol], writes=[otj])
            p.step("vector", lambda e, otj=otj, xrj=xrj: e.tensor_add(
                otj[:, :], otj[:, :], xrj[:, :]), reads=[otj, xrj], writes=[otj])
            p.step("sync", lambda e, otj=otj, j=j: e.dma_start(
                out=_tile(out_ap, j), in_=otj[:, :]), reads=[otj], dma=True)
        p.emit()


def emit_allgather(nc, S, in_ap, out_ap):
    p = Prog(nc, S)
    p.step("gpsimd", lambda e: e.collective_compute(
        "AllGather", mybir.AluOpType.bypass,
        replica_groups=[list(range(W))],
        ins=[in_ap], outs=[out_ap]),
        reads=[in_ap], writes=[out_ap])
    p.step("gpsimd", lambda e: e.engine_nop(), reads=[out_ap])
    p.emit()


def _mk_nc():
    return bass.Bass("TRN2", target_bir_lowering=False, debug=False)


_SHAPES = {
    "wqkvP": ([48, 128, C], F32), "woutP": ([16, 128, C], F32),
    "wupP": ([64, 128, C], F32), "wgateP": ([64, 128, C], F32),
    "wdownP": ([16, 128, INTER], F32), "wzP": ([16, 128, C], F32),
    "whP": ([16, 128, C], F32), "wbP": ([16, 128, C], F32),
    "decay_row": ([1, H], F32), "su_ug_rows": ([1, C], F32),
    "su_dp_cols": ([128, 64], F32), "wsu_up_cols": ([128, 64], F32),
    "wsu_gt_cols": ([128, 64], F32), "wsd_cols": ([128, 16], F32),
    "bz_cols": ([128, 16], F32), "bh_cols": ([128, 16], F32),
    "lnw_cols": ([128, 16], F32), "lnb_cols": ([128, 16], F32),
    "h0_cols": ([128, 16], F32), "bias_cols": ([128, 16], F32),
    "ones1f": ([1, 128], F32), "ones_row": ([1, C], F32),
    "l0cm": ([128, 24], F32), "l1m": ([128, 8], F32), "l1mneg": ([128, 8], F32),
    "f_col": ([128, 1], F32), "gneg_col": ([128, 1], F32),
    "H128b": ([128, 128], BF16), "TRIb": ([128, 128], BF16),
    "ones128b": ([128, 128], BF16),
}


class LazyAps(dict):
    def __init__(self, nc):
        super().__init__()
        self.nc = nc

    def __missing__(self, k):
        shape, dt = _SHAPES[k]
        ap = self.nc.dram_tensor(k, shape, dt, kind="ExternalInput").ap()
        self[k] = ap
        return ap


def _shared_aps(nc):
    return LazyAps(nc)


_PROG_CACHE = {}


def _build(name):
    if name in _PROG_CACHE:
        return _PROG_CACHE[name]
    nc = _mk_nc()
    A = _shared_aps(nc)
    with ExitStack() as ctx:
        S = Sems(ctx, nc)
        if name == "l0a1":
            x = nc.dram_tensor("xT", [C, TL], F32, kind="ExternalInput").ap()
            ln0 = nc.dram_tensor("ln0b", [C, TL], BF16, kind="ExternalOutput").ap()
            st = nc.dram_tensor("states", [128, 32], F32, kind="ExternalOutput").ap()
            emit_l0a(nc, S, A, x, ln0, st, phase2=False)
        elif name in ("l0a2", "l0a2dbg"):
            x = nc.dram_tensor("xT", [C, TL], F32, kind="ExternalInput").ap()
            ln0 = nc.dram_tensor("ln0b", [C, TL], BF16, kind="ExternalInput").ap()
            sall = nc.dram_tensor("states_all", [W * 128, 32], F32, kind="ExternalInput").ap()
            x1 = nc.dram_tensor("x1T", [C, TL], F32, kind="ExternalOutput").ap()
            yT = nc.dram_tensor("yT", [C, TL], BF16,
                                kind="ExternalOutput" if name == "l0a2dbg" else "Internal").ap()
            emit_l0a(nc, S, A, x, ln0, None, phase2=True, x1_ap=x1, states_all_ap=sall,
                     y_ap=yT)
        elif name in ("l0f", "l0fdbg"):
            x = nc.dram_tensor("xT", [C, TL], F32, kind="ExternalInput").ap()
            x2 = nc.dram_tensor("x2T", [C, TL], F32, kind="ExternalOutput").ap()
            lnsu = nc.dram_tensor("lnsu", [C, TL], F32,
                                  kind="ExternalOutput" if name == "l0fdbg" else "Internal").ap()
            x1h = nc.dram_tensor("x1h", [C, TL], BF16,
                                 kind="ExternalOutput" if name == "l0fdbg" else "Internal").ap()
            x1c_d = (nc.dram_tensor("x1c_dbg", [C, TL], BF16, kind="ExternalOutput").ap()
                     if name == "l0fdbg" else None)
            w2 = nc.dram_tensor("w2", [16, 128, INTER], BF16).ap()
            p6 = nc.dram_tensor("p6", [128, 64 * 512], BF16).ap()
            emit_l0f(nc, S, A, x, x2, lnsu, x1h, w2, p6,
                     x1c_ap=(x1c_d if name == "l0fdbg" else None))
        elif name == "l1a1":
            x = nc.dram_tensor("xT", [C, TL], F32, kind="ExternalInput").ap()
            hs = nc.dram_tensor("hsT", [C, TL], F32, kind="ExternalOutput").ap()
            l = nc.dram_tensor("lT", [C, TL], F32, kind="ExternalOutput").ap()
            ast = nc.dram_tensor("astT", [C, TL], F32, kind="ExternalOutput").ap()
            mz = nc.dram_tensor("mz", [128, 32], F32, kind="ExternalOutput").ap()
            emit_l1a1(nc, S, A, x, hs, l, ast, mz, None)
        elif name == "l1a2":
            l = nc.dram_tensor("lT", [C, TL], F32, kind="ExternalInput").ap()
            ast = nc.dram_tensor("astT", [C, TL], F32, kind="ExternalInput").ap()
            mzall = nc.dram_tensor("mz_all", [W * 128, 32], F32, kind="ExternalInput").ap()
            logh = nc.dram_tensor("loghT", [C, TL], F32, kind="ExternalOutput").ap()
            stats = nc.dram_tensor("stats", [1, 2], F32, kind="ExternalOutput").ap()
            emit_l1a2(nc, S, A, l, ast, mzall, logh, stats)
        elif name == "l1a3":
            x = nc.dram_tensor("xT", [C, TL], F32, kind="ExternalInput").ap()
            hs = nc.dram_tensor("hsT", [C, TL], F32, kind="ExternalInput").ap()
            logh = nc.dram_tensor("loghT", [C, TL], F32, kind="ExternalInput").ap()
            stall = nc.dram_tensor("stats_all", [W, 2], F32, kind="ExternalInput").ap()
            x3 = nc.dram_tensor("x3T", [C, TL], F32, kind="ExternalOutput").ap()
            emit_l1a3(nc, S, A, x, hs, logh, stall, x3)
        elif name == "l1f":
            x = nc.dram_tensor("xT", [C, TL], F32, kind="ExternalInput").ap()
            ln4 = nc.dram_tensor("ln4b", [C, TL], BF16).ap()
            out = nc.dram_tensor("outT", [C, TL], F32, kind="ExternalOutput").ap()
            emit_l1f(nc, S, A, x, ln4, out)
        elif name == "full":
            x = nc.dram_tensor("xT", [C, TL], F32, kind="ExternalInput").ap()
            out = nc.dram_tensor("outT", [C, TL], F32, kind="ExternalOutput").ap()
            dr = lambda nm, shape, dt=F32, sp="Local": nc.dram_tensor(
                nm, shape, dt, addr_space=sp).ap()
            ln0 = dr("ln0b", [C, TL], BF16)
            states = dr("states", [128, 32])
            states_all = dr("states_all", [W * 128, 32], sp="Shared")
            x1 = dr("x1T", [C, TL])
            yT = dr("yT", [C, TL], BF16)
            x2 = dr("x2T", [C, TL])
            lnsu = dr("lnsu", [C, TL])
            x1h = dr("x1h", [C, TL], BF16)
            w2 = dr("w2", [16, 128, INTER], BF16)
            p6 = dr("p6", [128, 64 * 512], BF16)
            hs = dr("hsT", [C, TL])
            l = dr("lT", [C, TL])
            ast = dr("astT", [C, TL])
            mz = dr("mz", [128, 32])
            mz_all = dr("mz_all", [W * 128, 32], sp="Shared")
            logh = dr("loghT", [C, TL])
            stats = dr("stats", [1, 2])
            stats_all = dr("stats_all", [W, 2], sp="Shared")
            x3 = dr("x3T", [C, TL])
            ln4 = dr("ln4b", [C, TL], BF16)
            emit_l0a(nc, S, A, x, ln0, states, phase2=False)
            emit_allgather(nc, S, states, states_all)
            emit_l0a(nc, S, A, x, ln0, None, phase2=True, x1_ap=x1,
                     states_all_ap=states_all, y_ap=yT)
            emit_l0f(nc, S, A, x1, x2, lnsu, x1h, w2, p6)
            emit_l1a1(nc, S, A, x2, hs, l, ast, mz, None)
            emit_allgather(nc, S, mz, mz_all)
            emit_l1a2(nc, S, A, l, ast, mz_all, logh, stats)
            emit_allgather(nc, S, stats, stats_all)
            emit_l1a3(nc, S, A, x2, hs, logh, stats_all, x3)
            emit_l1f(nc, S, A, x3, ln4, out)
        else:
            raise KeyError(name)
    _PROG_CACHE[name] = nc
    return nc


def _run(name, sh, per_core_extra):
    nc = _build(name)
    maps = [{**sh, **per_core_extra[c]} for c in range(W)]
    res = run_bass_kernel_spmd(nc, maps, core_ids=list(range(W)))
    return res.results


FUSED = True


def kernel(**inputs):
    sh, pc = pack_inputs(inputs)
    core = [dict(pc[c]) for c in range(W)]

    if FUSED:
        r = _run("full", sh, core)
        out = np.empty((2, 8192, C), np.float32)
        for c in range(W):
            b, s = c // 4, c % 4
            out[b, s * TL:(s + 1) * TL, :] = np.asarray(r[c]["outT"]).T
        return out

    r = _run("l0a1", sh, core)
    ln0 = [np.asarray(r[c]["ln0b"]) for c in range(W)]
    states_all = np.concatenate([np.asarray(r[c]["states"]) for c in range(W)], axis=0)
    for c in range(W):
        core[c]["ln0b"] = ln0[c]
        core[c]["states_all"] = states_all

    r = _run("l0a2", sh, core)
    for c in range(W):
        core[c]["xT"] = np.asarray(r[c]["x1T"])

    r = _run("l0f", sh, core)
    for c in range(W):
        core[c]["xT"] = np.asarray(r[c]["x2T"])

    r = _run("l1a1", sh, core)
    mz_all = np.concatenate([np.asarray(r[c]["mz"]) for c in range(W)], axis=0)
    for c in range(W):
        core[c]["hsT"] = np.asarray(r[c]["hsT"])
        core[c]["lT"] = np.asarray(r[c]["lT"])
        core[c]["astT"] = np.asarray(r[c]["astT"])
        core[c]["mz_all"] = mz_all

    r = _run("l1a2", sh, core)
    stats_all = np.concatenate([np.asarray(r[c]["stats"]) for c in range(W)], axis=0)
    for c in range(W):
        core[c]["loghT"] = np.asarray(r[c]["loghT"])
        core[c]["stats_all"] = stats_all

    r = _run("l1a3", sh, core)
    for c in range(W):
        core[c]["xT"] = np.asarray(r[c]["x3T"])

    r = _run("l1f", sh, core)
    out = np.empty((2, 8192, C), np.float32)
    for c in range(W):
        b, s = c // 4, c % 4
        out[b, s * TL:(s + 1) * TL, :] = np.asarray(r[c]["outT"]).T
    return out


# version 12 (final)
# speedup vs baseline: 1.1660x; 1.0161x over previous
import math
import numpy as np
import ml_dtypes
from contextlib import ExitStack

import concourse.bass as bass
import concourse.mybir as mybir
from concourse.bass_utils import run_bass_kernel_spmd

F32 = mybir.dt.float32
BF16 = mybir.dt.bfloat16
U8 = mybir.dt.uint8
AF = mybir.ActivationFunctionType
ALU = mybir.AluOpType

C = 2048
TL = 2048
NT = 16
H = 16
W = 8
INTER = 8192
NEG = -1.0e30
EPS_LN = 1e-5
R2048 = 1.0 / math.sqrt(2048.0)
R8192 = 1.0 / math.sqrt(8192.0)
NBF = ml_dtypes.bfloat16

ENGINES = ("sync", "gpsimd", "tensor", "scalar", "vector")

_NMC = [0]


def _nm():
    _NMC[0] += 1
    return "t%d" % _NMC[0]


def _butterfly_plan(n, dists):
    slot = list(range(n))
    spare = n
    steps = []
    for dist in dists:
        stage = []
        for base in range(0, n, 2 * dist):
            for a in range(base, base + dist):
                b = a + dist
                pa, pb = slot[a], slot[b]
                stage.append((pa, pb, spare))
                slot[b] = spare
                spare = pb
        steps.append(stage)
    return steps, slot


def _pack_ktiles(w):
    K, M = w.shape
    nk, nj = K // 128, M // 128
    p = w.reshape(nk, 128, nj, 128).transpose(2, 1, 0, 3).reshape(nj, 128, K)
    return np.ascontiguousarray(p)


def _cols(v):
    return np.ascontiguousarray(v.reshape(-1, 128).T)


def _hadamard(n):
    Hm = np.array([[1.0]], dtype=np.float32)
    while Hm.shape[0] < n:
        Hm = np.block([[Hm, Hm], [Hm, -Hm]])
    return np.ascontiguousarray(Hm.astype(np.float32))


def pack_inputs(inp):
    f32 = np.float32
    sh = {}
    sh["wqkvP"] = _pack_ktiles(np.asarray(inp["L0_attn_w_qkv"], f32))
    sh["woutP"] = _pack_ktiles(np.asarray(inp["L0_attn_w_out"], f32))
    sh["wupP"] = _pack_ktiles(np.ascontiguousarray(np.asarray(inp["L0_ffn_W_up"], f32).T))
    sh["wgateP"] = _pack_ktiles(np.ascontiguousarray(np.asarray(inp["L0_ffn_W_gate"], f32).T))
    wdT = np.ascontiguousarray(np.asarray(inp["L0_ffn_W_down"], f32).T)
    sh["wdownP"] = _pack_ktiles(wdT)
    sh["wzP"] = _pack_ktiles(np.ascontiguousarray(np.asarray(inp["L1_attn_W_z"], f32).T))
    sh["whP"] = _pack_ktiles(np.ascontiguousarray(np.asarray(inp["L1_attn_W_h"], f32).T))
    sh["wbP"] = _pack_ktiles(np.ascontiguousarray(np.asarray(inp["L1_ffn_weight"], f32).T))

    sh["decay_row"] = np.ascontiguousarray(np.asarray(inp["L0_attn_decay_param"], f32).reshape(1, H))
    sh["su_ug_rows"] = np.ascontiguousarray(np.asarray(inp["L0_ffn_SU_ug"], f32).reshape(1, C))
    sh["su_dp_cols"] = _cols(np.asarray(inp["L0_ffn_SU_dp"], f32))
    sh["wsu_up_cols"] = _cols(np.asarray(inp["L0_ffn_Wscale_ug"], f32)[:INTER])
    sh["wsu_gt_cols"] = _cols(np.asarray(inp["L0_ffn_Wscale_ug"], f32)[INTER:])
    sh["wsd_cols"] = _cols(np.asarray(inp["L0_ffn_Wscale_dp"], f32))
    sh["bz_cols"] = _cols(np.asarray(inp["L1_attn_b_z"], f32))
    sh["bh_cols"] = _cols(np.asarray(inp["L1_attn_b_h"], f32))
    sh["lnw_cols"] = _cols(np.asarray(inp["L1_attn_ln_w"], f32))
    sh["lnb_cols"] = _cols(np.asarray(inp["L1_attn_ln_b"], f32))
    sh["h0_cols"] = _cols(np.asarray(inp["L1_attn_h0"], f32))
    sh["bias_cols"] = _cols(np.asarray(inp["L1_ffn_bias"], f32))

    sh["H128b"] = _hadamard(128).astype(NBF)
    sh["TRIb"] = np.ascontiguousarray(np.triu(np.ones((128, 128), f32)).astype(NBF))
    sh["ones1f"] = np.ones((1, 128), f32)
    sh["ones128b"] = np.ones((128, 128), NBF)
    sh["ones_row"] = np.ones((1, C), f32)

    x = np.asarray(inp["L0_attn_x"], f32)
    pc = []
    for c in range(W):
        b, s = c // 4, c % 4
        d = {}
        d["xT"] = np.ascontiguousarray(x[b, s * TL:(s + 1) * TL, :].T)
        cm = np.zeros((3, W), f32)
        m = np.zeros((W,), f32)
        for j in range(W):
            bj, sj = j // 4, j % 4
            if bj == b and sj < s:
                m[j] = 1.0
                g = s - 1 - sj
                if g < 3:
                    cm[g, j] = 1.0
        d["l0cm"] = np.ascontiguousarray(np.broadcast_to(cm.T.reshape(1, W * 3), (128, W * 3)))
        d["l1m"] = np.ascontiguousarray(np.broadcast_to(m.reshape(1, W), (128, W)))
        d["l1mneg"] = np.ascontiguousarray(np.broadcast_to(((m - 1.0) * 1e30).reshape(1, W), (128, W)))
        first = 1.0 if s == 0 else 0.0
        d["f_col"] = np.full((128, 1), first, f32)
        d["gneg_col"] = np.full((128, 1), (first - 1.0) * 1e30, f32)
        pc.append(d)
    return sh, pc


class Sems:
    def __init__(self, ctx, nc):
        self.nc = nc
        self.ctx = ctx
        self.sem = {e: ctx.enter_context(nc.semaphore(name="sem_" + e)) for e in ENGINES}
        self.cnt = {e: 0 for e in ENGINES}
        self.dpool = []
        self.dcnt = []

    def dma_pool(self, n):
        while len(self.dpool) < n:
            self.dpool.append(self.ctx.enter_context(
                self.nc.semaphore(name="dma%d" % len(self.dpool))))
            self.dcnt.append(0)
        return self.dpool


class Prog:

    def __init__(self, nc, S):
        self.nc = nc
        self.S = S
        self.steps = []
        self.last_writer = {}
        self.readers = {}

    def step(self, engine, fn, reads=(), writes=(), dma=False, fence=False,
             hazard=False):
        idx = len(self.steps)
        deps = set()
        for r in reads:
            k = id(r)
            if k in self.last_writer:
                deps.add(self.last_writer[k])
        for w in writes:
            k = id(w)
            if k in self.last_writer:
                deps.add(self.last_writer[k])
            for rd in self.readers.get(k, ()):
                deps.add(rd)
        for w in writes:
            k = id(w)
            self.last_writer[k] = idx
            self.readers[k] = []
        for r in reads:
            self.readers.setdefault(id(r), []).append(idx)
        if dma:
            key = ("d", id(writes[0]) if writes else id(reads[0]))
        else:
            key = engine
        small = False
        for w_ in writes:
            try:
                f = int(w_.shape[-1])
            except Exception:
                f = 1 << 20
            if f < 256:
                small = True
        self.steps.append([engine, fn, 16 if dma else 1, sorted(deps), key,
                           hazard or small])

    def emit(self):
        S = self.S
        steps = self.steps
        slot = {}
        for st in steps:
            if not isinstance(st[4], str) and st[4] not in slot:
                slot[st[4]] = len(slot)
        pool = S.dma_pool(len(slot))
        cnt = dict(S.cnt)
        for k, s_ in slot.items():
            cnt[k] = S.dcnt[s_]
        after = []
        for st in steps:
            cnt[st[4]] += st[2]
            after.append(cnt[st[4]])

        def handle(key):
            return S.sem[key] if isinstance(key, str) else pool[slot[key]]

        waits = []
        marks = {}
        for i, (e, fn, inc, deps, key, _hz) in enumerate(steps):
            wl = {}
            for d in deps:
                kd = steps[d][4]
                v = after[d]
                if kd == e:
                    if steps[d][5]:
                        wl[kd] = max(wl.get(kd, 0), v)
                    continue
                m = marks.get((e, kd))
                if m is None:
                    m = S.cnt[kd] if isinstance(kd, str) else S.dcnt[slot[kd]]
                if v > m:
                    wl[kd] = max(wl.get(kd, 0), v)
            for kd, v in wl.items():
                marks[(e, kd)] = max(marks.get((e, kd), 0), v)
            waits.append(wl)

        dkeys = {}
        for i, st in enumerate(steps):
            if not isinstance(st[4], str):
                dkeys[st[4]] = after[i]
        qeng = "sync" if any(st[0] == "sync" for st in steps) else (
            "gpsimd" if any(st[0] == "gpsimd" for st in steps) else None)

        def mkbody(mine, eng_name):
            def body(eng):
                for i in mine:
                    fn, inc, key = steps[i][1], steps[i][2], steps[i][4]
                    for kd, v in sorted(waits[i].items(), key=lambda kv: str(kv[0])):
                        eng.wait_ge(handle(kd), v)
                    fn(eng).then_inc(handle(key), inc)
                if eng_name == qeng:
                    for kd, v in sorted(dkeys.items(), key=lambda kv: str(kv[0])):
                        eng.wait_ge(handle(kd), v)
            return body

        with self.nc.Block() as blk:
            for eng_name in ENGINES:
                mine = [i for i, st in enumerate(steps) if st[0] == eng_name]
                if not mine:
                    continue
                getattr(blk, eng_name)(mkbody(mine, eng_name))
        for st in steps:
            if isinstance(st[4], str):
                S.cnt[st[4]] += st[2]
        for k, s_ in slot.items():
            S.dcnt[s_] = dkeys[k]


def _tile(ap, g, t0=0, t1=TL):
    return ap[g * 128:(g + 1) * 128, t0:t1]


def emit_ln(nc, S, A, x_ap, out_bf=None, out_f32=None, su=False):
    with ExitStack() as ctx:
        sb = lambda shape, dt=F32: ctx.enter_context(nc.sbuf_tensor(_nm(), shape, dt))
        xs = [sb([128, TL]) for _ in range(2)]
        xb = [sb([128, TL], BF16) for _ in range(2)]
        sq = [sb([128, TL], BF16) for _ in range(2)]
        rows = sb([1, 6 * C])
        t1f = sb([128, TL])
        t2f = [sb([128, TL]) for _ in range(2)]
        t2b = [sb([128, TL], BF16) for _ in range(2)]
        o128 = sb([128, 128], BF16)
        onesr = sb([1, C])
        ps = ctx.enter_context(nc.psum_tensor(_nm(), [128, TL], F32))
        pss = ctx.enter_context(nc.psum_tensor(_nm(), [128, TL], F32))

        p = Prog(nc, S)
        p.step("sync", lambda e: e.dma_start(out=o128[:, :], in_=A["ones128b"]),
               writes=[o128], dma=True)
        srcrow = A["su_ug_rows"] if su else A["ones_row"]
        p.step("sync", lambda e, s=srcrow: e.dma_start(out=onesr[0:1, :], in_=s),
               writes=[onesr], dma=True)
        for g in range(NT):
            xg, xbg, sqg = xs[g % 2], xb[g % 2], sq[g % 2]
            p.step("sync", lambda e, xg=xg, g=g: e.dma_start(out=xg[:, :], in_=_tile(x_ap, g)),
                   writes=[xg], dma=True)
            p.step("gpsimd", lambda e, xg=xg, xbg=xbg: e.tensor_copy(xbg[:, :], xg[:, :]),
                   reads=[xg], writes=[xbg])
            p.step("scalar", lambda e, xg=xg, sqg=sqg: e.activation(sqg[:, :], xg[:, :], AF.Square),
                   reads=[xg], writes=[sqg])

            def mms(e, g=g, xbg=xbg, sqg=sqg):
                for n in range(4):
                    e.matmul(ps[:, n * 512:(n + 1) * 512], o128[:, :],
                             xbg[:, n * 512:(n + 1) * 512],
                             start=(g == 0), stop=(g == NT - 1))
                for n in range(4):
                    i = e.matmul(pss[:, n * 512:(n + 1) * 512], o128[:, :],
                                 sqg[:, n * 512:(n + 1) * 512],
                                 start=(g == 0), stop=(g == NT - 1))
                return i
            p.step("tensor", mms, reads=[o128, xbg, sqg], writes=[ps, pss])
        p.emit()

        p = Prog(nc, S)

        def rows1(e):
            e.tensor_scalar_mul(rows[0:1, 0:C], ps[0:1, :], 1.0 / C)
            e.tensor_scalar_mul(rows[0:1, C:2 * C], pss[0:1, :], 1.0 / C)
            e.tensor_mul(rows[0:1, 4 * C:5 * C], rows[0:1, 0:C], rows[0:1, 0:C])
            e.tensor_sub(rows[0:1, C:2 * C], rows[0:1, C:2 * C], rows[0:1, 4 * C:5 * C])
            return e.tensor_scalar_add(rows[0:1, 5 * C:6 * C], rows[0:1, C:2 * C], EPS_LN)
        p.step("vector", rows1, reads=[ps, pss], writes=[rows])

        def rows2(e):
            e.activation(rows[0:1, 5 * C:6 * C], rows[0:1, 5 * C:6 * C], AF.Ln)
            return e.activation(rows[0:1, 2 * C:3 * C], rows[0:1, 5 * C:6 * C],
                                AF.Exp, scale=-0.5)
        p.step("scalar", rows2, reads=[rows], writes=[rows])
        p.step("vector", lambda e: e.tensor_mul(rows[0:1, 3 * C:4 * C],
                                                rows[0:1, 0:C], rows[0:1, 2 * C:3 * C]),
               reads=[rows], writes=[rows])

        ngrp = NT if su else 1
        for g in range(NT):
            xg = xs[g % 2]
            of = t2f[g % 2]
            ob = t2b[g % 2]
            p.step("sync", lambda e, xg=xg, g=g: e.dma_start(out=xg[:, :], in_=_tile(x_ap, g)),
                   writes=[xg], dma=True)
            if g < ngrp:
                def bcast(e, g=g):
                    for n in range(4):
                        e.matmul(ps[:, n * 512:(n + 1) * 512],
                                 onesr[0:1, g * 128:(g + 1) * 128],
                                 rows[0:1, 2 * C + n * 512:2 * C + (n + 1) * 512],
                                 start=True, stop=True)
                    for n in range(4):
                        i = e.matmul(pss[:, n * 512:(n + 1) * 512],
                                     onesr[0:1, g * 128:(g + 1) * 128],
                                     rows[0:1, 3 * C + n * 512:3 * C + (n + 1) * 512],
                                     start=True, stop=True)
                    return i
                p.step("tensor", bcast, reads=[onesr, rows], writes=[ps, pss])

            def apply(e, xg=xg, of=of):
                e.scalar_tensor_tensor(t1f[:, :], xg[:, :], 1.0, ps[:, :],
                                       ALU.mult, ALU.mult)
                return e.tensor_sub(of[:, :], t1f[:, :], pss[:, :])
            p.step("vector", apply, reads=[xg, ps, pss], writes=[t1f, of])
            if out_bf is not None:
                p.step("vector", lambda e, of=of, ob=ob: e.tensor_copy(ob[:, :], of[:, :]),
                       reads=[of], writes=[ob])
                p.step("gpsimd", lambda e, ob=ob, g=g: e.dma_start(out=_tile(out_bf, g), in_=ob[:, :]),
                       reads=[ob], dma=True)
            if out_f32 is not None:
                p.step("gpsimd", lambda e, of=of, g=g: e.dma_start(out=_tile(out_f32, g), in_=of[:, :]),
                       reads=[of], dma=True)
        p.emit()


def _sigmoid_rows(p, drow, drow2, drow3):
    p.step("scalar", lambda e: e.activation(drow[0:1, 16:32], drow[0:1, 0:16],
                                            AF.Exp, scale=-1.0),
           reads=[drow], writes=[drow])
    p.step("vector", lambda e: e.tensor_scalar_add(drow3[0:1, 0:16],
                                                   drow[0:1, 16:32], 1.0),
           reads=[drow], writes=[drow3])
    p.step("vector", lambda e: e.reciprocal(drow2[0:1, 0:16], drow3[0:1, 0:16]),
           reads=[drow3], writes=[drow2], fence=True, hazard=True)


def _phi(p, psum, e_scr, out):
    p.step("vector", lambda e: e.tensor_scalar_min(e_scr[:, :], psum[:, :], 0.0),
           reads=[psum], writes=[e_scr])
    p.step("scalar", lambda e: e.activation(e_scr[:, :], e_scr[:, :], AF.Exp),
           reads=[e_scr], writes=[e_scr])
    p.step("vector", lambda e: e.scalar_tensor_tensor(out[:, :], psum[:, :], 1.0,
                                                      e_scr[:, :], ALU.add, ALU.max),
           reads=[psum, e_scr], writes=[out])


def _mm_ct(e, psum, wtile, lnb, nts=NT, tl=TL, t0=0, tw=None, row=None):
    tw = tl if tw is None else tw
    row = tl if row is None else row
    nch = tw // 512
    for ct in range(nts):
        for n in range(nch):
            i = e.matmul(psum[:, n * 512:(n + 1) * 512],
                         wtile[:, ct * 128:(ct + 1) * 128],
                         lnb[:, ct * row + t0 + n * 512:ct * row + t0 + (n + 1) * 512],
                         start=(ct == 0), stop=(ct == nts - 1))
    return i


def emit_l0a(nc, S, A, x_ap, ln0b_ap, states_ap, phase2, x1_ap=None,
             states_all_ap=None, y_ap=None):
    if not phase2:
        emit_ln(nc, S, A, x_ap, out_bf=ln0b_ap)
    with ExitStack() as ctx:
        sb = lambda shape, dt=F32: ctx.enter_context(nc.sbuf_tensor(_nm(), shape, dt))
        lnb = sb([128, NT * TL], BF16)
        wk = sb([128, C], BF16)
        wv = sb([128, C], BF16)
        wq = sb([128, C], BF16)
        TH2_ = TL // 2
        kf2 = [sb([128, TH2_]) for _ in range(2)]
        qf2 = [sb([128, TH2_]) for _ in range(2)]
        kv2 = [sb([128, TH2_]) for _ in range(2)]
        es2 = [sb([128, TH2_]) for _ in range(2)]
        sckv2 = [sb([128, TH2_]) for _ in range(2)]
        scks2 = [sb([128, TH2_]) for _ in range(2)]
        qks2 = [sb([128, TH2_], BF16) for _ in range(2)]
        ys2 = [sb([128, TH2_], BF16) for _ in range(2)]
        drow = sb([1, 64])
        drow2 = sb([1, 16])
        drow3 = sb([1, 16])
        denr = sb([1, TL])
        denr2 = sb([1, TL])
        dcols = sb([128, 16])
        stc = sb([128, 32])
        o1f = sb([1, 128])
        o128 = sb([128, 128], BF16)
        PS4 = [ctx.enter_context(nc.psum_tensor(_nm(), [128, TH2_], F32))
               for _ in range(4)]
        PA = PS4[0]
        PQ2 = [PS4[0], PS4[2]]

        p = Prog(nc, S)
        for g in range(NT):
            p.step("sync", lambda e, g=g: e.dma_start(
                out=lnb[:, g * TL:(g + 1) * TL], in_=_tile(ln0b_ap, g)),
                writes=[lnb], dma=True)
        p.step("sync", lambda e: e.dma_start(out=drow[0:1, 0:16], in_=A["decay_row"]),
               writes=[drow], dma=True)
        p.step("sync", lambda e: e.dma_start(out=o1f[0:1, :], in_=A["ones1f"]),
               writes=[o1f], dma=True)
        p.step("sync", lambda e: e.dma_start(out=o128[:, :], in_=A["ones128b"]),
               writes=[o128], dma=True)
        _sigmoid_rows(p, drow, drow2, drow3)
        p.step("tensor", lambda e: e.matmul(PA[:, 0:16], o1f[0:1, :], drow2[0:1, 0:16],
                                            start=True, stop=True),
               reads=[o1f, drow2], writes=[PA])
        p.step("vector", lambda e: e.tensor_copy(dcols[:, :], PA[:, 0:16]),
               reads=[PA], writes=[dcols])

        if phase2:
            sall = sb([128, 8 * 32])
            cmks = sb([128, 24])
            dLc = sb([128, 16 * 4])
            Pkv = sb([128, 16])
            Pks = sb([128, 16])
            coef = sb([128, 16])
            ys = sb([128, TL], BF16)
            p.step("sync", lambda e: e.dma_start(
                out=sall[:, :], in_=states_all_ap.rearrange("(j p) s -> p j s", p=128)),
                writes=[sall], dma=True)
            p.step("sync", lambda e: e.dma_start(out=cmks[:, :], in_=A["l0cm"]),
                   writes=[cmks], dma=True)
            p.step("scalar", lambda e: e.activation(dLc[:, 16:32], dcols[:, :], AF.Ln),
                   reads=[dcols], writes=[dLc])

            p.step("scalar", lambda e: e.activation(dLc[:, 32:48], dLc[:, 16:32],
                                                    AF.Exp, scale=4096.0),
                   reads=[dLc], writes=[dLc])
            p.step("scalar", lambda e: e.activation(dLc[:, 16:32], dLc[:, 16:32],
                                                    AF.Exp, scale=2048.0),
                   reads=[dLc], writes=[dLc])
            p.step("gpsimd", lambda e: e.memset(dLc[:, 0:16], 1.0), writes=[dLc])
            p.step("gpsimd", lambda e: e.memset(Pkv[:, :], 0.0), writes=[Pkv])
            p.step("gpsimd", lambda e: e.memset(Pks[:, :], 0.0), writes=[Pks])
            for j in range(W):
                fr = [dLc, cmks, sall, Pkv, Pks, coef]
                fw = [coef, dLc, Pkv, Pks]
                for fs in [
                    lambda e, j=j: e.tensor_scalar(coef[:, :], dLc[:, 16:32],
                                                   cmks[:, 3 * j + 1:3 * j + 2], None, ALU.mult),
                    lambda e, j=j: e.tensor_scalar(dLc[:, 48:64], dLc[:, 32:48],
                                                   cmks[:, 3 * j + 2:3 * j + 3], None, ALU.mult),
                    lambda e: e.tensor_add(coef[:, :], coef[:, :], dLc[:, 48:64]),
                    lambda e, j=j: e.tensor_scalar(dLc[:, 48:64], dLc[:, 0:16],
                                                   cmks[:, 3 * j:3 * j + 1], None, ALU.mult),
                    lambda e: e.tensor_add(coef[:, :], coef[:, :], dLc[:, 48:64]),
                    lambda e, j=j: e.tensor_mul(dLc[:, 48:64], coef[:, :],
                                                sall[:, 32 * j:32 * j + 16]),
                    lambda e: e.tensor_add(Pkv[:, :], Pkv[:, :], dLc[:, 48:64]),
                    lambda e, j=j: e.tensor_mul(dLc[:, 48:64], coef[:, :],
                                                sall[:, 32 * j + 16:32 * j + 32]),
                    lambda e: e.tensor_add(Pks[:, :], Pks[:, :], dLc[:, 48:64]),
                ]:
                    p.step("vector", fs, reads=fr, writes=fw)

        TH2 = TL // 2
        for h in range(H):
            p.step("gpsimd", lambda e, h=h: e.dma_start(
                out=wk[:, :], in_=A["wqkvP"][16 + h, :, :]), writes=[wk], dma=True)
            p.step("gpsimd", lambda e, h=h: e.dma_start(
                out=wv[:, :], in_=A["wqkvP"][32 + h, :, :]), writes=[wv], dma=True)
            if phase2:
                p.step("gpsimd", lambda e, h=h: e.dma_start(
                    out=wq[:, :], in_=A["wqkvP"][h, :, :]), writes=[wq], dma=True)
            for hf in range(2):
                t0 = hf * TH2
                PK, PV = PS4[2 * hf], PS4[2 * hf + 1]
                kfh = kf2[hf]
                esh = es2[hf]
                kvh = kv2[hf]
                sckvh, scksh = sckv2[hf], scks2[hf]
                p.step("tensor", lambda e, PK=PK, t0=t0: _mm_ct(
                    e, PK, wk, lnb, t0=t0, tw=TH2), reads=[wk, lnb], writes=[PK])
                p.step("tensor", lambda e, PV=PV, t0=t0: _mm_ct(
                    e, PV, wv, lnb, t0=t0, tw=TH2), reads=[wv, lnb], writes=[PV])
                p.step("vector", lambda e, PK=PK, esh=esh: e.tensor_scalar_min(
                    esh[:, :], PK[:, :], 0.0), reads=[PK], writes=[esh])
                p.step("scalar", lambda e, esh=esh: e.activation(esh[:, :], esh[:, :], AF.Exp),
                       reads=[esh], writes=[esh])
                p.step("vector", lambda e, PK=PK, esh=esh, kfh=kfh: e.scalar_tensor_tensor(
                    kfh[:, :], PK[:, :], 1.0, esh[:, :], ALU.add, ALU.max),
                    reads=[PK, esh], writes=[kfh])
                p.step("vector", lambda e, kfh=kfh, PV=PV, kvh=kvh: e.tensor_mul(
                    kvh[:, :], kfh[:, :], PV[:, :]), reads=[kfh, PV], writes=[kvh])
                p.step("gpsimd", lambda e, esh=esh: e.memset(esh[:, :], 1.0), writes=[esh])
                p.step("vector", lambda e, h=h, esh=esh: e.tensor_scalar(
                    esh[:, :], esh[:, :], dcols[:, h:h + 1], None, ALU.mult),
                    reads=[esh, dcols], writes=[esh])
                if phase2:
                    PQ = PQ2[hf]
                    qfh = qf2[hf]
                    p.step("tensor", lambda e, PQ=PQ, t0=t0: _mm_ct(
                        e, PQ, wq, lnb, t0=t0, tw=TH2), reads=[wq, lnb], writes=[PQ])
                    init_kv = Pkv[:, h:h + 1] if hf == 0 else sckv2[0][:, TH2 - 1:TH2]
                    init_ks = Pks[:, h:h + 1] if hf == 0 else scks2[0][:, TH2 - 1:TH2]
                    rd_kv = [esh, kvh, Pkv] + ([sckv2[0]] if hf else [])
                    rd_ks = [esh, kfh, Pks] + ([scks2[0]] if hf else [])
                    p.step("vector", lambda e, esh=esh, kvh=kvh, sckvh=sckvh,
                           init_kv=init_kv: e.tensor_tensor_scan(
                        sckvh[:, :], esh[:, :], kvh[:, :], init_kv, ALU.mult, ALU.add),
                        reads=rd_kv, writes=[sckvh], fence=True, hazard=True)
                    p.step("vector", lambda e, esh=esh, kfh=kfh, scksh=scksh,
                           init_ks=init_ks: e.tensor_tensor_scan(
                        scksh[:, :], esh[:, :], kfh[:, :], init_ks, ALU.mult, ALU.add),
                        reads=rd_ks, writes=[scksh], fence=True, hazard=True)
                    p.step("vector", lambda e, PQ=PQ, esh=esh: e.tensor_scalar_min(
                        esh[:, :], PQ[:, :], 0.0), reads=[PQ], writes=[esh])
                    p.step("scalar", lambda e, esh=esh: e.activation(
                        esh[:, :], esh[:, :], AF.Exp), reads=[esh], writes=[esh])
                    p.step("vector", lambda e, PQ=PQ, esh=esh, qfh=qfh: e.scalar_tensor_tensor(
                        qfh[:, :], PQ[:, :], 1.0, esh[:, :], ALU.add, ALU.max),
                        reads=[PQ, esh], writes=[qfh])
                    qksh = qks2[hf]
                    p.step("vector", lambda e, qfh=qfh, scksh=scksh, qksh=qksh: e.tensor_mul(
                        qksh[:, :], qfh[:, :], scksh[:, :]),
                        reads=[qfh, scksh], writes=[qksh])
                    PD = PQ

                    def denmm(e, qksh=qksh, PD=PD):
                        for n in range(TH2 // 512):
                            i = e.matmul(PD[:, n * 512:(n + 1) * 512], o128[:, :],
                                         qksh[:, n * 512:(n + 1) * 512],
                                         start=True, stop=True)
                        return i
                    p.step("tensor", denmm, reads=[o128, qksh], writes=[PD])
                    dr1, dr2 = denr, denr2

                    def denrow(e, PD=PD, hf=hf):
                        e.tensor_scalar_max(dr1[0:1, hf * TH2:(hf + 1) * TH2], PD[0:1, :], 1e-6)
                        return e.reciprocal(dr2[0:1, hf * TH2:(hf + 1) * TH2],
                                            dr1[0:1, hf * TH2:(hf + 1) * TH2])
                    p.step("vector", denrow, reads=[PD], writes=[denr, denr2],
                           fence=True, hazard=True)

                    def denbc(e, PD=PD, hf=hf):
                        for n in range(TH2 // 512):
                            i = e.matmul(PD[:, n * 512:(n + 1) * 512], o1f[0:1, :],
                                         denr2[0:1, hf * TH2 + n * 512:hf * TH2 + (n + 1) * 512],
                                         start=True, stop=True)
                        return i
                    p.step("tensor", denbc, reads=[o1f, denr2], writes=[PD])
                    p.step("vector", lambda e, qfh=qfh, sckvh=sckvh, scksh=scksh: e.tensor_mul(
                        scksh[:, :], qfh[:, :], sckvh[:, :]),
                        reads=[qfh, sckvh], writes=[scksh])
                    ysh = ys2[hf]
                    p.step("vector", lambda e, scksh=scksh, PD=PD, ysh=ysh: e.tensor_mul(
                        ysh[:, :], scksh[:, :], PD[:, :]), reads=[scksh, PD], writes=[ysh])
                    p.step("sync", lambda e, h=h, t0=t0, ysh=ysh: e.dma_start(
                        out=_tile(y_ap, h, t0, t0 + TH2), in_=ysh[:, :]),
                        reads=[ysh, y_ap], dma=True)
                else:
                    init_kv = 0.0 if hf == 0 else sckv2[0][:, TH2 - 1:TH2]
                    init_ks = 0.0 if hf == 0 else scks2[0][:, TH2 - 1:TH2]
                    rd_kv = [esh, kvh] + ([sckv2[0]] if hf else [])
                    rd_ks = [esh, kfh] + ([scks2[0]] if hf else [])
                    p.step("vector", lambda e, esh=esh, kvh=kvh, sckvh=sckvh,
                           init_kv=init_kv: e.tensor_tensor_scan(
                        sckvh[:, :], esh[:, :], kvh[:, :], init_kv, ALU.mult, ALU.add),
                        reads=rd_kv, writes=[sckvh], fence=True, hazard=True)
                    p.step("vector", lambda e, esh=esh, kfh=kfh, scksh=scksh,
                           init_ks=init_ks: e.tensor_tensor_scan(
                        scksh[:, :], esh[:, :], kfh[:, :], init_ks, ALU.mult, ALU.add),
                        reads=rd_ks, writes=[scksh], fence=True, hazard=True)
                    if hf == 1:
                        p.step("vector", lambda e, h=h, sckvh=sckvh: e.tensor_copy(
                            stc[:, h:h + 1], sckvh[:, TH2 - 1:TH2]),
                            reads=[sckvh], writes=[stc])
                        p.step("vector", lambda e, h=h, scksh=scksh: e.tensor_copy(
                            stc[:, 16 + h:16 + h + 1], scksh[:, TH2 - 1:TH2]),
                            reads=[scksh], writes=[stc])

        if not phase2:
            p.step("sync", lambda e: e.dma_start(out=states_ap, in_=stc[:, :]),
                   reads=[stc], dma=True)
            p.emit()
        else:
            p.emit()
            ybs = lnb
            p = Prog(nc, S)
            wo = [wk, wv]
            for g in range(NT):
                p.step("sync", lambda e, g=g: e.dma_start(
                    out=ybs[:, g * TL:(g + 1) * TL], in_=_tile(y_ap, g)),
                    reads=[y_ap], writes=[ybs], dma=True)
            for j in range(NT):
                woj = wo[j % 2]
                p.step("gpsimd", lambda e, j=j, woj=woj: e.dma_start(
                    out=woj[:, :], in_=A["woutP"][j, :, :]), writes=[woj], dma=True)
                for hf in range(2):
                    t0 = hf * TH2
                    PX = PS4[(2 * j + hf) % 4]
                    xr = kf2[hf]
                    ot = kv2[hf]
                    p.step("sync", lambda e, j=j, t0=t0, xr=xr: e.dma_start(
                        out=xr[:, :], in_=_tile(x_ap, j, t0, t0 + TH2)),
                        writes=[xr], dma=True)
                    p.step("tensor", lambda e, PX=PX, t0=t0, woj=woj: _mm_ct(
                        e, PX, woj, ybs, t0=t0, tw=TH2),
                        reads=[woj, ybs], writes=[PX])
                    p.step("vector", lambda e, PX=PX, xr=xr, ot=ot: e.tensor_add(
                        ot[:, :], PX[:, :], xr[:, :]), reads=[PX, xr], writes=[ot])
                    p.step("sync", lambda e, j=j, t0=t0, ot=ot: e.dma_start(
                        out=_tile(x1_ap, j, t0, t0 + TH2), in_=ot[:, :]),
                        reads=[ot], dma=True)
            p.emit()


def _mm_den(e, psum, o128, qks):
    for ct in range(1):
        pass
    for n in range(4):
        i = e.matmul(psum[:, n * 512:(n + 1) * 512], o128[:, :],
                     qks[:, n * 512:(n + 1) * 512], start=True, stop=True)
    return i


def emit_l0f(nc, S, A, x_ap, x2_ap, lnsu_ap, x1h_ap, w2_ap, p6_ap, x1c_ap=None):
    emit_ln(nc, S, A, x_ap, out_f32=lnsu_ap, su=True)

    with ExitStack() as ctx:
        sb = lambda shape, dt=F32: ctx.enter_context(nc.sbuf_tensor(_nm(), shape, dt))
        h128 = sb([128, 128], BF16)
        wdb = [sb([128, INTER], BF16) for _ in range(2)]
        w2s = [sb([128, INTER], BF16) for _ in range(2)]
        PW = ctx.enter_context(nc.psum_tensor(_nm(), [128, 2048], F32))
        p = Prog(nc, S)
        p.step("sync", lambda e: e.dma_start(out=h128[:, :], in_=A["H128b"]),
               writes=[h128], dma=True)
        for j in range(NT):
            wdj, w2j = wdb[j % 2], w2s[j % 2]
            p.step("gpsimd", lambda e, wdj=wdj, j=j: e.dma_start(
                out=wdj[:, :], in_=A["wdownP"][j, :, :]), writes=[wdj], dma=True)
            for q in range(4):
                def fold(e, wdj=wdj, q=q):
                    for s in range(16):
                        i = e.matmul(PW[:, s * 128:(s + 1) * 128], h128[:, :],
                                     wdj[:, (q * 16 + s) * 128:(q * 16 + s + 1) * 128],
                                     start=True, stop=True)
                    return i
                p.step("tensor", fold, reads=[h128, wdj], writes=[PW])
                p.step("scalar", lambda e, w2j=w2j, q=q: e.copy(
                    w2j[:, q * 2048:(q + 1) * 2048], PW[:, :]), reads=[PW], writes=[w2j])
            p.step("sync", lambda e, w2j=w2j, j=j: e.dma_start(
                out=w2_ap[j, :, :], in_=w2j[:, :]), reads=[w2j, w2_ap], dma=True)
        p.emit()

    bsteps16, _ = _butterfly_plan(16, [8, 4, 2, 1])
    with ExitStack() as ctx:
        sb = lambda shape, dt=F32: ctx.enter_context(nc.sbuf_tensor(_nm(), shape, dt))
        TH = TL // 2
        ftile = [sb([128, TH]) for _ in range(17)]
        x1c = sb([128, NT * TH], BF16)
        x1o = sb([128, NT * TH], BF16)
        h128 = sb([128, 128], BF16)
        PF = ctx.enter_context(nc.psum_tensor(_nm(), [128, TH], F32))
        PG = ctx.enter_context(nc.psum_tensor(_nm(), [128, TH], F32))
        for half in range(2):
            t0 = half * TH
            p = Prog(nc, S)
            if half == 0:
                p.step("sync", lambda e: e.dma_start(out=h128[:, :], in_=A["H128b"]),
                       writes=[h128], dma=True)
            for g in range(NT):
                p.step("sync", lambda e, g=g, t0=t0: e.dma_start(
                    out=ftile[g][:, :], in_=_tile(lnsu_ap, g, t0, t0 + TH)),
                    writes=[ftile[g]], dma=True)
            for si, stage in enumerate(bsteps16):
                last = si == len(bsteps16) - 1
                for pi, (pa, pb, sp) in enumerate(stage):
                    if not last:
                        def bf_(e, pa=pa, pb=pb, sp=sp):
                            e.tensor_sub(ftile[sp][:, :], ftile[pa][:, :], ftile[pb][:, :])
                            return e.tensor_add(ftile[pa][:, :], ftile[pa][:, :], ftile[pb][:, :])
                        p.step("vector", bf_, reads=[ftile[pa], ftile[pb]],
                               writes=[ftile[sp], ftile[pa]])
                    else:
                        la = 2 * pi
                        def bf2(e, pa=pa, pb=pb, la=la):
                            e.tensor_sub(x1c[:, (la + 1) * TH:(la + 2) * TH],
                                         ftile[pa][:, :], ftile[pb][:, :])
                            return e.tensor_add(x1c[:, la * TH:(la + 1) * TH],
                                                ftile[pa][:, :], ftile[pb][:, :])
                        p.step("vector", bf2, reads=[ftile[pa], ftile[pb]], writes=[x1c])
            if x1c_ap is not None:
                for a in range(NT):
                    p.step("sync", lambda e, a=a, t0=t0: e.dma_start(
                        out=_tile(x1c_ap, a, t0, t0 + TH), in_=x1c[:, a * TH:(a + 1) * TH]),
                        reads=[x1c], dma=True)
            for a in range(NT):
                PX = PF if a % 2 == 0 else PG
                def h128mm(e, a=a, PX=PX):
                    for n in range(TH // 512):
                        i = e.matmul(PX[:, n * 512:(n + 1) * 512], h128[:, :],
                                     x1c[:, a * TH + n * 512:a * TH + (n + 1) * 512],
                                     start=True, stop=True)
                    return i
                p.step("tensor", h128mm, reads=[h128, x1c], writes=[PX])
                p.step("scalar", lambda e, a=a, PX=PX: e.copy(
                    x1o[:, a * TH:(a + 1) * TH], PX[:, :]), reads=[PX], writes=[x1o])
                p.step("sync", lambda e, a=a, t0=t0: e.dma_start(
                    out=_tile(x1h_ap, a, t0, t0 + TH), in_=x1o[:, a * TH:(a + 1) * TH]),
                    reads=[x1o, x1h_ap], dma=True)
            p.emit()

    bsteps64, _ = _butterfly_plan(64, [32, 16, 8, 4, 2, 1])
    TC = 512
    with ExitStack() as ctx:
        sb = lambda shape, dt=F32: ctx.enter_context(nc.sbuf_tensor(_nm(), shape, dt))
        sup = sb([128, 64])
        sgt = sb([128, 64])
        sdn = sb([128, 16])
        sud = sb([128, 64])
        p = Prog(nc, S)
        p.step("sync", lambda e: e.dma_start(out=sup[:, :], in_=A["wsu_up_cols"]),
               writes=[sup], dma=True)
        p.step("sync", lambda e: e.dma_start(out=sgt[:, :], in_=A["wsu_gt_cols"]),
               writes=[sgt], dma=True)
        p.step("sync", lambda e: e.dma_start(out=sdn[:, :], in_=A["wsd_cols"]),
               writes=[sdn], dma=True)
        p.step("sync", lambda e: e.dma_start(out=sud[:, :], in_=A["su_dp_cols"]),
               writes=[sud], dma=True)
        p.step("vector", lambda e: e.tensor_scalar_mul(sup[:, :], sup[:, :], R2048),
               reads=[sup], writes=[sup])
        p.step("vector", lambda e: e.tensor_scalar_mul(sgt[:, :], sgt[:, :], R2048),
               reads=[sgt], writes=[sgt])
        p.step("vector", lambda e: e.tensor_scalar_mul(sdn[:, :], sdn[:, :], R8192),
               reads=[sdn], writes=[sdn])
        p.emit()

        for tc in range(TL // TC):
            t0 = tc * TC
            with ExitStack() as c1:
                sb1 = lambda shape, dt=F32: c1.enter_context(nc.sbuf_tensor(_nm(), shape, dt))
                x1s = sb1([128, NT * TC], BF16)
                wub = [sb1([128, C], BF16) for _ in range(2)]
                wgb = [sb1([128, C], BF16) for _ in range(2)]
                us = [sb1([128, TC]) for _ in range(2)]
                gs = [sb1([128, TC]) for _ in range(2)]
                pF = [sb1([128, TC]) for _ in range(65)]
                bnc = sb1([128, 2 * TC], BF16)
                PU = [c1.enter_context(nc.psum_tensor(_nm(), [128, TC], F32))
                      for _ in range(2)]
                PG2 = [c1.enter_context(nc.psum_tensor(_nm(), [128, TC], F32))
                       for _ in range(2)]
                p = Prog(nc, S)
                p.step("sync", lambda e, t0=t0: e.dma_start(
                    out=x1s[:, :],
                    in_=x1h_ap.rearrange("(g p) t -> p g t", p=128)[:, :, t0:t0 + TC]),
                    reads=[x1h_ap], writes=[x1s], dma=True)
                for j in range(64):
                    wuj, wgj = wub[j % 2], wgb[j % 2]
                    PUj, PGj = PU[j % 2], PG2[j % 2]
                    usj, gsj = us[j % 2], gs[j % 2]
                    p.step("gpsimd", lambda e, wuj=wuj, j=j: e.dma_start(
                        out=wuj[:, :], in_=A["wupP"][j, :, :]), writes=[wuj], dma=True)
                    p.step("gpsimd", lambda e, wgj=wgj, j=j: e.dma_start(
                        out=wgj[:, :], in_=A["wgateP"][j, :, :]), writes=[wgj], dma=True)
                    p.step("tensor", lambda e, wuj=wuj, PUj=PUj: _mm_ct(e, PUj, wuj, x1s, tl=TC),
                           reads=[wuj, x1s], writes=[PUj])
                    p.step("tensor", lambda e, wgj=wgj, PGj=PGj: _mm_ct(e, PGj, wgj, x1s, tl=TC),
                           reads=[wgj, x1s], writes=[PGj])
                    p.step("scalar", lambda e, j=j, PUj=PUj, usj=usj: e.activation(
                        usj[:, :], PUj[:, :], AF.Identity, scale=sup[:, j:j + 1]),
                        reads=[PUj, sup], writes=[usj])
                    p.step("scalar", lambda e, j=j, PGj=PGj, gsj=gsj: e.activation(
                        gsj[:, :], PGj[:, :], AF.Silu, scale=sgt[:, j:j + 1]),
                        reads=[PGj, sgt], writes=[gsj])
                    p.step("vector", lambda e, j=j, gsj=gsj, usj=usj: e.scalar_tensor_tensor(
                        pF[j][:, :], gsj[:, :], sud[:, j:j + 1], usj[:, :],
                        ALU.mult, ALU.mult),
                        reads=[gsj, sud, usj], writes=[pF[j]])
                for si, stage in enumerate(bsteps64):
                    last = si == len(bsteps64) - 1
                    for pi, (pa, pb, sp) in enumerate(stage):
                        if not last:
                            def bf_(e, pa=pa, pb=pb, sp=sp):
                                e.tensor_sub(pF[sp][:, :], pF[pa][:, :], pF[pb][:, :])
                                return e.tensor_add(pF[pa][:, :], pF[pa][:, :], pF[pb][:, :])
                            p.step("vector", bf_, reads=[pF[pa], pF[pb]],
                                   writes=[pF[sp], pF[pa]])
                        else:
                            la = 2 * pi
                            def bf2(e, pa=pa, pb=pb):
                                e.tensor_sub(bnc[:, TC:2 * TC], pF[pa][:, :], pF[pb][:, :])
                                return e.tensor_add(bnc[:, 0:TC], pF[pa][:, :], pF[pb][:, :])
                            p.step("vector", bf2, reads=[pF[pa], pF[pb]], writes=[bnc])
                            p.step("sync", lambda e, la=la: e.dma_start(
                                out=p6_ap[:, la * TC:(la + 2) * TC], in_=bnc[:, :]),
                                reads=[bnc, p6_ap], dma=True)
                p.emit()

            with ExitStack() as c2:
                sb2 = lambda shape, dt=F32: c2.enter_context(nc.sbuf_tensor(_nm(), shape, dt))
                p6s = sb2([128, 64 * TC], BF16)
                w2b = [sb2([128, INTER], BF16) for _ in range(2)]
                us2 = sb2([128, TC])
                xr = [sb2([128, TC]) for _ in range(2)]
                ot = [sb2([128, TC]) for _ in range(2)]
                PO = c2.enter_context(nc.psum_tensor(_nm(), [128, TC], F32))
                PO2 = c2.enter_context(nc.psum_tensor(_nm(), [128, TC], F32))
                p = Prog(nc, S)
                p.step("sync", lambda e: e.dma_start(out=p6s[:, :], in_=p6_ap),
                       reads=[p6_ap], writes=[p6s], dma=True)
                for j in range(NT):
                    w2j = w2b[j % 2]
                    xrj, otj = xr[j % 2], ot[j % 2]
                    PX = PO if j % 2 == 0 else PO2
                    p.step("gpsimd", lambda e, w2j=w2j, j=j: e.dma_start(
                        out=w2j[:, :], in_=w2_ap[j, :, :]),
                        reads=[w2_ap], writes=[w2j], dma=True)
                    p.step("sync", lambda e, xrj=xrj, j=j, t0=t0: e.dma_start(
                        out=xrj[:, :], in_=_tile(x_ap, j, t0, t0 + TC)),
                        writes=[xrj], dma=True)

                    def wdmm(e, w2j=w2j, PX=PX):
                        for s in range(64):
                            i = e.matmul(PX[:, :], w2j[:, s * 128:(s + 1) * 128],
                                         p6s[:, s * TC:(s + 1) * TC],
                                         start=(s == 0), stop=(s == 63))
                        return i
                    p.step("tensor", wdmm, reads=[w2j, p6s], writes=[PX])
                    p.step("scalar", lambda e, j=j, PX=PX: e.activation(
                        us2[:, :], PX[:, :], AF.Identity, scale=sdn[:, j:j + 1]),
                        reads=[PX, sdn], writes=[us2])
                    p.step("vector", lambda e, xrj=xrj, otj=otj: e.tensor_add(
                        otj[:, :], us2[:, :], xrj[:, :]), reads=[us2, xrj], writes=[otj])
                    p.step("sync", lambda e, otj=otj, j=j, t0=t0: e.dma_start(
                        out=_tile(x2_ap, j, t0, t0 + TC), in_=otj[:, :]),
                        reads=[otj], dma=True)
                p.emit()


def emit_l1a1(nc, S, A, x_ap, hs_ap, l_ap, ast_ap, mz_ap, pcA):
    emit_ln(nc, S, A, x_ap, out_f32=hs_ap)
    with ExitStack() as ctx:
        sb = lambda shape, dt=F32: ctx.enter_context(nc.sbuf_tensor(_nm(), shape, dt))
        hsb = sb([128, NT * TL], BF16)
        wz = sb([128, C], BF16)
        wh = sb([128, C], BF16)
        k1 = sb([128, TL])
        z2 = sb([128, TL])
        sp_ = sb([128, TL])
        scr = sb([128, TL])
        lcb = sb([128, TL], BF16)
        lzt = sb([128, TL])
        u_s = sb([128, TL])
        ast = sb([128, TL])
        R_s = sb([128, TL])
        d_s = sb([128, TL])
        b_s = sb([128, TL])
        Z_s = sb([128, TL])
        l_s = sb([128, TL])
        msk = sb([128, TL], U8)
        bzc = sb([128, 16])
        bhc = sb([128, 16])
        h0c = sb([128, 16])
        u0c = sb([128, 16])
        tnc = sb([128, 16])
        fcol = sb([128, 1])
        gcol = sb([128, 1])
        rin = sb([128, 16])
        cst = sb([128, 2])
        carry = sb([1, TL])
        crow = sb([1, TL])
        mzc = sb([128, 32])
        trib = sb([128, 128], BF16)
        o1f = sb([1, 128])
        o128 = sb([128, 128], BF16)
        PA = ctx.enter_context(nc.psum_tensor(_nm(), [128, TL], F32))
        PB = ctx.enter_context(nc.psum_tensor(_nm(), [128, TL], F32))

        p = Prog(nc, S)
        p.step("gpsimd", lambda e: e.dma_start(
            out=hsb[:, :], in_=hs_ap.rearrange("(g p) t -> p g t", p=128)),
            writes=[hsb], dma=True)
        for nm, dst in (("bz_cols", bzc), ("bh_cols", bhc), ("h0_cols", h0c),
                        ("TRIb", trib), ("ones1f", o1f), ("ones128b", o128),
                        ("f_col", fcol), ("gneg_col", gcol)):
            p.step("sync", lambda e, nm=nm, dst=dst: e.dma_start(out=dst[:, :], in_=A[nm]),
                   writes=[dst], dma=True)
        p.step("gpsimd", lambda e: e.memset(carry[0:1, :], 0.0), writes=[carry])
        p.step("gpsimd", lambda e: e.memset(cst[:, 0:1], 0.5), writes=[cst])
        p.step("gpsimd", lambda e: e.memset(cst[:, 1:2], 1.0), writes=[cst])

        for fs in [
            lambda e: e.activation(u0c[:, :], h0c[:, :], AF.Relu),
            lambda e: e.activation(u0c[:, :], u0c[:, :], AF.Ln, bias=cst[:, 0:1]),
            lambda e: e.activation(tnc[:, :], h0c[:, :], AF.Abs),
            lambda e: e.activation(tnc[:, :], tnc[:, :], AF.Exp, scale=-1.0),
            lambda e: e.activation(tnc[:, :], tnc[:, :], AF.Ln, bias=cst[:, 1:2]),
        ]:
            p.step("scalar", fs, reads=[h0c, cst, u0c, tnc], writes=[u0c, tnc])

        for fs in [
            lambda e: e.scalar_tensor_tensor(tnc[:, :], h0c[:, :], 0.0, tnc[:, :],
                                             ALU.max, ALU.add),
            lambda e: e.tensor_sub(tnc[:, :], h0c[:, :], tnc[:, :]),
            lambda e: e.tensor_single_scalar(rin[:, :], h0c[:, :], 0.0, ALU.is_ge),
        ]:
            p.step("vector", fs, reads=[h0c, tnc], writes=[tnc, rin])
        p.step("vector", lambda e: e.tensor_copy(msk[:, 0:16], rin[:, :]),
               reads=[rin], writes=[msk])
        p.step("vector", lambda e: e.select(u0c[:, :], msk[:, 0:16], u0c[:, :], tnc[:, :]),
               reads=[msk, u0c, tnc], writes=[u0c])
        p.step("vector", lambda e: e.tensor_scalar(rin[:, :], u0c[:, :], fcol[:, 0:1],
                                                   None, ALU.mult),
               reads=[u0c, fcol], writes=[rin])
        p.step("vector", lambda e: e.tensor_scalar(rin[:, :], rin[:, :], gcol[:, 0:1],
                                                   None, ALU.add),
               reads=[rin, gcol], writes=[rin])

        for j in range(NT):
            p.step("gpsimd", lambda e, j=j: e.dma_start(
                out=wz[:, :], in_=A["wzP"][j, :, :]), writes=[wz], dma=True)
            p.step("gpsimd", lambda e, j=j: e.dma_start(
                out=wh[:, :], in_=A["whP"][j, :, :]), writes=[wh], dma=True)
            p.step("tensor", lambda e: _mm_ct(e, PA, wz, hsb),
                   reads=[wz, hsb], writes=[PA])
            p.step("tensor", lambda e: _mm_ct(e, PB, wh, hsb),
                   reads=[wh, hsb], writes=[PB])
            p.step("scalar", lambda e, j=j: e.activation(
                k1[:, :], PA[:, :], AF.Identity, bias=bzc[:, j:j + 1]),
                reads=[PA, bzc], writes=[k1])
            p.step("scalar", lambda e, j=j: e.activation(
                z2[:, :], PB[:, :], AF.Identity, bias=bhc[:, j:j + 1]),
                reads=[PB, bhc], writes=[z2])

            def spa(e):
                e.activation(scr[:, :], k1[:, :], AF.Abs)
                e.activation(scr[:, :], scr[:, :], AF.Exp, scale=-1.0)
                return e.activation(scr[:, :], scr[:, :], AF.Ln, bias=cst[:, 1:2])
            p.step("scalar", spa, reads=[k1, cst], writes=[scr])
            p.step("vector", lambda e: e.scalar_tensor_tensor(
                sp_[:, :], k1[:, :], 0.0, scr[:, :], ALU.max, ALU.add),
                reads=[k1, scr], writes=[sp_])
            p.step("vector", lambda e: e.tensor_scalar_mul(lcb[:, :], sp_[:, :], -1.0),
                   reads=[sp_], writes=[lcb])
            def gla(e):
                e.activation(lzt[:, :], z2[:, :], AF.Relu)
                e.activation(lzt[:, :], lzt[:, :], AF.Ln, bias=cst[:, 0:1])
                e.activation(scr[:, :], z2[:, :], AF.Abs)
                e.activation(scr[:, :], scr[:, :], AF.Exp, scale=-1.0)
                return e.activation(scr[:, :], scr[:, :], AF.Ln, bias=cst[:, 1:2])
            p.step("scalar", gla, reads=[z2, cst], writes=[lzt, scr])

            def glb(e):
                e.scalar_tensor_tensor(scr[:, :], z2[:, :], 0.0, scr[:, :], ALU.max, ALU.add)
                e.tensor_sub(scr[:, :], z2[:, :], scr[:, :])
                e.tensor_single_scalar(msk[:, :], z2[:, :], 0.0, ALU.is_ge)
                return e.select(lzt[:, :], msk[:, :], lzt[:, :], scr[:, :])
            p.step("vector", glb, reads=[z2, scr, lzt], writes=[scr, msk, lzt])

            def lz2(e):
                e.tensor_sub(u_s[:, :], k1[:, :], sp_[:, :])
                return e.tensor_add(lzt[:, :], lzt[:, :], u_s[:, :])
            p.step("vector", lz2, reads=[k1, sp_, lzt], writes=[lzt, u_s])

            def amm(e):
                for n in range(4):
                    e.matmul(PA[:, n * 512:(n + 1) * 512], trib[:, :],
                             lcb[:, n * 512:(n + 1) * 512], start=True, stop=True)
                for n in range(4):
                    i = e.matmul(PB[:, n * 512:(n + 1) * 512], o1f[0:1, :],
                                 carry[0:1, n * 512:(n + 1) * 512], start=True, stop=True)
                return i
            p.step("tensor", amm, reads=[trib, lcb, o1f, carry], writes=[PA, PB])

            p.step("scalar", lambda e: e.copy(ast[:, :], PA[:, :]),
                   reads=[PA], writes=[ast])
            p.step("vector", lambda e: e.tensor_add(ast[:, :], ast[:, :], PB[:, :]),
                   reads=[ast, PB], writes=[ast])
            p.step("vector", lambda e: e.tensor_sub(u_s[:, :], lzt[:, :], ast[:, :]),
                   reads=[lzt, ast], writes=[u_s])
            p.step("sync", lambda e, j=j: e.dma_start(out=_tile(ast_ap, j), in_=ast[:, :]),
                   reads=[ast], dma=True)

            def cmm(e):
                for ct in range(1):
                    pass
                for n in range(4):
                    i = e.matmul(PA[:, n * 512:(n + 1) * 512], o128[:, :],
                                 lcb[:, n * 512:(n + 1) * 512], start=True, stop=True)
                return i
            p.step("tensor", cmm, reads=[o128, lcb], writes=[PA])
            p.step("vector", lambda e: e.tensor_add(carry[0:1, :], carry[0:1, :], PA[0:1, :]),
                   reads=[PA, carry], writes=[carry])

            p.step("gpsimd", lambda e: e.memset(scr[:, :], 0.0), writes=[scr])
            p.step("vector", lambda e, j=j: e.tensor_tensor_scan(
                R_s[:, :], u_s[:, :], scr[:, :], rin[:, j:j + 1], ALU.max, ALU.add),
                reads=[u_s, scr, rin], writes=[R_s], fence=True, hazard=True)

            def dsh(e, j=j):
                e.tensor_sub(d_s[:, 1:TL], R_s[:, 0:TL - 1], R_s[:, 1:TL])
                e.tensor_scalar(d_s[:, 0:1], R_s[:, 0:1], -1.0, rin[:, j:j + 1],
                                ALU.mult, ALU.add)
                return e.tensor_sub(b_s[:, :], u_s[:, :], R_s[:, :])
            p.step("vector", dsh, reads=[R_s, u_s, rin], writes=[d_s, b_s])

            def exps(e):
                e.activation(d_s[:, :], d_s[:, :], AF.Exp)
                return e.activation(b_s[:, :], b_s[:, :], AF.Exp)
            p.step("scalar", exps, reads=[d_s, b_s], writes=[d_s, b_s])
            p.step("vector", lambda e, j=j: e.tensor_tensor_scan(
                Z_s[:, :], d_s[:, :], b_s[:, :], fcol[:, 0:1], ALU.mult, ALU.add),
                reads=[d_s, b_s, fcol], writes=[Z_s], fence=True, hazard=True)
            p.step("scalar", lambda e: e.activation(l_s[:, :], Z_s[:, :], AF.Ln),
                   reads=[Z_s], writes=[l_s])
            p.step("vector", lambda e: e.tensor_add(l_s[:, :], l_s[:, :], R_s[:, :]),
                   reads=[R_s, l_s], writes=[l_s])
            p.step("sync", lambda e, j=j: e.dma_start(out=_tile(l_ap, j), in_=l_s[:, :]),
                   reads=[l_s], dma=True)
            p.step("vector", lambda e, j=j: e.tensor_copy(mzc[:, j:j + 1], R_s[:, TL - 1:TL]),
                   reads=[R_s], writes=[mzc])
            p.step("vector", lambda e, j=j: e.tensor_copy(mzc[:, 16 + j:17 + j], Z_s[:, TL - 1:TL]),
                   reads=[Z_s], writes=[mzc])
        p.step("sync", lambda e: e.dma_start(out=mz_ap, in_=mzc[:, :]),
               reads=[mzc], dma=True)
        p.emit()


def emit_l1a2(nc, S, A, l_ap, ast_ap, mzall_ap, logh_ap, stats_ap):
    with ExitStack() as ctx:
        sb = lambda shape, dt=F32: ctx.enter_context(nc.sbuf_tensor(_nm(), shape, dt))
        sall = sb([128, 8 * 32])
        l1m = sb([128, 8])
        l1mn = sb([128, 8])
        fcol = sb([128, 1])
        PM = sb([128, 16])
        PZ = sb([128, 16])
        Me = sb([128, 16])
        Ze = sb([128, 16])
        Mn = sb([128, 16])
        t1 = sb([128, 16])
        t2 = sb([128, 16])
        Pcol = sb([128, 16])
        h0c = sb([128, 16])
        u0c = sb([128, 16])
        tnc = sb([128, 16])
        msk16 = sb([128, 16], U8)
        cst = sb([128, 2])
        l_s = [sb([128, TL]) for _ in range(2)]
        ast = [sb([128, TL]) for _ in range(2)]
        dl = [sb([128, TL]) for _ in range(2)]
        mx = [sb([128, TL]) for _ in range(2)]
        scr = [sb([128, TL]) for _ in range(2)]
        acc = sb([128, 34])
        o1c = sb([128, 1])
        rowst = sb([1, 40])
        PS = ctx.enter_context(nc.psum_tensor(_nm(), [1, 512], F32))

        p = Prog(nc, S)
        p.step("sync", lambda e: e.dma_start(
            out=sall[:, :], in_=mzall_ap.rearrange("(j p) s -> p j s", p=128)),
            writes=[sall], dma=True)
        p.step("sync", lambda e: e.dma_start(out=l1m[:, :], in_=A["l1m"]),
               writes=[l1m], dma=True)
        p.step("sync", lambda e: e.dma_start(out=l1mn[:, :], in_=A["l1mneg"]),
               writes=[l1mn], dma=True)
        p.step("sync", lambda e: e.dma_start(out=fcol[:, :], in_=A["f_col"]),
               writes=[fcol], dma=True)
        p.step("sync", lambda e: e.dma_start(out=h0c[:, :], in_=A["h0_cols"]),
               writes=[h0c], dma=True)
        p.step("gpsimd", lambda e: e.memset(PM[:, :], NEG), writes=[PM])
        p.step("gpsimd", lambda e: e.memset(PZ[:, :], 0.0), writes=[PZ])
        p.step("gpsimd", lambda e: e.memset(o1c[:, :], 1.0), writes=[o1c])
        p.step("gpsimd", lambda e: e.memset(cst[:, 0:1], 0.5), writes=[cst])
        p.step("gpsimd", lambda e: e.memset(cst[:, 1:2], 1.0), writes=[cst])
        for j in range(W):
            fr = [sall, l1m, l1mn, PM, Me, Ze, Mn]
            fw = [Me, Ze, Mn, t1, t2]
            for fs in [
                lambda e, j=j: e.tensor_scalar(Me[:, :], sall[:, 32 * j:32 * j + 16],
                                               l1m[:, j:j + 1], l1mn[:, j:j + 1],
                                               ALU.mult, ALU.add),
                lambda e, j=j: e.tensor_scalar(Ze[:, :], sall[:, 32 * j + 16:32 * j + 32],
                                               l1m[:, j:j + 1], None, ALU.mult),
                lambda e: e.tensor_max(Mn[:, :], PM[:, :], Me[:, :]),
                lambda e: e.tensor_sub(t1[:, :], PM[:, :], Mn[:, :]),
                lambda e: e.tensor_sub(t2[:, :], Me[:, :], Mn[:, :]),
            ]:
                p.step("vector", fs, reads=fr, writes=fw)

            def f2(e):
                e.activation(t1[:, :], t1[:, :], AF.Exp)
                return e.activation(t2[:, :], t2[:, :], AF.Exp)
            p.step("scalar", f2, reads=[t1, t2], writes=[t1, t2])

            for fs in [
                lambda e: e.tensor_mul(PZ[:, :], PZ[:, :], t1[:, :]),
                lambda e: e.tensor_mul(t2[:, :], Ze[:, :], t2[:, :]),
                lambda e: e.tensor_add(PZ[:, :], PZ[:, :], t2[:, :]),
                lambda e: e.tensor_copy(PM[:, :], Mn[:, :]),
            ]:
                p.step("vector", fs, reads=[PZ, t1, t2, Ze, Mn], writes=[PZ, PM, t2])
        p.step("vector", lambda e: e.tensor_scalar_max(PZ[:, :], PZ[:, :], 1e-38),
               reads=[PZ], writes=[PZ])
        p.step("scalar", lambda e: e.activation(Pcol[:, :], PZ[:, :], AF.Ln),
               reads=[PZ], writes=[Pcol])
        p.step("vector", lambda e: e.tensor_add(Pcol[:, :], Pcol[:, :], PM[:, :]),
               reads=[Pcol, PM], writes=[Pcol])

        for fs in [
            lambda e: e.activation(u0c[:, :], h0c[:, :], AF.Relu),
            lambda e: e.activation(u0c[:, :], u0c[:, :], AF.Ln, bias=cst[:, 0:1]),
            lambda e: e.activation(tnc[:, :], h0c[:, :], AF.Abs),
            lambda e: e.activation(tnc[:, :], tnc[:, :], AF.Exp, scale=-1.0),
            lambda e: e.activation(tnc[:, :], tnc[:, :], AF.Ln, bias=cst[:, 1:2]),
        ]:
            p.step("scalar", fs, reads=[h0c, cst, u0c, tnc], writes=[u0c, tnc])

        for fs in [
            lambda e: e.scalar_tensor_tensor(tnc[:, :], h0c[:, :], 0.0, tnc[:, :],
                                             ALU.max, ALU.add),
            lambda e: e.tensor_sub(tnc[:, :], h0c[:, :], tnc[:, :]),
            lambda e: e.tensor_single_scalar(msk16[:, :], h0c[:, :], 0.0, ALU.is_ge),
            lambda e: e.select(u0c[:, :], msk16[:, :], u0c[:, :], tnc[:, :]),
            lambda e: e.tensor_scalar(u0c[:, :], u0c[:, :], fcol[:, 0:1], None, ALU.mult),
        ]:
            p.step("vector", fs, reads=[h0c, tnc, msk16, u0c, fcol],
                   writes=[tnc, msk16, u0c])
        p.step("scalar", lambda e: e.activation(scr[0][:, 0:16], u0c[:, :], AF.Copy,
                                                accum_out=acc[:, 32:33]),
               reads=[u0c], writes=[scr[0], acc])
        p.step("scalar", lambda e: e.activation(scr[0][:, 0:16], u0c[:, :], AF.Square,
                                                accum_out=acc[:, 33:34]),
               reads=[u0c], writes=[scr[0], acc])

        for j in range(NT):
            lj, aj, dj, mj, sj = (l_s[j % 2], ast[j % 2], dl[j % 2], mx[j % 2],
                                  scr[j % 2])
            p.step("sync", lambda e, j=j, lj=lj: e.dma_start(out=lj[:, :], in_=_tile(l_ap, j)),
                   writes=[lj], dma=True)
            p.step("sync", lambda e, j=j, aj=aj: e.dma_start(out=aj[:, :], in_=_tile(ast_ap, j)),
                   writes=[aj], dma=True)

            def lae(e, j=j, lj=lj, dj=dj, mj=mj):
                e.tensor_scalar(dj[:, :], lj[:, :], Pcol[:, j:j + 1], None, ALU.subtract)
                return e.tensor_scalar(mj[:, :], lj[:, :], Pcol[:, j:j + 1], None, ALU.max)
            p.step("vector", lae, reads=[lj, Pcol], writes=[dj, mj])

            def lex(e, dj=dj):
                e.activation(dj[:, :], dj[:, :], AF.Abs)
                e.activation(dj[:, :], dj[:, :], AF.Exp, scale=-1.0)
                return e.activation(dj[:, :], dj[:, :], AF.Ln, bias=cst[:, 1:2])
            p.step("scalar", lex, reads=[dj, cst], writes=[dj])

            def lsum(e, mj=mj, dj=dj, aj=aj):
                e.tensor_add(mj[:, :], mj[:, :], dj[:, :])
                return e.tensor_add(mj[:, :], mj[:, :], aj[:, :])
            p.step("vector", lsum, reads=[mj, dj, aj], writes=[mj])
            p.step("sync", lambda e, j=j, mj=mj: e.dma_start(out=_tile(logh_ap, j), in_=mj[:, :]),
                   reads=[mj], dma=True)
            p.step("scalar", lambda e, j=j, mj=mj, sj=sj: e.activation(
                sj[:, :], mj[:, :], AF.Copy, accum_out=acc[:, j:j + 1]),
                reads=[mj], writes=[sj, acc])
            p.step("scalar", lambda e, j=j, mj=mj, sj=sj: e.activation(
                sj[:, :], mj[:, :], AF.Square, accum_out=acc[:, 16 + j:17 + j]),
                reads=[mj], writes=[sj, acc])

        def statmm(e):
            return e.matmul(PS[0:1, 0:34], o1c[:, 0:1], acc[:, :], start=True, stop=True)
        p.step("tensor", statmm, reads=[o1c, acc], writes=[PS])

        for fs in [
            lambda e: e.reduce_sum(rowst[0:1, 0:1], PS[0:1, 0:16], axis=mybir.AxisListType.X),
            lambda e: e.reduce_sum(rowst[0:1, 1:2], PS[0:1, 16:32], axis=mybir.AxisListType.X),
            lambda e: e.tensor_add(rowst[0:1, 0:1], rowst[0:1, 0:1], PS[0:1, 32:33]),
            lambda e: e.tensor_add(rowst[0:1, 1:2], rowst[0:1, 1:2], PS[0:1, 33:34]),
        ]:
            p.step("vector", fs, reads=[PS, rowst], writes=[rowst])
        p.step("sync", lambda e: e.dma_start(out=stats_ap, in_=rowst[0:1, 0:2]),
               reads=[rowst], dma=True)
        p.emit()


def emit_l1a3(nc, S, A, x_ap, hs_ap, logh_ap, statsall_ap, x3_ap):
    NTOT = 2.0 * 8193.0 * 2048.0
    with ExitStack() as ctx:
        sb = lambda shape, dt=F32: ctx.enter_context(nc.sbuf_tensor(_nm(), shape, dt))
        stall = sb([8, 2])
        rowst = sb([1, 16])
        o8 = sb([8, 1])
        o1f = sb([1, 128])
        sccol = sb([128, 2])
        x2aL = [sb([128, TL]) for _ in range(2)]
        lg = [sb([128, TL]) for _ in range(2)]
        hsx = [sb([128, TL]) for _ in range(2)]
        x2b = [sb([128, TL], BF16) for _ in range(2)]
        sqb = [sb([128, TL], BF16) for _ in range(2)]
        rows = sb([1, 6 * C])
        t1fL = [sb([128, TL]) for _ in range(2)]
        t2fL = [sb([128, TL]) for _ in range(2)]
        retL = [sb([128, TL]) for _ in range(2)]
        xrL = [sb([128, TL]) for _ in range(2)]
        lnw = sb([128, 16])
        lnb_ = sb([128, 16])
        o128 = sb([128, 128], BF16)
        ps = ctx.enter_context(nc.psum_tensor(_nm(), [128, TL], F32))
        pss = ctx.enter_context(nc.psum_tensor(_nm(), [128, TL], F32))

        p = Prog(nc, S)
        p.step("sync", lambda e: e.dma_start(out=stall[:, :], in_=statsall_ap),
               writes=[stall], dma=True)
        p.step("sync", lambda e: e.dma_start(out=o1f[0:1, :], in_=A["ones1f"]),
               writes=[o1f], dma=True)
        p.step("sync", lambda e: e.dma_start(out=o128[:, :], in_=A["ones128b"]),
               writes=[o128], dma=True)
        p.step("sync", lambda e: e.dma_start(out=lnw[:, :], in_=A["lnw_cols"]),
               writes=[lnw], dma=True)
        p.step("sync", lambda e: e.dma_start(out=lnb_[:, :], in_=A["lnb_cols"]),
               writes=[lnb_], dma=True)
        p.step("gpsimd", lambda e: e.memset(o8[:, :], 1.0), writes=[o8])
        p.step("tensor", lambda e: e.matmul(ps[0:1, 0:2], o8[:, 0:1], stall[:, :],
                                            start=True, stop=True),
               reads=[o8, stall], writes=[ps])

        for fs in [
            lambda e: e.tensor_scalar_mul(rowst[0:1, 0:1], ps[0:1, 0:1], 1.0 / NTOT),
            lambda e: e.tensor_mul(rowst[0:1, 2:3], rowst[0:1, 0:1], rowst[0:1, 0:1]),
            lambda e: e.tensor_scalar_mul(rowst[0:1, 2:3], rowst[0:1, 2:3], NTOT),
            lambda e: e.tensor_sub(rowst[0:1, 3:4], ps[0:1, 1:2], rowst[0:1, 2:3]),
            lambda e: e.tensor_scalar_mul(rowst[0:1, 3:4], rowst[0:1, 3:4],
                                          1.0 / (NTOT - 1.0)),
        ]:
            p.step("vector", fs, reads=[ps, rowst], writes=[rowst])

        p.step("scalar", lambda e: e.activation(rowst[0:1, 3:4], rowst[0:1, 3:4], AF.Ln),
               reads=[rowst], writes=[rowst])
        p.step("scalar", lambda e: e.activation(rowst[0:1, 4:5], rowst[0:1, 3:4],
                                                AF.Exp, scale=-0.5),
               reads=[rowst], writes=[rowst])

        p.step("vector", lambda e: e.tensor_mul(rowst[0:1, 5:6], rowst[0:1, 0:1],
                                                rowst[0:1, 4:5]),
               reads=[rowst], writes=[rowst])
        p.step("vector", lambda e: e.tensor_scalar_mul(rowst[0:1, 5:6], rowst[0:1, 5:6],
                                                       -1.0),
               reads=[rowst], writes=[rowst])
        p.step("tensor", lambda e: e.matmul(ps[:, 2:4], o1f[0:1, :], rowst[0:1, 4:6],
                                            start=True, stop=True),
               reads=[o1f, rowst], writes=[ps])
        p.step("vector", lambda e: e.tensor_copy(sccol[:, :], ps[:, 2:4]),
               reads=[ps], writes=[sccol])

        for g in range(NT):
            lgg, hsg = lg[g % 2], hsx[g % 2]
            x2bg, sqg = x2b[g % 2], sqb[g % 2]
            x2a = x2aL[g % 2]
            p.step("sync", lambda e, lgg=lgg, g=g: e.dma_start(
                out=lgg[:, :], in_=_tile(logh_ap, g)), writes=[lgg], dma=True)
            p.step("sync", lambda e, hsg=hsg, g=g: e.dma_start(
                out=hsg[:, :], in_=_tile(hs_ap, g)), writes=[hsg], dma=True)
            p.step("scalar", lambda e, lgg=lgg, x2a=x2a: e.activation(
                x2a[:, :], lgg[:, :], AF.Exp, scale=sccol[:, 0:1], bias=sccol[:, 1:2]),
                reads=[lgg, sccol], writes=[x2a])
            p.step("vector", lambda e, hsg=hsg, x2a=x2a: e.tensor_add(
                x2a[:, :], x2a[:, :], hsg[:, :]), reads=[x2a, hsg], writes=[x2a])
            p.step("gpsimd", lambda e, x2bg=x2bg, x2a=x2a: e.tensor_copy(x2bg[:, :], x2a[:, :]),
                   reads=[x2a], writes=[x2bg])
            p.step("scalar", lambda e, sqg=sqg, x2a=x2a: e.activation(sqg[:, :], x2a[:, :], AF.Square),
                   reads=[x2a], writes=[sqg])

            def mms(e, g=g, x2bg=x2bg, sqg=sqg):
                for n in range(4):
                    e.matmul(ps[:, n * 512:(n + 1) * 512], o128[:, :],
                             x2bg[:, n * 512:(n + 1) * 512],
                             start=(g == 0), stop=(g == NT - 1))
                for n in range(4):
                    i = e.matmul(pss[:, n * 512:(n + 1) * 512], o128[:, :],
                                 sqg[:, n * 512:(n + 1) * 512],
                                 start=(g == 0), stop=(g == NT - 1))
                return i
            p.step("tensor", mms, reads=[o128, x2bg, sqg], writes=[ps, pss])

        def rows1(e):
            e.tensor_scalar_mul(rows[0:1, 0:C], ps[0:1, :], 1.0 / C)
            e.tensor_scalar_mul(rows[0:1, C:2 * C], pss[0:1, :], 1.0 / C)
            e.tensor_mul(rows[0:1, 4 * C:5 * C], rows[0:1, 0:C], rows[0:1, 0:C])
            e.tensor_sub(rows[0:1, C:2 * C], rows[0:1, C:2 * C], rows[0:1, 4 * C:5 * C])
            return e.tensor_scalar_add(rows[0:1, 5 * C:6 * C], rows[0:1, C:2 * C], EPS_LN)
        p.step("vector", rows1, reads=[ps, pss], writes=[rows])

        def rows2(e):
            e.activation(rows[0:1, 5 * C:6 * C], rows[0:1, 5 * C:6 * C], AF.Ln)
            return e.activation(rows[0:1, 2 * C:3 * C], rows[0:1, 5 * C:6 * C],
                                AF.Exp, scale=-0.5)
        p.step("scalar", rows2, reads=[rows], writes=[rows])
        p.step("vector", lambda e: e.tensor_mul(rows[0:1, 3 * C:4 * C],
                                                rows[0:1, 0:C], rows[0:1, 2 * C:3 * C]),
               reads=[rows], writes=[rows])

        def bcast(e):
            for n in range(4):
                e.matmul(ps[:, n * 512:(n + 1) * 512], o1f[0:1, :],
                         rows[0:1, 2 * C + n * 512:2 * C + (n + 1) * 512],
                         start=True, stop=True)
            for n in range(4):
                i = e.matmul(pss[:, n * 512:(n + 1) * 512], o1f[0:1, :],
                             rows[0:1, 3 * C + n * 512:3 * C + (n + 1) * 512],
                             start=True, stop=True)
            return i
        p.step("tensor", bcast, reads=[o1f, rows], writes=[ps, pss])

        for g in range(NT):
            lgg, hsg = lg[g % 2], hsx[g % 2]
            x2a, t1f, t2f, ret, xr = (x2aL[g % 2], t1fL[g % 2], t2fL[g % 2],
                                      retL[g % 2], xrL[g % 2])
            p.step("sync", lambda e, lgg=lgg, g=g: e.dma_start(
                out=lgg[:, :], in_=_tile(logh_ap, g)), writes=[lgg], dma=True)
            p.step("sync", lambda e, hsg=hsg, g=g: e.dma_start(
                out=hsg[:, :], in_=_tile(hs_ap, g)), writes=[hsg], dma=True)
            p.step("sync", lambda e, g=g, xr=xr: e.dma_start(
                out=xr[:, :], in_=_tile(x_ap, g)), writes=[xr], dma=True)
            p.step("scalar", lambda e, lgg=lgg, x2a=x2a: e.activation(
                x2a[:, :], lgg[:, :], AF.Exp, scale=sccol[:, 0:1], bias=sccol[:, 1:2]),
                reads=[lgg, sccol], writes=[x2a])
            p.step("vector", lambda e, hsg=hsg, x2a=x2a: e.tensor_add(
                x2a[:, :], x2a[:, :], hsg[:, :]), reads=[x2a, hsg], writes=[x2a])

            def apply(e, x2a=x2a, t1f=t1f, t2f=t2f):
                e.scalar_tensor_tensor(t1f[:, :], x2a[:, :], 1.0, ps[:, :],
                                       ALU.mult, ALU.mult)
                return e.tensor_sub(t2f[:, :], t1f[:, :], pss[:, :])
            p.step("vector", apply, reads=[x2a, ps, pss], writes=[t1f, t2f])
            p.step("scalar", lambda e, g=g, t2f=t2f, ret=ret: e.activation(
                ret[:, :], t2f[:, :], AF.Identity, scale=lnw[:, g:g + 1], bias=lnb_[:, g:g + 1]),
                reads=[t2f, lnw, lnb_], writes=[ret])
            p.step("vector", lambda e, ret=ret, xr=xr: e.tensor_add(ret[:, :], ret[:, :], xr[:, :]),
                   reads=[ret, xr], writes=[ret])
            p.step("sync", lambda e, g=g, ret=ret: e.dma_start(
                out=_tile(x3_ap, g), in_=ret[:, :]), reads=[ret], dma=True)
        p.emit()


def emit_l1f(nc, S, A, x_ap, ln4b_ap, out_ap):
    emit_ln(nc, S, A, x_ap, out_bf=ln4b_ap)
    with ExitStack() as ctx:
        sb = lambda shape, dt=F32: ctx.enter_context(nc.sbuf_tensor(_nm(), shape, dt))
        lnb = sb([128, NT * TL], BF16)
        wbf = [sb([128, C]) for _ in range(2)]
        awb = sb([128, C], BF16)
        sgb = sb([128, C], BF16)
        sgf = sb([128, C])
        arow = sb([1, 128])
        acol = sb([128, 1])
        bcol = sb([128, 16])
        o128 = sb([128, 128], BF16)
        o11 = sb([1, 1])
        ot = [sb([128, TL]) for _ in range(2)]
        xr = [sb([128, TL]) for _ in range(2)]
        PA = ctx.enter_context(nc.psum_tensor(_nm(), [128, TL], F32))
        PAL = ctx.enter_context(nc.psum_tensor(_nm(), [128, 128], F32))
        PAC = ctx.enter_context(nc.psum_tensor(_nm(), [128, 1], F32))

        p = Prog(nc, S)
        p.step("sync", lambda e: e.dma_start(out=o128[:, :], in_=A["ones128b"]),
               writes=[o128], dma=True)
        p.step("sync", lambda e: e.dma_start(out=bcol[:, :], in_=A["bias_cols"]),
               writes=[bcol], dma=True)
        p.step("gpsimd", lambda e: e.memset(o11[:, :], 1.0), writes=[o11])
        for g in range(NT):
            p.step("sync", lambda e, g=g: e.dma_start(
                out=lnb[:, g * TL:(g + 1) * TL], in_=_tile(ln4b_ap, g)),
                writes=[lnb], dma=True)
        for j in range(NT):
            wbj = wbf[j % 2]
            PX = PA
            otj, xrj = ot[j % 2], xr[j % 2]
            p.step("sync", lambda e, wbj=wbj, j=j: e.dma_start(
                out=wbj[:, :], in_=A["wbP"][j, :, :]), writes=[wbj], dma=True)
            p.step("scalar", lambda e, wbj=wbj: e.activation(awb[:, :], wbj[:, :], AF.Abs),
                   reads=[wbj], writes=[awb])

            def amm(e):
                for ct in range(NT):
                    i = e.matmul(PAL[:, :], o128[:, :], awb[:, ct * 128:(ct + 1) * 128],
                                 start=(ct == 0), stop=(ct == NT - 1))
                return i
            p.step("tensor", amm, reads=[o128, awb], writes=[PAL])
            p.step("vector", lambda e: e.tensor_scalar_mul(arow[0:1, :], PAL[0:1, :], 1.0 / C),
                   reads=[PAL], writes=[arow])
            p.step("tensor", lambda e: e.matmul(PAC[:, :], arow[0:1, :], o11[0:1, :],
                                                start=True, stop=True),
                   reads=[arow, o11], writes=[PAC])
            p.step("vector", lambda e: e.tensor_copy(acol[:, :], PAC[:, :]),
                   reads=[PAC], writes=[acol])

            def sgn(e, wbj=wbj):
                e.tensor_single_scalar(sgf[:, :], wbj[:, :], 0.0, ALU.is_ge)
                return e.tensor_scalar(sgb[:, :], sgf[:, :], 2.0, -1.0, ALU.mult, ALU.add)
            p.step("vector", sgn, reads=[wbj], writes=[sgf, sgb])
            p.step("tensor", lambda e, PX=PX: _mm_ct(e, PX, sgb, lnb),
                   reads=[sgb, lnb], writes=[PX])
            p.step("sync", lambda e, xrj=xrj, j=j: e.dma_start(
                out=xrj[:, :], in_=_tile(x_ap, j)), writes=[xrj], dma=True)
            p.step("scalar", lambda e, PX=PX, otj=otj, j=j: e.activation(
                otj[:, :], PX[:, :], AF.Identity, scale=acol[:, 0:1], bias=bcol[:, j:j + 1]),
                reads=[PX, acol, bcol], writes=[otj])
            p.step("vector", lambda e, otj=otj, xrj=xrj: e.tensor_add(
                otj[:, :], otj[:, :], xrj[:, :]), reads=[otj, xrj], writes=[otj])
            p.step("sync", lambda e, otj=otj, j=j: e.dma_start(
                out=_tile(out_ap, j), in_=otj[:, :]), reads=[otj], dma=True)
        p.emit()


def emit_allgather(nc, S, in_ap, out_ap):
    p = Prog(nc, S)
    p.step("gpsimd", lambda e: e.collective_compute(
        "AllGather", mybir.AluOpType.bypass,
        replica_groups=[list(range(W))],
        ins=[in_ap], outs=[out_ap]),
        reads=[in_ap], writes=[out_ap])
    p.step("gpsimd", lambda e: e.engine_nop(), reads=[out_ap])
    p.emit()


def _mk_nc():
    return bass.Bass("TRN2", target_bir_lowering=False, debug=False)


_SHAPES = {
    "wqkvP": ([48, 128, C], F32), "woutP": ([16, 128, C], F32),
    "wupP": ([64, 128, C], F32), "wgateP": ([64, 128, C], F32),
    "wdownP": ([16, 128, INTER], F32), "wzP": ([16, 128, C], F32),
    "whP": ([16, 128, C], F32), "wbP": ([16, 128, C], F32),
    "decay_row": ([1, H], F32), "su_ug_rows": ([1, C], F32),
    "su_dp_cols": ([128, 64], F32), "wsu_up_cols": ([128, 64], F32),
    "wsu_gt_cols": ([128, 64], F32), "wsd_cols": ([128, 16], F32),
    "bz_cols": ([128, 16], F32), "bh_cols": ([128, 16], F32),
    "lnw_cols": ([128, 16], F32), "lnb_cols": ([128, 16], F32),
    "h0_cols": ([128, 16], F32), "bias_cols": ([128, 16], F32),
    "ones1f": ([1, 128], F32), "ones_row": ([1, C], F32),
    "l0cm": ([128, 24], F32), "l1m": ([128, 8], F32), "l1mneg": ([128, 8], F32),
    "f_col": ([128, 1], F32), "gneg_col": ([128, 1], F32),
    "H128b": ([128, 128], BF16), "TRIb": ([128, 128], BF16),
    "ones128b": ([128, 128], BF16),
}


class LazyAps(dict):
    def __init__(self, nc):
        super().__init__()
        self.nc = nc

    def __missing__(self, k):
        shape, dt = _SHAPES[k]
        ap = self.nc.dram_tensor(k, shape, dt, kind="ExternalInput").ap()
        self[k] = ap
        return ap


def _shared_aps(nc):
    return LazyAps(nc)


_PROG_CACHE = {}


def _build(name):
    if name in _PROG_CACHE:
        return _PROG_CACHE[name]
    nc = _mk_nc()
    A = _shared_aps(nc)
    with ExitStack() as ctx:
        S = Sems(ctx, nc)
        if name == "l0a1":
            x = nc.dram_tensor("xT", [C, TL], F32, kind="ExternalInput").ap()
            ln0 = nc.dram_tensor("ln0b", [C, TL], BF16, kind="ExternalOutput").ap()
            st = nc.dram_tensor("states", [128, 32], F32, kind="ExternalOutput").ap()
            emit_l0a(nc, S, A, x, ln0, st, phase2=False)
        elif name in ("l0a2", "l0a2dbg"):
            x = nc.dram_tensor("xT", [C, TL], F32, kind="ExternalInput").ap()
            ln0 = nc.dram_tensor("ln0b", [C, TL], BF16, kind="ExternalInput").ap()
            sall = nc.dram_tensor("states_all", [W * 128, 32], F32, kind="ExternalInput").ap()
            x1 = nc.dram_tensor("x1T", [C, TL], F32, kind="ExternalOutput").ap()
            yT = nc.dram_tensor("yT", [C, TL], BF16,
                                kind="ExternalOutput" if name == "l0a2dbg" else "Internal").ap()
            emit_l0a(nc, S, A, x, ln0, None, phase2=True, x1_ap=x1, states_all_ap=sall,
                     y_ap=yT)
        elif name in ("l0f", "l0fdbg"):
            x = nc.dram_tensor("xT", [C, TL], F32, kind="ExternalInput").ap()
            x2 = nc.dram_tensor("x2T", [C, TL], F32, kind="ExternalOutput").ap()
            lnsu = nc.dram_tensor("lnsu", [C, TL], F32,
                                  kind="ExternalOutput" if name == "l0fdbg" else "Internal").ap()
            x1h = nc.dram_tensor("x1h", [C, TL], BF16,
                                 kind="ExternalOutput" if name == "l0fdbg" else "Internal").ap()
            x1c_d = (nc.dram_tensor("x1c_dbg", [C, TL], BF16, kind="ExternalOutput").ap()
                     if name == "l0fdbg" else None)
            w2 = nc.dram_tensor("w2", [16, 128, INTER], BF16).ap()
            p6 = nc.dram_tensor("p6", [128, 64 * 512], BF16).ap()
            emit_l0f(nc, S, A, x, x2, lnsu, x1h, w2, p6,
                     x1c_ap=(x1c_d if name == "l0fdbg" else None))
        elif name == "l1a1":
            x = nc.dram_tensor("xT", [C, TL], F32, kind="ExternalInput").ap()
            hs = nc.dram_tensor("hsT", [C, TL], F32, kind="ExternalOutput").ap()
            l = nc.dram_tensor("lT", [C, TL], F32, kind="ExternalOutput").ap()
            ast = nc.dram_tensor("astT", [C, TL], F32, kind="ExternalOutput").ap()
            mz = nc.dram_tensor("mz", [128, 32], F32, kind="ExternalOutput").ap()
            emit_l1a1(nc, S, A, x, hs, l, ast, mz, None)
        elif name == "l1a2":
            l = nc.dram_tensor("lT", [C, TL], F32, kind="ExternalInput").ap()
            ast = nc.dram_tensor("astT", [C, TL], F32, kind="ExternalInput").ap()
            mzall = nc.dram_tensor("mz_all", [W * 128, 32], F32, kind="ExternalInput").ap()
            logh = nc.dram_tensor("loghT", [C, TL], F32, kind="ExternalOutput").ap()
            stats = nc.dram_tensor("stats", [1, 2], F32, kind="ExternalOutput").ap()
            emit_l1a2(nc, S, A, l, ast, mzall, logh, stats)
        elif name == "l1a3":
            x = nc.dram_tensor("xT", [C, TL], F32, kind="ExternalInput").ap()
            hs = nc.dram_tensor("hsT", [C, TL], F32, kind="ExternalInput").ap()
            logh = nc.dram_tensor("loghT", [C, TL], F32, kind="ExternalInput").ap()
            stall = nc.dram_tensor("stats_all", [W, 2], F32, kind="ExternalInput").ap()
            x3 = nc.dram_tensor("x3T", [C, TL], F32, kind="ExternalOutput").ap()
            emit_l1a3(nc, S, A, x, hs, logh, stall, x3)
        elif name == "l1f":
            x = nc.dram_tensor("xT", [C, TL], F32, kind="ExternalInput").ap()
            ln4 = nc.dram_tensor("ln4b", [C, TL], BF16).ap()
            out = nc.dram_tensor("outT", [C, TL], F32, kind="ExternalOutput").ap()
            emit_l1f(nc, S, A, x, ln4, out)
        elif name == "full":
            x = nc.dram_tensor("xT", [C, TL], F32, kind="ExternalInput").ap()
            out = nc.dram_tensor("outT", [C, TL], F32, kind="ExternalOutput").ap()
            dr = lambda nm, shape, dt=F32, sp="Local": nc.dram_tensor(
                nm, shape, dt, addr_space=sp).ap()
            ln0 = dr("ln0b", [C, TL], BF16)
            states = dr("states", [128, 32])
            states_all = dr("states_all", [W * 128, 32], sp="Shared")
            x1 = dr("x1T", [C, TL])
            yT = dr("yT", [C, TL], BF16)
            x2 = dr("x2T", [C, TL])
            lnsu = dr("lnsu", [C, TL])
            x1h = dr("x1h", [C, TL], BF16)
            w2 = dr("w2", [16, 128, INTER], BF16)
            p6 = dr("p6", [128, 64 * 512], BF16)
            hs = dr("hsT", [C, TL])
            l = dr("lT", [C, TL])
            ast = dr("astT", [C, TL])
            mz = dr("mz", [128, 32])
            mz_all = dr("mz_all", [W * 128, 32], sp="Shared")
            logh = dr("loghT", [C, TL])
            stats = dr("stats", [1, 2])
            stats_all = dr("stats_all", [W, 2], sp="Shared")
            x3 = dr("x3T", [C, TL])
            ln4 = dr("ln4b", [C, TL], BF16)
            emit_l0a(nc, S, A, x, ln0, states, phase2=False)
            emit_allgather(nc, S, states, states_all)
            emit_l0a(nc, S, A, x, ln0, None, phase2=True, x1_ap=x1,
                     states_all_ap=states_all, y_ap=yT)
            emit_l0f(nc, S, A, x1, x2, lnsu, x1h, w2, p6)
            emit_l1a1(nc, S, A, x2, hs, l, ast, mz, None)
            emit_allgather(nc, S, mz, mz_all)
            emit_l1a2(nc, S, A, l, ast, mz_all, logh, stats)
            emit_allgather(nc, S, stats, stats_all)
            emit_l1a3(nc, S, A, x2, hs, logh, stats_all, x3)
            emit_l1f(nc, S, A, x3, ln4, out)
        else:
            raise KeyError(name)
    _PROG_CACHE[name] = nc
    return nc


def _run(name, sh, per_core_extra):
    nc = _build(name)
    maps = [{**sh, **per_core_extra[c]} for c in range(W)]
    res = run_bass_kernel_spmd(nc, maps, core_ids=list(range(W)))
    return res.results


FUSED = True


def kernel(**inputs):
    sh, pc = pack_inputs(inputs)
    core = [dict(pc[c]) for c in range(W)]

    if FUSED:
        r = _run("full", sh, core)
        out = np.empty((2, 8192, C), np.float32)
        for c in range(W):
            b, s = c // 4, c % 4
            out[b, s * TL:(s + 1) * TL, :] = np.asarray(r[c]["outT"]).T
        return out

    r = _run("l0a1", sh, core)
    ln0 = [np.asarray(r[c]["ln0b"]) for c in range(W)]
    states_all = np.concatenate([np.asarray(r[c]["states"]) for c in range(W)], axis=0)
    for c in range(W):
        core[c]["ln0b"] = ln0[c]
        core[c]["states_all"] = states_all

    r = _run("l0a2", sh, core)
    for c in range(W):
        core[c]["xT"] = np.asarray(r[c]["x1T"])

    r = _run("l0f", sh, core)
    for c in range(W):
        core[c]["xT"] = np.asarray(r[c]["x2T"])

    r = _run("l1a1", sh, core)
    mz_all = np.concatenate([np.asarray(r[c]["mz"]) for c in range(W)], axis=0)
    for c in range(W):
        core[c]["hsT"] = np.asarray(r[c]["hsT"])
        core[c]["lT"] = np.asarray(r[c]["lT"])
        core[c]["astT"] = np.asarray(r[c]["astT"])
        core[c]["mz_all"] = mz_all

    r = _run("l1a2", sh, core)
    stats_all = np.concatenate([np.asarray(r[c]["stats"]) for c in range(W)], axis=0)
    for c in range(W):
        core[c]["loghT"] = np.asarray(r[c]["loghT"])
        core[c]["stats_all"] = stats_all

    r = _run("l1a3", sh, core)
    for c in range(W):
        core[c]["xT"] = np.asarray(r[c]["x3T"])

    r = _run("l1f", sh, core)
    out = np.empty((2, 8192, C), np.float32)
    for c in range(W):
        b, s = c // 4, c % 4
        out[b, s * TL:(s + 1) * TL, :] = np.asarray(r[c]["outT"]).T
    return out
